# version 26
# baseline (speedup 1.0000x reference)
import numpy as np
import concourse.bass as bass
import concourse.mybir as mybir
from concourse.bass_utils import run_bass_kernel_spmd

F32 = mybir.dt.float32
FP16 = mybir.dt.float16
ALU = mybir.AluOpType
ACT = mybir.ActivationFunctionType

ENGINES = ("sync", "tensor", "vector", "scalar", "gpsimd")
SEMS = ENGINES + ("gpsimdc",)
EPS = 1e-5
R = 32


class Sched:

    def __init__(self):
        self.ops = {e: [] for e in ENGINES}
        self.cnt = {e: 0 for e in SEMS}
        self.bank_hist = {}

    def op(self, eng, fn, waits=(), sig=False, dma=False, sem=None):
        ws = [w for w in waits if w is not None]
        for (e, v) in ws:
            assert v <= self.cnt[e], f"wait on future {e}>={v} (now {self.cnt[e]})"
        amt = 16 if dma else 1
        sname = sem or eng
        if sig:
            self.cnt[sname] += amt
        self.ops[eng].append((fn, tuple(ws), sig and sname, amt))
        return (sname, self.cnt[sname]) if sig else None

    def now(self, eng):
        return (eng, self.cnt[eng])

    def bank_wait(self, bk):
        return self.bank_hist.get(bk)

    def bank_set(self, bk, sig):
        self.bank_hist[bk] = sig


class LnPar(dict):

    def __init__(self):
        super().__init__()
        self[0] = 0
        for i in range(1, 7):
            self[i] = None
        self["pend"] = [False] * 6


def emit(blk, sched, sems):
    for ename in ENGINES:
        ops = sched.ops[ename]
        if not ops:
            continue

        def body(eng, ops=ops):
            for fn, waits, sig, amt in ops:
                for (we, wv) in waits:
                    if wv > 0:
                        eng.wait_ge(sems[we], wv)
                instr = fn(eng)
                if sig:
                    if instr is None:
                        instr = eng.nop()
                    instr.then_inc(sems[sig], amt)
        getattr(blk, ename)(body)


def build_program(B=2, T=4096, D=1024, S=64, SIDE=256):
    assert T % (2 * SIDE) == 0 and D % 128 == 0 and SIDE % 128 == 0
    KC = D // 128
    NSUB = SIDE // 128
    NH = max(1, D // 512)
    HCOL = D // NH
    NST = max(1, D // 512)
    STW = min(D, 512)
    NPAIR = T // (2 * SIDE)
    half = T // 2
    BS = B * S
    TGRP = 4
    NTG = (KC + TGRP - 1) // TGRP
    CNS = 2 * NSUB

    nc = bass.Bass("TRN2", target_bir_lowering=False,
                   detect_race_conditions=False, dynamic_dma_scratch_size=4096)

    x_in = nc.dram_tensor("x", [B, T, D], F32, kind="ExternalInput").ap()
    wt_in = nc.dram_tensor("wt", [D, D], F32, kind="ExternalInput").ap()
    wst_in = nc.dram_tensor("wst", [D, D], F32, kind="ExternalInput").ap()
    wkvt_in = nc.dram_tensor("wkvt", [D, 2 * S], F32, kind="ExternalInput").ap()
    vr_in = nc.dram_tensor("vr", [D, R], F32, kind="ExternalInput").ap()
    ctb_in = nc.dram_tensor("ctb", [R + 1, D], F32, kind="ExternalInput").ap()
    wot_in = nc.dram_tensor("wot", [S, D], F32, kind="ExternalInput").ap()
    td_in = nc.dram_tensor("td", [S, 1], F32, kind="ExternalInput").ap()
    tf_in = nc.dram_tensor("tf", [S, 1], F32, kind="ExternalInput").ap()
    sg_in = nc.dram_tensor("sg", [1, D], F32, kind="ExternalInput").ap()
    lnw_in = nc.dram_tensor("lnw", [D], F32, kind="ExternalInput").ap()
    lnb_in = nc.dram_tensor("lnb", [D], F32, kind="ExternalInput").ap()
    id_in = nc.dram_tensor("ident", [128, 128], F32, kind="ExternalInput").ap()
    out_hbm = nc.dram_tensor("out", [B, T, D], F32, kind="ExternalOutput").ap()
    x1_hbm = nc.dram_tensor("x1s", [B, T, D], F32).ap()

    from contextlib import ExitStack
    with ExitStack() as ctx:
        e = ctx.enter_context
        wd0_h = e(nc.sbuf_tensor("wd0_h", [128, KC * D], FP16))
        wd0_l = e(nc.sbuf_tensor("wd0_l", [128, KC * D], FP16))
        wd1_h = e(nc.sbuf_tensor("wd1_h", [128, KC * D], FP16))
        wd1_l = e(nc.sbuf_tensor("wd1_l", [128, KC * D], FP16))
        wkv_h = e(nc.sbuf_tensor("wkv_h", [128, KC * 2 * S], FP16))
        wkv_l = e(nc.sbuf_tensor("wkv_l", [128, KC * 2 * S], FP16))
        vr_h = e(nc.sbuf_tensor("vr_h", [128, KC * R], FP16))
        vr_l = e(nc.sbuf_tensor("vr_l", [128, KC * R], FP16))
        cts_h = e(nc.sbuf_tensor("cts_h", [96, D], FP16))
        cts_l = e(nc.sbuf_tensor("cts_l", [96, D], FP16))
        wo_h = e(nc.sbuf_tensor("wo_h", [128, D], FP16))
        wo_l = e(nc.sbuf_tensor("wo_l", [128, D], FP16))
        oneg = e(nc.sbuf_tensor("oneg", [128, D], F32))
        ident = e(nc.sbuf_tensor("ident_s", [128, 128], F32))
        wdec = e(nc.sbuf_tensor("wdec", [128, 1], F32))
        negef = e(nc.sbuf_tensor("negef", [128, 1], F32))
        negkb = e(nc.sbuf_tensor("negkb", [128, 1], F32))
        vbias = e(nc.sbuf_tensor("vbias", [128, 1], F32))
        epsv = e(nc.sbuf_tensor("epsv", [128, 1], F32))
        wkvbuf = e(nc.sbuf_tensor("wkvbuf", [128, T], F32))
        st6 = e(nc.sbuf_tensor("st6", [128, 2 * 6 * NST], F32))
        st2 = e(nc.sbuf_tensor("st2", [128, 6 * 2 * NSUB], F32))
        rln = e(nc.sbuf_tensor("rln", [128, 6 * NSUB], F32))
        negmr = e(nc.sbuf_tensor("negmr", [128, 6 * NSUB], F32))
        sc512 = e(nc.sbuf_tensor("sc512", [128, 2 * HCOL], F32))
        eb = e(nc.sbuf_tensor("eb", [128, 2 * SIDE], F32))
        sb = e(nc.sbuf_tensor("sb", [128, 2 * SIDE], F32))
        xvlo = e(nc.sbuf_tensor("xvlo", [R, SIDE], FP16))
        pb = [e(nc.psum_tensor(f"pb{i}", [128, 512], F32)) for i in range(8)]

        semsets = [{n: e(nc.semaphore(f"s{bi}_{n}")) for n in SEMS} for bi in range(3)]

        ctxc = ExitStack()
        blk_clr = ctxc.enter_context(nc.Block())

        def _clear(eng):
            for ss_ in semsets:
                for hh_ in ss_.values():
                    eng.sem_clear(hh_)
        blk_clr.sync(_clear)
        ctxc.close()

        LNRING = 6

        def ln_stats(S_, src, u, waits, lnpar):
            pp = lnpar[0] % LNRING
            assert not lnpar["pend"][pp], "LN stats ring slot still pending apply"
            lnpar["pend"][pp] = True
            st2c = 2 * NSUB * pp + 2 * u
            lnpar[0] += 1
            w_u = list(waits) + [lnpar[1 + pp]]
            for hh in range(NST):
                S_.op("vector", lambda g, u=u, hh=hh, pp=pp: g.bn_stats(
                    st6[:, 6 * NST * (pp % 2) + 6 * hh: 6 * NST * (pp % 2) + 6 * hh + 6],
                    src[:, u * D + hh * STW: u * D + hh * STW + STW]),
                    waits=w_u if hh == 0 else (),
                    sig=(hh == NST - 1))
            fence = S_.now("vector")
            S_.op("vector", lambda g, st2c=st2c: g.bn_aggr(
                st2[:, st2c:st2c + 2],
                st6[:, 6 * NST * (pp % 2): 6 * NST * (pp % 2) + 6 * NST]),
                waits=[fence], sig=True)
            return pp, S_.now("vector")

        def ln_rln(S_, u, pp, ve):
            rlc = NSUB * pp + u
            st2c = 2 * NSUB * pp + 2 * u
            var_ap = st2[:, st2c + 1:st2c + 2]
            S_.op("scalar", lambda g, rlc=rlc, var_ap=var_ap: g.activation(
                rln[:, rlc:rlc + 1], var_ap, ACT.Ln, bias=epsv[:, :]),
                waits=[ve], sig=True)
            S_.op("scalar", lambda g, rlc=rlc: g.activation(
                rln[:, rlc:rlc + 1], rln[:, rlc:rlc + 1], ACT.Exp, scale=-0.5),
                sig=True)
            return S_.now("scalar")

        def ln_apply(S_, src, u, pp, ve, slot_of, lnpar, on_vector=False,
                     rln_sig=None):
            st2c = 2 * NSUB * pp + 2 * u
            rlc = NSUB * pp + u
            m_ap = st2[:, st2c:st2c + 1]
            if rln_sig is None:
                rln_sig = ln_rln(S_, u, pp, ve)
            slot, slot_free = slot_of(u)
            if on_vector:
                S_.op("vector", lambda g, u=u, slot=slot, rlc=rlc, m_ap=m_ap: g.tensor_scalar(
                    slot, src[:, u * D:(u + 1) * D], m_ap, rln[:, rlc:rlc + 1],
                    op0=ALU.subtract, op1=ALU.mult),
                    waits=[rln_sig, slot_free], sig=True)
                sig = S_.now("vector")
            else:
                S_.op("scalar", lambda g, rlc=rlc, m_ap=m_ap: g.activation(
                    negmr[:, rlc:rlc + 1], m_ap, ACT.Identity, scale=-1.0), sig=True)
                S_.op("scalar", lambda g, rlc=rlc: g.activation(
                    negmr[:, rlc:rlc + 1], negmr[:, rlc:rlc + 1], ACT.Identity,
                    scale=rln[:, rlc:rlc + 1]), sig=True)
                S_.op("scalar", lambda g, u=u, slot=slot, rlc=rlc: g.activation(
                    slot, src[:, u * D:(u + 1) * D], ACT.Identity,
                    scale=rln[:, rlc:rlc + 1], bias=negmr[:, rlc:rlc + 1]),
                    waits=[slot_free], sig=True)
                sig = S_.now("scalar")
            lnpar[1 + pp] = sig
            lnpar["pend"][pp] = False
            return slot, sig

        def ln_sub(S_, src, u, waits, slot_of, lnpar):
            pp, ve = ln_stats(S_, src, u, waits, lnpar)
            return ln_apply(S_, src, u, pp, ve, slot_of, lnpar)

        def ln_chain(S_, src, waits, slot_of, lnpar, waits_u=None):
            sigs = []
            slots = []
            for u in range(NSUB):
                w_u = list(waits) if u == 0 else []
                if waits_u is not None:
                    w_u = w_u + [waits_u[u]]
                slot, sig = ln_sub(S_, src, u, w_u, slot_of, lnpar)
                slots.append(slot)
                sigs.append(sig)
            return slots, sigs

        def transpose_stage(S_, slots, apply_sigs, dsth, dstl, tpst):
            cg = [None] * NTG
            for u in range(NSUB):
                for h4 in range(NTG):
                    nck = min(TGRP, KC - h4 * TGRP)
                    bk = 2 + (tpst[0] % 4)
                    tpst[0] += 1
                    waits = [S_.bank_wait(bk), apply_sigs[u]]
                    for cc in range(nck):
                        c = h4 * TGRP + cc
                        S_.op("tensor", lambda g, u=u, c=c, cc=cc, bk=bk: g.matmul(
                            pb[bk][:, cc * 128:(cc + 1) * 128],
                            slots[u][:, c * 128:(c + 1) * 128],
                            ident[:, :], is_transpose=True,
                            start=(cc == 0), stop=(cc == nck - 1)),
                            waits=waits if cc == 0 else (), sig=(cc == nck - 1))
                    te = S_.now("tensor")
                    dh = bass.AP(dsth[:, :].tensor, h4 * TGRP * SIDE + u * 128,
                                 [[KC * SIDE, 128], [SIDE, nck], [1, 128]])
                    dl = bass.AP(dstl[:, :].tensor, h4 * TGRP * SIDE + u * 128,
                                 [[KC * SIDE, 128], [SIDE, nck], [1, 128]])
                    S_.op("scalar", lambda g, bk=bk, dh=dh, nck=nck: g.activation(
                        dh, pb[bk][:, 0:nck * 128], ACT.Identity),
                        waits=[te], sig=True)
                    sc = S_.now("scalar")
                    S_.op("vector", lambda g, bk=bk, dh=dh, dl=dl, nck=nck: g.tensor_tensor(
                        dl, pb[bk][:, 0:nck * 128], dh, op=ALU.subtract),
                        waits=[te, sc], sig=True)
                    S_.bank_set(bk, S_.now("vector"))
                    cg[h4] = S_.now("vector")
            return cg

        xv_bank = [0]
        xv_mod = [2]

        def xvr_stage(S_, xFh, xFl, xvs, ready, lo_free, cg):
            bq = 6 + (xv_bank[0] % xv_mod[0])
            xv_bank[0] += 1
            for p, (lh, rh) in enumerate(((vr_h, xFh), (vr_h, xFl), (vr_l, xFh))):
                for c in range(KC):
                    waits = []
                    if p == 0 and c == 0:
                        waits += [S_.bank_wait(bq)] + list(ready)
                    if p == 0 and c % TGRP == 0:
                        waits.append(cg[c // TGRP])
                    S_.op("tensor", lambda g, p=p, c=c, lh=lh, rh=rh, bq=bq: g.matmul(
                        pb[bq][0:R, 0:SIDE], lh[:, c * R:(c + 1) * R],
                        rh[:, c * SIDE:(c + 1) * SIDE],
                        start=(p == 0 and c == 0), stop=(p == 2 and c == KC - 1)),
                        waits=waits, sig=(p == 2 and c == KC - 1))
            te = S_.now("tensor")
            S_.op("scalar", lambda g, xvs=xvs, bq=bq: g.activation(
                xvs[0:R, :], pb[bq][0:R, 0:SIDE], ACT.Identity), waits=[te], sig=True)
            sc = S_.now("scalar")
            S_.op("vector", lambda g, xvs=xvs, bq=bq: g.tensor_tensor(
                xvlo[0:R, 0:SIDE], pb[bq][0:R, 0:SIDE], xvs[0:R, :], op=ALU.subtract),
                waits=[te, sc, lo_free], sig=True)
            ve = S_.now("vector")
            S_.bank_set(bq, ve)
            S_.op("sync", lambda g, xvs=xvs: g.dma_start(
                out=xvs[R + 1:2 * R + 1, :], in_=xvlo[0:R, 0:SIDE]),
                waits=[ve], sig=True, dma=True)
            return S_.now("sync")

        def dense_stage(S_, xFh, xFl, wh, wl, ready, evac, dnst, xvs=None,
                        cg=None, xv_sig=None, post_u=None):
            evs = []
            for u in range(NSUB):
                for hh in range(NH):
                    bk = dnst[1][dnst[0] % len(dnst[1])]
                    dnst[0] += 1
                    first = (u == 0 and hh == 0)
                    for p, (lh, rh) in enumerate(((xFh, wh), (xFh, wl), (xFl, wh))):
                        for c in range(KC):
                            waits = []
                            if p == 0 and c == 0:
                                waits.append(S_.bank_wait(bk))
                                if first:
                                    waits += list(ready)
                            if first and p == 0 and cg is not None and c % TGRP == 0:
                                waits.append(cg[c // TGRP])
                            S_.op("tensor", lambda g, u=u, hh=hh, bk=bk, c=c, lh=lh, rh=rh, p=p: g.matmul(
                                pb[bk][:, 0:HCOL],
                                lh[:, c * SIDE + u * 128: c * SIDE + u * 128 + 128],
                                rh[:, c * D + hh * HCOL: c * D + hh * HCOL + HCOL],
                                start=(p == 0 and c == 0),
                                stop=(xvs is None and p == 2 and c == KC - 1)),
                                waits=waits, sig=(xvs is None and p == 2 and c == KC - 1))
                    if xvs is not None:
                        for p, rh in enumerate((cts_h, cts_l)):
                            S_.op("tensor", lambda g, u=u, hh=hh, bk=bk, rh=rh, p=p, xvs=xvs: g.matmul(
                                pb[bk][:, 0:HCOL],
                                xvs[0:96, u * 128:(u + 1) * 128],
                                rh[0:96, hh * HCOL:(hh + 1) * HCOL],
                                start=False, stop=(p == 1)),
                                waits=[xv_sig] if (first and p == 0) else (),
                                sig=(p == 1))
                    te = S_.now("tensor")
                    S_.op("vector", lambda g, u=u, hh=hh, bk=bk: evac(
                        g, u, hh, pb[bk][:, 0:HCOL]), waits=[te], sig=True)
                    evs.append(S_.now("vector"))
                    S_.bank_set(bk, evs[-1])
                    if post_u is not None and hh == NH - 1:
                        post_u(u, evs[-1])
            return evs

        ctx0 = ExitStack()
        e0 = ctx0.enter_context
        stg = e0(nc.sbuf_tensor("stg", [128, 2048], F32))
        stg2 = e0(nc.sbuf_tensor("stg2", [128, 2048], F32))
        grow = e0(nc.sbuf_tensor("grow", [1, 3 * D], F32))
        lnwf = e0(nc.sbuf_tensor("lnwf", [128, KC], F32))
        lnbf = e0(nc.sbuf_tensor("lnbf", [128, KC], F32))
        onecol = e0(nc.sbuf_tensor("onecol", [1, 128], F32))
        kvb = e0(nc.sbuf_tensor("kvb", [128, 1], F32))
        blk = e0(nc.Block())
        if True:
            S0 = Sched()
            S0.op("vector", lambda g: g.memset(epsv[:, :], EPS), sig=True)

            def ld(dst, src, waits=(), slow=False):
                return S0.op("sync", lambda g, d=dst, s=src, slow=slow: g.dma_start(
                    out=d, in_=s, allow_slow_non_contiguous=slow),
                    waits=waits, sig=True, dma=True)

            ld(ident[:, :], id_in[:, :])
            w_td0 = ld(wdec[0:S, :], td_in[:, :])
            w_td1 = ld(wdec[S:2 * S, :], td_in[:, :])
            w_tf0 = ld(negef[0:S, :], tf_in[:, :])
            w_tf1 = ld(negef[S:2 * S, :], tf_in[:, :])
            w_sg = ld(grow[0:1, 0:D], sg_in[:, :])
            lnw_src = bass.AP(lnw_in.tensor, 0, [[1, 128], [128, KC]])
            lnb_src = bass.AP(lnb_in.tensor, 0, [[1, 128], [128, KC]])
            w_lnw = ld(lnwf[:, :], lnw_src, slow=True)
            w_lnb = ld(lnbf[:, :], lnb_src, slow=True)
            w_kvt = ld(stg2[:, 0:KC * 2 * S].rearrange("p (c n) -> p c n", c=KC),
                       bass.AP(wkvt_in.tensor, 0, [[2 * S, 128], [128 * 2 * S, KC], [1, 2 * S]]))
            w_vr = ld(stg[:, 0:KC * R].rearrange("p (c n) -> p c n", c=KC),
                      bass.AP(vr_in.tensor, 0, [[R, 128], [128 * R, KC], [1, R]]))
            w_ctb = ld(stg[0:R + 1, 1024:1024 + D], ctb_in[:, :])
            w_wot = ld(stg2[0:S, 1024:1024 + D], wot_in[:, :])
            w_wot2 = ld(stg2[S:2 * S, 1024:1024 + D], wot_in[:, :])

            S0.op("scalar", lambda g: g.activation(wdec[:, :], wdec[:, :], ACT.Exp),
                  waits=[w_td0, w_td1], sig=True)
            S0.op("scalar", lambda g: g.activation(negef[:, :], negef[:, :], ACT.Exp),
                  waits=[w_tf0, w_tf1], sig=True)
            sc_e = S0.now("scalar")
            ve_ef = S0.op("vector", lambda g: g.tensor_scalar_mul(negef[:, :], negef[:, :], -1.0),
                          waits=[sc_e], sig=True)

            S0.op("scalar", lambda g: g.activation(grow[0:1, D:2 * D], grow[0:1, 0:D], ACT.Exp),
                  waits=[w_sg], sig=True)
            S0.op("scalar", lambda g: g.activation(grow[0:1, 2 * D:3 * D], grow[0:1, 0:D],
                                                   ACT.Exp, scale=-1.0), sig=True)
            sc_g = S0.now("scalar")
            S0.op("vector", lambda g: g.tensor_scalar_add(grow[0:1, D:3 * D],
                                                          grow[0:1, D:3 * D], 1.0),
                  waits=[sc_g], sig=True)
            fence_g = S0.now("vector")
            ve_g = S0.op("vector", lambda g: g.reciprocal(grow[0:1, D:3 * D],
                                                         grow[0:1, D:3 * D]),
                         waits=[fence_g], sig=True)
            ve_one = S0.op("vector", lambda g: g.memset(onecol[:, :], 1.0), sig=True)

            for row, dst in ((1, oneg),):
                for hc in range(max(1, D // 512)):
                    w = min(512, D)
                    ssl = slice(row * D + hc * 512, row * D + hc * 512 + w)
                    bk = hc % 2 + (0 if row == 1 else 2)
                    S0.op("tensor", lambda g, ssl=ssl, bk=bk, w=w: g.matmul(
                        pb[bk][:, 0:w], onecol[:, :], grow[0:1, ssl],
                        start=True, stop=True), waits=[ve_g, ve_one], sig=True)
                    te = S0.now("tensor")
                    S0.op("vector", lambda g, hc=hc, w=w, bk=bk, dst=dst: g.tensor_copy(
                        dst[:, hc * 512:hc * 512 + w], pb[bk][:, 0:w]),
                        waits=[te], sig=True)
            ve_gbc = S0.now("vector")

            for c in range(KC):
                S0.op("tensor", lambda g, c=c: g.matmul(
                    pb[4][0:2 * S, 0:1], stg2[:, c * 2 * S:(c + 1) * 2 * S],
                    lnbf[:, c:c + 1], start=(c == 0), stop=(c == KC - 1)),
                    waits=[w_kvt, w_lnb] if c == 0 else (), sig=(c == KC - 1))
            te_kvb = S0.now("tensor")
            ve_kvb = S0.op("vector", lambda g: g.tensor_copy(kvb[0:2 * S, :], pb[4][0:2 * S, 0:1]),
                           waits=[te_kvb], sig=True)
            ve_nkb = S0.op("vector", lambda g: g.tensor_scalar_mul(negkb[0:S, :], kvb[0:S, :], -1.0),
                           waits=[ve_kvb], sig=True)
            g_b1 = S0.op("gpsimd", lambda g: g.dma_start(out=negkb[S:2 * S, :], in_=negkb[0:S, :]),
                         waits=[ve_nkb], sig=True, dma=True)
            g_b2 = S0.op("gpsimd", lambda g: g.dma_start(out=vbias[0:S, :], in_=kvb[S:2 * S, :]),
                         waits=[ve_kvb], sig=True, dma=True)
            g_b3 = S0.op("gpsimd", lambda g: g.dma_start(out=vbias[S:2 * S, :], in_=kvb[S:2 * S, :]),
                         waits=[ve_kvb], sig=True, dma=True)

            for c in range(KC):
                S0.op("vector", lambda g, c=c: g.tensor_scalar_mul(
                    stg2[:, c * 2 * S:(c + 1) * 2 * S], stg2[:, c * 2 * S:(c + 1) * 2 * S],
                    lnwf[:, c:c + 1]),
                    waits=[w_lnw, te_kvb] if c == 0 else (), sig=(c == KC - 1))
            f1_ = S0.now("vector")
            S0.op("vector", lambda g: g.tensor_copy(wkv_h[:, :], stg2[:, 0:KC * 2 * S]),
                  waits=[f1_], sig=True)
            f2_ = S0.now("vector")
            S0.op("vector", lambda g: g.tensor_tensor(wkv_l[:, :], stg2[:, 0:KC * 2 * S],
                                                      wkv_h[:, :], op=ALU.subtract),
                  waits=[f2_], sig=True)
            S0.op("vector", lambda g: g.tensor_copy(vr_h[:, :], stg[:, 0:KC * R]),
                  waits=[w_vr], sig=True)
            f3_ = S0.now("vector")
            S0.op("vector", lambda g: g.tensor_tensor(vr_l[:, :], stg[:, 0:KC * R],
                                                      vr_h[:, :], op=ALU.subtract),
                  waits=[f3_], sig=True)
            S0.op("vector", lambda g: g.memset(cts_h[:, :], 0.0), sig=True)
            S0.op("vector", lambda g: g.memset(cts_l[:, :], 0.0), sig=True)
            zc = S0.now("vector")
            S0.op("vector", lambda g: g.tensor_copy(cts_h[0:R + 1, :], stg[0:R + 1, 1024:1024 + D]),
                  waits=[w_ctb, zc], sig=True)
            f4_ = S0.now("vector")
            S0.op("vector", lambda g: g.tensor_tensor(cts_l[0:R + 1, :], stg[0:R + 1, 1024:1024 + D],
                                                      cts_h[0:R + 1, :], op=ALU.subtract),
                  waits=[f4_], sig=True)
            g_ct = S0.op("gpsimd", lambda g: g.dma_start(
                out=cts_h[R + 1:2 * R + 2, :], in_=cts_h[0:R + 1, :]),
                waits=[f4_], sig=True, dma=True)
            S0.op("vector", lambda g: g.memset(wo_l[:, :], 0.0), sig=True)
            zw = S0.now("vector")
            S0.op("vector", lambda g: g.tensor_copy(wo_h[:, :], stg2[:, 1024:1024 + D]),
                  waits=[w_wot, w_wot2], sig=True)
            f5_ = S0.now("vector")
            S0.op("vector", lambda g: g.tensor_tensor(wo_l[0:S, :], stg2[0:S, 1024:1024 + D],
                                                      wo_h[0:S, :], op=ALU.subtract),
                  waits=[f5_, zw], sig=True)
            ve_small = S0.now("vector")

            S0.op("gpsimd", lambda g: g.nop(), waits=[g_b1, g_b2, g_b3, g_ct])
            S0.op("sync", lambda g: g.nop(), waits=[ve_ef, S0.now("vector")])
            emit(blk, S0, semsets[0])
        ctx0.close()

        ctx1 = ExitStack()
        e1 = ctx1.enter_context
        PAr = [e1(nc.sbuf_tensor(f"pa{i}", [128, D * NSUB], F32)) for i in range(3)]
        PBr = [e1(nc.sbuf_tensor(f"pbx{i}", [128, D * NSUB], F32)) for i in range(3)]
        yF = [(e1(nc.sbuf_tensor(f"yfh{i}", [128, KC * SIDE], FP16)),
               e1(nc.sbuf_tensor(f"yfl{i}", [128, KC * SIDE], FP16))) for i in range(2)]
        F1 = [(e1(nc.sbuf_tensor(f"f1h{i}", [128, KC * SIDE], FP16)),
               e1(nc.sbuf_tensor(f"f1l{i}", [128, KC * SIDE], FP16))) for i in range(2)]
        Cn = e1(nc.sbuf_tensor("cn", [128, CNS * D], F32))
        xvs2 = [e1(nc.sbuf_tensor(f"xvs{i}", [128, SIDE], FP16)) for i in range(2)]
        blk = e1(nc.Block())
        if True:
            S1 = Sched()
            tpst = [0]
            dnst = [0, (0, 1)]
            cn_ring = [0]
            cn_sig = [None] * CNS
            lnpar = LnPar()

            for xs_ in xvs2:
                S1.op("vector", lambda g, xs_=xs_: g.memset(xs_[:, :], 0.0), sig=True)
                zv = S1.now("vector")
                S1.op("vector", lambda g, xs_=xs_: g.memset(xs_[R:R + 1, :], 1.0),
                      waits=[zv], sig=True)
            ve_ones = S1.now("vector")

            ring_trace = []

            def ring_slot(u):
                i = cn_ring[0] % CNS
                cn_ring[0] += 1
                ring_trace.append(i)
                return Cn[:, i * D:(i + 1) * D], cn_sig[i]

            def run_ln(src, waits, dst=None, waits_u=None):
                if dst is None:
                    i0 = cn_ring[0]
                    slots, sigs = ln_chain(S1, src, waits, ring_slot, lnpar,
                                           waits_u=waits_u)
                    return slots, sigs, i0
                slots, sigs = ln_chain(
                    S1, src, waits,
                    lambda u: (dst[:, u * D:(u + 1) * D], None), lnpar,
                    waits_u=waits_u)
                return slots, sigs, None

            def run_T(slots, sigs, i0, dsth, dstl, ring_idx=None):
                cg = transpose_stage(S1, slots, sigs, dsth, dstl, tpst)
                te_sig = S1.now("tensor")
                if i0 is not None:
                    for u in range(NSUB):
                        cn_sig[(i0 + u) % CNS] = te_sig
                if ring_idx is not None:
                    for i in ring_idx:
                        cn_sig[i] = te_sig
                return cg

            def load_x(dst, b, t0, waits):
                ap_s = bass.AP(x_in.tensor, b * T * D + t0 * D,
                               [[D, 128], [128 * D, NSUB], [1, D]])
                S1.op("sync", lambda g, dst=dst, ap_s=ap_s: g.dma_start(
                    out=dst[:, :].rearrange("p (u d) -> p u d", u=NSUB), in_=ap_s),
                    waits=waits, sig=True, dma=True)
                return S1.now("sync")

            def kv_stage(xFh, xFl, b, t0, cg):
                lo, hi = b * S, (b + 1) * S
                kvp = kvpar[0] % 2
                c0 = kvp * SIDE
                kvpar[0] += 1
                tp = (0, b * S) if b > 0 else None
                te_sigs = []
                for bank, col0 in ((0, 0), (1, S)):
                    for p, (lh, rh) in enumerate(((wkv_h, xFh), (wkv_h, xFl), (wkv_l, xFh))):
                        for c in range(KC):
                            waits = []
                            if p == 0 and c == 0:
                                waits.append(S1.bank_wait(bank))
                            if bank == 6 and p == 0 and c % TGRP == 0:
                                waits.append(cg[c // TGRP])
                            S1.op("tensor", lambda g, bank=bank, col0=col0, p=p, c=c, lh=lh, rh=rh, lo=lo, hi=hi, tp=tp: g.matmul(
                                pb[bank][lo:hi, 0:SIDE],
                                lh[:, c * 2 * S + col0: c * 2 * S + col0 + S],
                                rh[:, c * SIDE:(c + 1) * SIDE],
                                start=(p == 0 and c == 0), stop=(p == 2 and c == KC - 1),
                                tile_position=tp),
                                waits=waits,
                                sig=(p == 2 and c == KC - 1))
                    te_sigs.append(S1.now("tensor"))
                te_k, te_v = te_sigs
                S1.op("scalar", lambda g, lo=lo, hi=hi, c0=c0: g.activation(
                    eb[lo:hi, c0:c0 + SIDE], pb[0][lo:hi, 0:SIDE], ACT.Exp,
                    bias=negkb[lo:hi, :], scale=-1.0),
                    waits=[te_k, kv_free[kvp]], sig=True)
                sc1 = S1.now("scalar")
                S1.bank_set(0, sc1)
                S1.op("gpsimd", lambda g, lo=lo, hi=hi, c0=c0: g.tensor_scalar_add(
                    eb[lo:hi, c0:c0 + SIDE], eb[lo:hi, c0:c0 + SIDE], 1.0),
                    waits=[sc1], sig=True, sem="gpsimdc")
                fe = S1.now("gpsimdc")
                S1.op("vector", lambda g, lo=lo, hi=hi, c0=c0: g.reciprocal(
                    sb[lo:hi, c0:c0 + SIDE], eb[lo:hi, c0:c0 + SIDE]),
                    waits=[fe], sig=True)
                ve1 = S1.now("vector")
                S1.op("scalar", lambda g, lo=lo, hi=hi, c0=c0: g.activation(
                    eb[lo:hi, c0:c0 + SIDE], sb[lo:hi, c0:c0 + SIDE], ACT.Exp,
                    scale=negef[lo:hi, :]),
                    waits=[ve1], sig=True)
                sc2 = S1.now("scalar")
                S1.op("vector", lambda g, lo=lo, hi=hi, t0=t0, c0=c0: g.scalar_tensor_tensor(
                    wkvbuf[lo:hi, t0:t0 + SIDE], pb[1][lo:hi, 0:SIDE], vbias[lo:hi, :],
                    eb[lo:hi, c0:c0 + SIDE], op0=ALU.add, op1=ALU.mult),
                    waits=[te_v, sc2], sig=True)
                ve2 = S1.now("vector")
                kv_free[kvp] = ve2
                S1.bank_set(1, ve2)
                return ve2

            def wsplit(chunks, wsrc, dh, dl, scale_by_g):
                last = None
                for c in chunks:
                    del ring_trace[:]
                    slot, sfree = ring_slot(0)
                    i_ = ring_trace[0]
                    S1.op("sync", lambda g, slot=slot, wsrc=wsrc, c=c: g.dma_start(
                        out=slot, in_=bass.AP(wsrc.tensor, c * 128 * D,
                                              [[D, 128], [1, D]])),
                        waits=[sfree], sig=True, dma=True)
                    wld = S1.now("sync")
                    waits = [wld]
                    if scale_by_g:
                        del ring_trace[:]
                        slot2, sfree2 = ring_slot(0)
                        i2_ = ring_trace[0]
                        S1.op("vector", lambda g, slot=slot, slot2=slot2: g.tensor_tensor(
                            slot2, slot, oneg[:, 0:D], op=ALU.mult),
                            waits=[wld, sfree2], sig=True)
                        fw = S1.now("vector")
                        S1.op("vector", lambda g, slot=slot, slot2=slot2: g.tensor_tensor(
                            slot, slot, slot2, op=ALU.subtract),
                            waits=[fw], sig=True)
                        cn_sig[i2_] = S1.now("vector")
                        waits = [S1.now("vector")]
                    S1.op("vector", lambda g, slot=slot, dh=dh, c=c: g.tensor_copy(
                        dh[:, c * D:(c + 1) * D], slot), waits=waits, sig=True)
                    fw2 = S1.now("vector")
                    S1.op("vector", lambda g, slot=slot, dh=dh, dl=dl, c=c: g.tensor_tensor(
                        dl[:, c * D:(c + 1) * D], slot, dh[:, c * D:(c + 1) * D],
                        op=ALU.subtract), waits=[fw2], sig=True)
                    last = S1.now("vector")
                    cn_sig[i_] = last
                return last

            pa_free = [None, None, None]
            pb_free = [[], [], []]
            xv_lo_free = None
            carry = None
            wd0_ready = None
            wd1_ready = None
            kvpar = [0]
            kv_free = [None, None]
            carry_n = {}

            pairs = [(b, pr) for b in range(B) for pr in range(NPAIR)]
            nside = 0
            for pi, (b, pr) in enumerate(pairs):
                if True:
                    t0A = pr * SIDE
                    t0B = t0A + half
                    sA = nside
                    nside += 2
                    iA, iB = sA % 3, (sA + 1) % 3
                    jA, jB = 0, 1
                    if carry is None:
                        lxA = load_x(PAr[iA], b, t0A, [pa_free[iA]])
                        lxB = load_x(PAr[iB], b, t0B, [pa_free[iB]])
                        slA, sgA, i0A = run_ln(PAr[iA], [lxA])
                        slB, sgB, i0B = run_ln(PAr[iB], [lxB])
                        cgA = run_T(slA, sgA, i0A, F1[jA][0], F1[jA][1])
                        cgB = run_T(slB, sgB, i0B, F1[jB][0], F1[jB][1])
                    else:
                        cgA, cgB = carry
                    xvA = xvr_stage(S1, F1[jA][0], F1[jA][1], xvs2[jA],
                                    [ve_ones], xv_lo_free, cgA)
                    if pi == 0:
                        wd0_ready = wsplit(range(KC), wt_in, wd0_h, wd0_l, False)
                    xvB = xvr_stage(S1, F1[jB][0], F1[jB][1], xvs2[jB],
                                    [ve_ones], xvA, cgB)
                    xv_lo_free = xvB
                    ev1 = {}
                    ysl = {}
                    for si, ii, jj, xv, cg, t0 in (("A", iA, jA, xvA, cgA, t0A),
                                                   ("B", iB, jB, xvB, cgB, t0B)):
                        def evac_x1(g, u, hh, ps, ii=ii):
                            return g.scalar_tensor_tensor(
                                PBr[ii][:, u * D + hh * HCOL: u * D + hh * HCOL + HCOL],
                                ps, 0.5,
                                PAr[ii][:, u * D + hh * HCOL: u * D + hh * HCOL + HCOL],
                                op0=ALU.mult, op1=ALU.add)
                        yslots = []
                        ysigs = []
                        deferred = []

                        def post_u(u, ev, si=si, ii=ii, t0=t0, yslots=yslots,
                                   ysigs=ysigs, deferred=deferred):
                            ap_d = bass.AP(x1_hbm.tensor,
                                           b * T * D + (t0 + u * 128) * D,
                                           [[D, 128], [1, D]])
                            S1.op("scalar", lambda g, ii=ii, ap_d=ap_d, u=u: g.dma_start(
                                out=ap_d, in_=PBr[ii][:, u * D:(u + 1) * D]),
                                waits=[ev], sig=True, dma=True)
                            if si == "B" and u == NSUB - 1:
                                deferred.append((u, ev))
                                return
                            sl, sg = ln_sub(S1, PBr[ii], u, [ev],
                                            lambda uu: (PAr[ii][:, uu * D:(uu + 1) * D],
                                                        None), lnpar)
                            yslots.append(sl)
                            ysigs.append(sg)
                        evs = dense_stage(S1, F1[jj][0], F1[jj][1], wd0_h, wd0_l,
                                          pb_free[ii] + ([wd0_ready] if pi == 0 else []),
                                          evac_x1, dnst,
                                          xvs=xvs2[jj], cg=cg, xv_sig=xv,
                                          post_u=post_u)
                        ev1[si] = evs[-1]
                        ev1[si + "sp"] = S1.now("scalar")
                        ysl[si] = (yslots, ysigs, deferred, ii)
                        if pi == 0:
                            half_c = range(0, KC // 2) if si == "A" else range(KC // 2, KC)
                            wd1_ready = wsplit(half_c, wst_in, wd1_h, wd1_l, True)
                    ycgA = run_T(ysl["A"][0], ysl["A"][1], None, yF[0][0], yF[0][1])
                    teA2 = S1.now("tensor")
                    onegb = bass.AP(oneg[:, :].tensor, 0,
                                    [[D, 128], [0, NSUB], [1, D]])
                    S1.op("vector", lambda g, iA=iA, onegb=onegb: g.tensor_tensor(
                        PAr[iA][:, :].rearrange("p (u d) -> p u d", u=NSUB),
                        PAr[iA][:, :].rearrange("p (u d) -> p u d", u=NSUB),
                        onegb, op=ALU.mult),
                        waits=[teA2, ysl["A"][1][-1]], sig=True)
                    ugA = S1.now("vector")
                    for (du, dev) in ysl["B"][2]:
                        iiB = ysl["B"][3]
                        sl, sg = ln_sub(S1, PBr[iiB], du, [dev],
                                        lambda uu: (PAr[iiB][:, uu * D:(uu + 1) * D],
                                                    None), lnpar)
                        ysl["B"][0].append(sl)
                        ysl["B"][1].append(sg)
                    ycgB = run_T(ysl["B"][0], ysl["B"][1], None, yF[1][0], yF[1][1])
                    teB2 = S1.now("tensor")
                    S1.op("vector", lambda g, iB=iB, onegb=onegb: g.tensor_tensor(
                        PAr[iB][:, :].rearrange("p (u d) -> p u d", u=NSUB),
                        PAr[iB][:, :].rearrange("p (u d) -> p u d", u=NSUB),
                        onegb, op=ALU.mult),
                        waits=[teB2, ysl["B"][1][-1]], sig=True)
                    ugB = S1.now("vector")
                    usig = {"A": ugA, "B": ugB}
                    sl3 = {}
                    for si, ii, jj, oj, t0 in (("A", iA, jA, 1, t0A),
                                               ("B", iB, jB, 0, t0B)):
                        ycg = ycgB if si == "A" else ycgA
                        last_ev = None
                        uev = []
                        for u in range(NSUB):
                            for hh in range(NH):
                                bk = dnst[1][dnst[0] % len(dnst[1])]
                                dnst[0] += 1
                                first = (u == 0 and hh == 0)
                                for p, (lh, rh) in enumerate(((yF[oj][0], wd1_h),
                                                              (yF[oj][0], wd1_l),
                                                              (yF[oj][1], wd1_h))):
                                    for c in range(KC):
                                        waits = []
                                        if p == 0 and c == 0:
                                            waits.append(S1.bank_wait(bk))
                                            if first:
                                                waits.append(ev1[si + "sp"])
                                                if pi == 0:
                                                    waits.append(wd1_ready)
                                        if first and p == 0 and c % TGRP == 0:
                                            waits.append(ycg[c // TGRP])
                                        S1.op("tensor", lambda g, u=u, hh=hh, bk=bk, c=c, lh=lh, rh=rh, p=p: g.matmul(
                                            pb[bk][:, 0:HCOL],
                                            lh[:, c * SIDE + u * 128: c * SIDE + u * 128 + 128],
                                            rh[:, c * D + hh * HCOL: c * D + hh * HCOL + HCOL],
                                            start=(p == 0 and c == 0),
                                            stop=(p == 2 and c == KC - 1)),
                                            waits=waits,
                                            sig=(p == 2 and c == KC - 1))
                                te = S1.now("tensor")
                                S1.op("vector", lambda g, u=u, hh=hh, bk=bk, ii=ii: g.tensor_tensor(
                                    PBr[ii][:, u * D + hh * HCOL: u * D + hh * HCOL + HCOL],
                                    pb[bk][:, 0:HCOL],
                                    PAr[ii][:, u * D + hh * HCOL: u * D + hh * HCOL + HCOL],
                                    op=ALU.add),
                                    waits=[te, usig[si]] if (u == 0 and hh == 0) else [te],
                                    sig=True)
                                last_ev = S1.now("vector")
                                S1.bank_set(bk, last_ev)
                                if hh == NH - 1:
                                    if si == "B" and u == NSUB - 1:
                                        uev.append((u, last_ev, True))
                                    else:
                                        del ring_trace[:]
                                        sl_, sg_ = ln_sub(S1, PBr[ii], u, [last_ev],
                                                          ring_slot, lnpar)
                                        uev.append((sl_, sg_, False, ring_trace[0]))
                        ev1[si + "xs"] = last_ev
                        sl3[si] = (uev, ii)
                        if pi + 1 < len(pairs):
                            nb_, npr_ = pairs[pi + 1]
                            if si == "A":
                                ni = nside % 3
                                nt0 = npr_ * SIDE
                            else:
                                ni = (nside + 1) % 3
                                nt0 = npr_ * SIDE + half
                                pa_free[iA] = ev1["Axs"]
                            nlx = load_x(PAr[ni], nb_, nt0, [pa_free[ni]])
                            stats_n = []
                            for u in range(NSUB):
                                pp, ve = ln_stats(S1, PAr[ni], u,
                                                  [nlx] if u == 0 else [], lnpar)
                                stats_n.append((pp, ve, ln_rln(S1, u, pp, ve)))
                            carry_n[si] = (ni, stats_n)
                    slotsA = [x[0] for x in sl3["A"][0]]
                    sigsA = [x[1] for x in sl3["A"][0]]
                    idxA = [x[3] for x in sl3["A"][0]]
                    cg3A = run_T(slotsA, sigsA, None, F1[jA][0], F1[jA][1],
                                 ring_idx=idxA)
                    slotsB = []
                    sigsB = []
                    idxB = []
                    for x in sl3["B"][0]:
                        if x[2]:
                            del ring_trace[:]
                            sl_, sg_ = ln_sub(S1, PBr[sl3["B"][1]], x[0], [x[1]],
                                              ring_slot, lnpar)
                            slotsB.append(sl_)
                            sigsB.append(sg_)
                            idxB.append(ring_trace[0])
                        else:
                            slotsB.append(x[0])
                            sigsB.append(x[1])
                            idxB.append(x[3])
                    cg3B = run_T(slotsB, sigsB, None, F1[jB][0], F1[jB][1],
                                 ring_idx=idxB)
                    kv_stage(F1[jA][0], F1[jA][1], b, t0A, cg3A)
                    pb_free[iA] = [S1.now("vector"), ev1["Asp"]]
                    pa_free[iA] = ev1["Axs"]
                    pa_free[iB] = ev1["Bxs"]
                    carry = None
                    if pi + 1 < len(pairs):
                        cc = []
                        for si2 in ("A", "B"):
                            ni, stats_n = carry_n[si2]
                            slots = []
                            sigs = []
                            idxs = []
                            for u in range(NSUB):
                                pp, ve, rs = stats_n[u]
                                del ring_trace[:]
                                sl_, sg_ = ln_apply(S1, PAr[ni], u, pp, ve,
                                                    ring_slot, lnpar,
                                                    on_vector=True, rln_sig=rs)
                                slots.append(sl_)
                                sigs.append(sg_)
                                idxs.append(ring_trace[0])
                            cc.append((slots, sigs, idxs))
                        ncgA = run_T(cc[0][0], cc[0][1], None,
                                     F1[jA][0], F1[jA][1], ring_idx=cc[0][2])
                        carry = (cc, ncgA)
                    kv_stage(F1[jB][0], F1[jB][1], b, t0B, cg3B)
                    pb_free[iB] = [S1.now("vector"), ev1["Bsp"]]
                    if carry is not None:
                        cc, ncgA = carry
                        ncgB = run_T(cc[1][0], cc[1][1], None,
                                     F1[jB][0], F1[jB][1], ring_idx=cc[1][2])
                        carry = (ncgA, ncgB)

            S1.op("scalar", lambda g: g.nop(), waits=[S1.now("scalar")])
            emit(blk, S1, semsets[1])
        ctx1.close()

        ctx2 = ExitStack()
        e2 = ctx2.enter_context
        q2 = [e2(nc.sbuf_tensor(f"q{i}", [128, D * NSUB], F32)) for i in range(3)]
        x2b2 = [e2(nc.sbuf_tensor(f"x2b{i}", [128, D * NSUB], F32)) for i in range(2)]
        Cn2 = e2(nc.sbuf_tensor("cn2", [128, CNS * D], F32))
        G1 = [(e2(nc.sbuf_tensor(f"g1h{i}", [128, KC * SIDE], FP16)),
               e2(nc.sbuf_tensor(f"g1l{i}", [128, KC * SIDE], FP16))) for i in range(2)]
        xv22 = [e2(nc.sbuf_tensor(f"xv2{i}", [128, SIDE], FP16)) for i in range(2)]
        stHL = e2(nc.sbuf_tensor("sthl", [128, B * T], FP16))
        states = e2(nc.sbuf_tensor("states", [128, T], F32))
        blk = e2(nc.Block())
        if True:
            S3 = Sched()
            xv_mod[0] = 1
            tpst3 = [0]
            dnst3 = [0, (0, 1, 7)]
            cn_ring3 = [0]
            cn_sig3 = [None] * CNS
            lnpar3 = LnPar()

            for xs_ in xv22:
                S3.op("vector", lambda g, xs_=xs_: g.memset(xs_[:, :], 0.0), sig=True)
                zv = S3.now("vector")
                S3.op("vector", lambda g, xs_=xs_: g.memset(xs_[R:R + 1, :], 1.0),
                      waits=[zv], sig=True)
            ve_ones3 = S3.now("vector")

            def ring_slot3(u):
                i = cn_ring3[0] % CNS
                cn_ring3[0] += 1
                return Cn2[:, i * D:(i + 1) * D], cn_sig3[i]

            def run_ln3(src, waits, waits_u=None):
                i0 = cn_ring3[0]
                slots, sigs = ln_chain(S3, src, waits, ring_slot3, lnpar3,
                                       waits_u=waits_u)
                return slots, sigs, i0

            def run_T3(slots, sigs, i0, dsth, dstl):
                cg = transpose_stage(S3, slots, sigs, dsth, dstl, tpst3)
                te_sig = S3.now("tensor")
                for u in range(NSUB):
                    cn_sig3[(i0 + u) % CNS] = te_sig
                return cg

            st_ready = {}
            wbc = wdec[:, 0:1].broadcast_to((128, T))

            def scan_seq(bq):
                lo, hi = bq * S, (bq + 1) * S
                S3.op("vector", lambda g, lo=lo, hi=hi: g.tensor_tensor_scan(
                    states[lo:hi, :], wbc[lo:hi, :], wkvbuf[lo:hi, :], 0.0,
                    op0=ALU.mult, op1=ALU.add), sig=True)
                ve_scan = S3.now("vector")
                if bq == 0:
                    S3.op("scalar", lambda g: g.activation(
                        stHL[0:S, 0:T], states[0:S, :], ACT.Identity),
                        waits=[ve_scan], sig=True)
                    sc_hi = S3.now("scalar")
                    S3.op("vector", lambda g: g.tensor_tensor(
                        wkvbuf[0:S, 0:T], states[0:S, :], stHL[0:S, 0:T],
                        op=ALU.subtract), waits=[sc_hi], sig=True)
                    ve_lo = S3.now("vector")
                    S3.op("gpsimd", lambda g: g.dma_start(
                        out=stHL[S:2 * S, 0:T], in_=wkvbuf[0:S, 0:T]),
                        waits=[ve_lo], sig=True, dma=True)
                else:
                    S3.op("scalar", lambda g: g.activation(
                        stHL[S:2 * S, T:2 * T], states[S:2 * S, :], ACT.Identity),
                        waits=[ve_scan], sig=True)
                    sc_hi = S3.now("scalar")
                    S3.op("gpsimd", lambda g: g.dma_start(
                        out=stHL[0:S, T:2 * T], in_=stHL[S:2 * S, T:2 * T]),
                        waits=[sc_hi], sig=True, dma=True)
                    gp_hi = S3.now("gpsimd")
                    S3.op("vector", lambda g: g.tensor_tensor(
                        stHL[S:2 * S, T:2 * T], states[S:2 * S, :],
                        stHL[S:2 * S, T:2 * T],
                        op=ALU.subtract), waits=[gp_hi], sig=True)
                st_ready[bq] = [S3.now("gpsimd"), S3.now("vector"), S3.now("scalar")]

            scan_seq(0)

            def load_x1(dst, b, t0, waits):
                ap_sx = bass.AP(x1_hbm.tensor, b * T * D + t0 * D,
                                [[D, 128], [128 * D, NSUB], [1, D]])
                S3.op("sync", lambda g, dst=dst, ap_sx=ap_sx: g.dma_start(
                    out=dst[:, :].rearrange("p (u d) -> p u d", u=NSUB), in_=ap_sx),
                    waits=waits, sig=True, dma=True)
                return S3.now("sync")

            free_q = [None, None, None]

            def stage1_op(i, b, t0):
                par = i % 2
                qar = i % 3
                lx = load_x1(q2[qar], b, t0, [free_q[qar]])
                uev = []
                for u in range(NSUB):
                    for hh in range(NH):
                        bk = dnst3[1][dnst3[0] % len(dnst3[1])]
                        dnst3[0] += 1
                        waits = [S3.bank_wait(bk)]
                        if u == 0 and hh == 0:
                            waits += st_ready[b]
                        for p, rh in enumerate((wo_h, wo_l)):
                            S3.op("tensor", lambda g, u=u, hh=hh, bk=bk, rh=rh, p=p, b=b, t0=t0: g.matmul(
                                pb[bk][:, 0:HCOL],
                                stHL[:, b * T + t0 + u * 128: b * T + t0 + u * 128 + 128],
                                rh[:, hh * HCOL:(hh + 1) * HCOL],
                                start=(p == 0), stop=(p == 1)),
                                waits=waits if p == 0 else (), sig=(p == 1))
                        te = S3.now("tensor")
                        S3.op("vector", lambda g, u=u, hh=hh, bk=bk, par=par, qar=qar: g.tensor_tensor(
                            x2b2[par][:, u * D + hh * HCOL: u * D + hh * HCOL + HCOL],
                            pb[bk][:, 0:HCOL],
                            q2[qar][:, u * D + hh * HCOL: u * D + hh * HCOL + HCOL],
                            op=ALU.add),
                            waits=[te, lx] if (u == 0 and hh == 0) else [te], sig=True)
                        last = S3.now("vector")
                        S3.bank_set(bk, last)
                        if hh == NH - 1:
                            uev.append(last)
                return par, qar, uev

            def stage1_ln(par, uev):
                i0 = cn_ring3[0]
                uln = [ln_sub(S3, x2b2[par], u, [uev[u]], ring_slot3, lnpar3)
                       for u in range(NSUB)]
                return [x[0] for x in uln], [x[1] for x in uln], i0

            xv_lo_free3 = [None]

            def stage2_T(i, par, slots, sigs, i0):
                return run_T3(slots, sigs, i0, G1[par][0], G1[par][1])

            def stage2(i, b, t0, par, qar, cg):
                xv = xvr_stage(S3, G1[par][0], G1[par][1], xv22[par],
                               [ve_ones3] if i < 2 else [], xv_lo_free3[0], cg)
                xv_lo_free3[0] = xv

                def evac_x3(g, u, hh, ps, par=par, qar=qar):
                    return g.scalar_tensor_tensor(
                        q2[qar][:, u * D + hh * HCOL: u * D + hh * HCOL + HCOL],
                        ps, 0.5,
                        x2b2[par][:, u * D + hh * HCOL: u * D + hh * HCOL + HCOL],
                        op0=ALU.mult, op1=ALU.add)
                evs = dense_stage(S3, G1[par][0], G1[par][1], wd0_h, wd0_l,
                                  [], evac_x3, dnst3, xvs=xv22[par],
                                  cg=cg, xv_sig=xv)
                last = None
                for u in range(NSUB):
                    ap_d = bass.AP(out_hbm.tensor, b * T * D + (t0 + u * 128) * D,
                                   [[D, 128], [1, D]])
                    S3.op("scalar", lambda g, qar=qar, ap_d=ap_d, u=u: g.dma_start(
                        out=ap_d, in_=q2[qar][:, u * D:(u + 1) * D]),
                        waits=[evs[u * NH + NH - 1]], sig=True, dma=True)
                    last = S3.now("scalar")
                free_q[qar] = last
                return last

            sides = [(b, t0) for b in range(B) for t0 in range(0, T, SIDE)]
            pend = None
            out_last = None
            for i, (b, t0) in enumerate(sides):
                if i == 1 and B > 1:
                    scan_seq(1)
                par, qar, uev = stage1_op(i, b, t0)
                if pend is not None:
                    (pi, pbb, pt0, ppar, pqar, pslots, psigs, pi0) = pend
                    pcg = stage2_T(pi, ppar, pslots, psigs, pi0)
                slots, sigs, i0 = stage1_ln(par, uev)
                if pend is not None:
                    out_last = stage2(pi, pbb, pt0, ppar, pqar, pcg)
                pend = (i, b, t0, par, qar, slots, sigs, i0)
            (pi, pbb, pt0, ppar, pqar, pslots, psigs, pi0) = pend
            pcg = stage2_T(pi, ppar, pslots, psigs, pi0)
            out_last = stage2(pi, pbb, pt0, ppar, pqar, pcg)

            S3.op("scalar", lambda g: g.nop(), waits=[out_last])
            emit(blk, S3, semsets[2])
        ctx2.close()

    return nc


def kernel(L0_attn_x, L0_attn_time_decay, L0_attn_time_first, L0_attn_W_key,
           L0_attn_W_value, L0_attn_W_output, L0_attn_W_shift, L0_attn_shift_gate,
           L0_attn_ln_w, L0_attn_ln_b, L0_ffn_W, L0_ffn_b, L0_ffn_V_r, L0_ffn_C):
    N = 8
    x = np.ascontiguousarray(np.asarray(L0_attn_x, dtype=np.float32))
    wt = np.ascontiguousarray(np.asarray(L0_ffn_W, dtype=np.float32).T)
    wst = np.ascontiguousarray(np.asarray(L0_attn_W_shift, dtype=np.float32).T)
    wkvt = np.ascontiguousarray(np.concatenate(
        [np.asarray(L0_attn_W_key, np.float32).T,
         np.asarray(L0_attn_W_value, np.float32).T], axis=1))
    vr = np.ascontiguousarray(np.asarray(L0_ffn_V_r, np.float32))
    ctb = np.ascontiguousarray(np.concatenate(
        [np.asarray(L0_ffn_C, np.float32).T,
         np.asarray(L0_ffn_b, np.float32)[None, :]], axis=0))
    wot = np.ascontiguousarray(np.asarray(L0_attn_W_output, np.float32).T)
    td = np.ascontiguousarray(np.asarray(L0_attn_time_decay, np.float32)[:, None])
    tf = np.ascontiguousarray(np.asarray(L0_attn_time_first, np.float32)[:, None])
    sg = np.ascontiguousarray(np.asarray(L0_attn_shift_gate, np.float32)[None, :])
    lnw = np.ascontiguousarray(np.asarray(L0_attn_ln_w, np.float32))
    lnb = np.ascontiguousarray(np.asarray(L0_attn_ln_b, np.float32))
    ident = np.eye(128, dtype=np.float32)

    nc = build_program()
    in_maps = []
    for c in range(N):
        in_maps.append(dict(x=np.ascontiguousarray(x[2 * c:2 * c + 2]),
                            wt=wt, wst=wst, wkvt=wkvt, vr=vr, ctb=ctb, wot=wot,
                            td=td, tf=tf, sg=sg, lnw=lnw, lnb=lnb, ident=ident))
    res = run_bass_kernel_spmd(nc, in_maps, core_ids=list(range(N)))
    return np.concatenate([np.asarray(r["out"], dtype=np.float32)
                           for r in res.results], axis=0)


# version 27
# speedup vs baseline: 1.0223x; 1.0223x over previous
import numpy as np
import concourse.bass as bass
import concourse.mybir as mybir
from concourse.bass_utils import run_bass_kernel_spmd

F32 = mybir.dt.float32
FP16 = mybir.dt.float16
ALU = mybir.AluOpType
ACT = mybir.ActivationFunctionType

ENGINES = ("sync", "tensor", "vector", "scalar", "gpsimd")
SEMS = ENGINES + ("gpsimdc",)
EPS = 1e-5
R = 32


class Sched:

    def __init__(self):
        self.ops = {e: [] for e in ENGINES}
        self.cnt = {e: 0 for e in SEMS}
        self.bank_hist = {}

    def op(self, eng, fn, waits=(), sig=False, dma=False, sem=None):
        ws = [w for w in waits if w is not None]
        for (e, v) in ws:
            assert v <= self.cnt[e], f"wait on future {e}>={v} (now {self.cnt[e]})"
        amt = 16 if dma else 1
        sname = sem or eng
        if sig:
            self.cnt[sname] += amt
        self.ops[eng].append((fn, tuple(ws), sig and sname, amt))
        return (sname, self.cnt[sname]) if sig else None

    def now(self, eng):
        return (eng, self.cnt[eng])

    def bank_wait(self, bk):
        return self.bank_hist.get(bk)

    def bank_set(self, bk, sig):
        self.bank_hist[bk] = sig


class LnPar(dict):

    def __init__(self):
        super().__init__()
        self[0] = 0
        for i in range(1, 7):
            self[i] = None
        self["pend"] = [False] * 6


def emit(blk, sched, sems):
    for ename in ENGINES:
        ops = sched.ops[ename]
        if not ops:
            continue

        def body(eng, ops=ops):
            for fn, waits, sig, amt in ops:
                for (we, wv) in waits:
                    if wv > 0:
                        eng.wait_ge(sems[we], wv)
                instr = fn(eng)
                if sig:
                    if instr is None:
                        instr = eng.nop()
                    instr.then_inc(sems[sig], amt)
        getattr(blk, ename)(body)


def build_program(B=2, T=4096, D=1024, S=64, SIDE=256):
    assert T % (2 * SIDE) == 0 and D % 128 == 0 and SIDE % 128 == 0
    KC = D // 128
    NSUB = SIDE // 128
    NH = max(1, D // 512)
    HCOL = D // NH
    NST = max(1, D // 512)
    STW = min(D, 512)
    NPAIR = T // (2 * SIDE)
    half = T // 2
    BS = B * S
    TGRP = 4
    NTG = (KC + TGRP - 1) // TGRP
    CNS = 2 * NSUB

    nc = bass.Bass("TRN2", target_bir_lowering=False,
                   detect_race_conditions=False, dynamic_dma_scratch_size=4096)

    x_in = nc.dram_tensor("x", [B, T, D], F32, kind="ExternalInput").ap()
    wt_in = nc.dram_tensor("wt", [D, D], F32, kind="ExternalInput").ap()
    wst_in = nc.dram_tensor("wst", [D, D], F32, kind="ExternalInput").ap()
    wkvt_in = nc.dram_tensor("wkvt", [D, 2 * S], F32, kind="ExternalInput").ap()
    vr_in = nc.dram_tensor("vr", [D, R], F32, kind="ExternalInput").ap()
    ctb_in = nc.dram_tensor("ctb", [R + 1, D], F32, kind="ExternalInput").ap()
    wot_in = nc.dram_tensor("wot", [S, D], F32, kind="ExternalInput").ap()
    td_in = nc.dram_tensor("td", [S, 1], F32, kind="ExternalInput").ap()
    tf_in = nc.dram_tensor("tf", [S, 1], F32, kind="ExternalInput").ap()
    sg_in = nc.dram_tensor("sg", [1, D], F32, kind="ExternalInput").ap()
    lnw_in = nc.dram_tensor("lnw", [D], F32, kind="ExternalInput").ap()
    lnb_in = nc.dram_tensor("lnb", [D], F32, kind="ExternalInput").ap()
    id_in = nc.dram_tensor("ident", [128, 128], F32, kind="ExternalInput").ap()
    out_hbm = nc.dram_tensor("out", [B, T, D], F32, kind="ExternalOutput").ap()
    x1_hbm = nc.dram_tensor("x1s", [B, T, D], F32).ap()

    from contextlib import ExitStack
    with ExitStack() as ctx:
        e = ctx.enter_context
        wd0_h = e(nc.sbuf_tensor("wd0_h", [128, KC * D], FP16))
        wd0_l = e(nc.sbuf_tensor("wd0_l", [128, KC * D], FP16))
        wd1_h = e(nc.sbuf_tensor("wd1_h", [128, KC * D], FP16))
        wd1_l = e(nc.sbuf_tensor("wd1_l", [128, KC * D], FP16))
        wkv_h = e(nc.sbuf_tensor("wkv_h", [128, KC * 2 * S], FP16))
        wkv_l = e(nc.sbuf_tensor("wkv_l", [128, KC * 2 * S], FP16))
        vr_h = e(nc.sbuf_tensor("vr_h", [128, KC * R], FP16))
        vr_l = e(nc.sbuf_tensor("vr_l", [128, KC * R], FP16))
        cts_h = e(nc.sbuf_tensor("cts_h", [96, D], FP16))
        cts_l = e(nc.sbuf_tensor("cts_l", [96, D], FP16))
        wo_h = e(nc.sbuf_tensor("wo_h", [128, D], FP16))
        wo_l = e(nc.sbuf_tensor("wo_l", [128, D], FP16))
        oneg = e(nc.sbuf_tensor("oneg", [128, D], F32))
        ident = e(nc.sbuf_tensor("ident_s", [128, 128], F32))
        wdec = e(nc.sbuf_tensor("wdec", [128, 1], F32))
        negef = e(nc.sbuf_tensor("negef", [128, 1], F32))
        negkb = e(nc.sbuf_tensor("negkb", [128, 1], F32))
        vbias = e(nc.sbuf_tensor("vbias", [128, 1], F32))
        epsv = e(nc.sbuf_tensor("epsv", [128, 1], F32))
        wkvbuf = e(nc.sbuf_tensor("wkvbuf", [128, T], F32))
        st6 = e(nc.sbuf_tensor("st6", [128, 2 * 6 * NST], F32))
        st2 = e(nc.sbuf_tensor("st2", [128, 6 * 2 * NSUB], F32))
        rln = e(nc.sbuf_tensor("rln", [128, 6 * NSUB], F32))
        negmr = e(nc.sbuf_tensor("negmr", [128, 6 * NSUB], F32))
        sc512 = e(nc.sbuf_tensor("sc512", [128, 2 * HCOL], F32))
        eb = e(nc.sbuf_tensor("eb", [128, 2 * SIDE], F32))
        sb = e(nc.sbuf_tensor("sb", [128, 2 * SIDE], F32))
        xvlo = e(nc.sbuf_tensor("xvlo", [R, SIDE], FP16))
        pb = [e(nc.psum_tensor(f"pb{i}", [128, 512], F32)) for i in range(8)]

        semsets = [{n: e(nc.semaphore(f"s{bi}_{n}")) for n in SEMS} for bi in range(3)]

        ctxc = ExitStack()
        blk_clr = ctxc.enter_context(nc.Block())

        def _clear(eng):
            for ss_ in semsets:
                for hh_ in ss_.values():
                    eng.sem_clear(hh_)
        blk_clr.sync(_clear)
        ctxc.close()

        LNRING = 6

        def ln_stats(S_, src, u, waits, lnpar):
            pp = lnpar[0] % LNRING
            assert not lnpar["pend"][pp], "LN stats ring slot still pending apply"
            lnpar["pend"][pp] = True
            st2c = 2 * NSUB * pp + 2 * u
            lnpar[0] += 1
            w_u = list(waits) + [lnpar[1 + pp]]
            for hh in range(NST):
                S_.op("vector", lambda g, u=u, hh=hh, pp=pp: g.bn_stats(
                    st6[:, 6 * NST * (pp % 2) + 6 * hh: 6 * NST * (pp % 2) + 6 * hh + 6],
                    src[:, u * D + hh * STW: u * D + hh * STW + STW]),
                    waits=w_u if hh == 0 else (),
                    sig=(hh == NST - 1))
            fence = S_.now("vector")
            S_.op("vector", lambda g, st2c=st2c: g.bn_aggr(
                st2[:, st2c:st2c + 2],
                st6[:, 6 * NST * (pp % 2): 6 * NST * (pp % 2) + 6 * NST]),
                waits=[fence], sig=True)
            return pp, S_.now("vector")

        def ln_rln(S_, u, pp, ve):
            rlc = NSUB * pp + u
            st2c = 2 * NSUB * pp + 2 * u
            var_ap = st2[:, st2c + 1:st2c + 2]
            S_.op("scalar", lambda g, rlc=rlc, var_ap=var_ap: g.activation(
                rln[:, rlc:rlc + 1], var_ap, ACT.Ln, bias=epsv[:, :]),
                waits=[ve], sig=True)
            S_.op("scalar", lambda g, rlc=rlc: g.activation(
                rln[:, rlc:rlc + 1], rln[:, rlc:rlc + 1], ACT.Exp, scale=-0.5),
                sig=True)
            return S_.now("scalar")

        def ln_apply(S_, src, u, pp, ve, slot_of, lnpar, on_vector=False,
                     rln_sig=None):
            st2c = 2 * NSUB * pp + 2 * u
            rlc = NSUB * pp + u
            m_ap = st2[:, st2c:st2c + 1]
            if rln_sig is None:
                rln_sig = ln_rln(S_, u, pp, ve)
            slot, slot_free = slot_of(u)
            if on_vector:
                S_.op("vector", lambda g, u=u, slot=slot, rlc=rlc, m_ap=m_ap: g.tensor_scalar(
                    slot, src[:, u * D:(u + 1) * D], m_ap, rln[:, rlc:rlc + 1],
                    op0=ALU.subtract, op1=ALU.mult),
                    waits=[rln_sig, slot_free], sig=True)
                sig = S_.now("vector")
            else:
                S_.op("scalar", lambda g, rlc=rlc, m_ap=m_ap: g.activation(
                    negmr[:, rlc:rlc + 1], m_ap, ACT.Identity, scale=-1.0), sig=True)
                S_.op("scalar", lambda g, rlc=rlc: g.activation(
                    negmr[:, rlc:rlc + 1], negmr[:, rlc:rlc + 1], ACT.Identity,
                    scale=rln[:, rlc:rlc + 1]), sig=True)
                S_.op("scalar", lambda g, u=u, slot=slot, rlc=rlc: g.activation(
                    slot, src[:, u * D:(u + 1) * D], ACT.Identity,
                    scale=rln[:, rlc:rlc + 1], bias=negmr[:, rlc:rlc + 1]),
                    waits=[slot_free], sig=True)
                sig = S_.now("scalar")
            lnpar[1 + pp] = sig
            lnpar["pend"][pp] = False
            return slot, sig

        def ln_sub(S_, src, u, waits, slot_of, lnpar):
            pp, ve = ln_stats(S_, src, u, waits, lnpar)
            return ln_apply(S_, src, u, pp, ve, slot_of, lnpar)

        def ln_chain(S_, src, waits, slot_of, lnpar, waits_u=None):
            sigs = []
            slots = []
            for u in range(NSUB):
                w_u = list(waits) if u == 0 else []
                if waits_u is not None:
                    w_u = w_u + [waits_u[u]]
                slot, sig = ln_sub(S_, src, u, w_u, slot_of, lnpar)
                slots.append(slot)
                sigs.append(sig)
            return slots, sigs

        def transpose_stage(S_, slots, apply_sigs, dsth, dstl, tpst):
            cg = [None] * NTG
            for u in range(NSUB):
                for h4 in range(NTG):
                    nck = min(TGRP, KC - h4 * TGRP)
                    bk = 2 + (tpst[0] % 4)
                    tpst[0] += 1
                    waits = [S_.bank_wait(bk), apply_sigs[u]]
                    for cc in range(nck):
                        c = h4 * TGRP + cc
                        S_.op("tensor", lambda g, u=u, c=c, cc=cc, bk=bk: g.matmul(
                            pb[bk][:, cc * 128:(cc + 1) * 128],
                            slots[u][:, c * 128:(c + 1) * 128],
                            ident[:, :], is_transpose=True,
                            start=(cc == 0), stop=(cc == nck - 1)),
                            waits=waits if cc == 0 else (), sig=(cc == nck - 1))
                    te = S_.now("tensor")
                    dh = bass.AP(dsth[:, :].tensor, h4 * TGRP * SIDE + u * 128,
                                 [[KC * SIDE, 128], [SIDE, nck], [1, 128]])
                    dl = bass.AP(dstl[:, :].tensor, h4 * TGRP * SIDE + u * 128,
                                 [[KC * SIDE, 128], [SIDE, nck], [1, 128]])
                    S_.op("scalar", lambda g, bk=bk, dh=dh, nck=nck: g.activation(
                        dh, pb[bk][:, 0:nck * 128], ACT.Identity),
                        waits=[te], sig=True)
                    sc = S_.now("scalar")
                    S_.op("vector", lambda g, bk=bk, dh=dh, dl=dl, nck=nck: g.tensor_tensor(
                        dl, pb[bk][:, 0:nck * 128], dh, op=ALU.subtract),
                        waits=[te, sc], sig=True)
                    S_.bank_set(bk, S_.now("vector"))
                    cg[h4] = S_.now("vector")
            return cg

        xv_bank = [0]
        xv_mod = [2]

        def xvr_stage(S_, xFh, xFl, xvs, ready, lo_free, cg):
            bq = 6 + (xv_bank[0] % xv_mod[0])
            xv_bank[0] += 1
            for p, (lh, rh) in enumerate(((vr_h, xFh), (vr_h, xFl), (vr_l, xFh))):
                for c in range(KC):
                    waits = []
                    if p == 0 and c == 0:
                        waits += [S_.bank_wait(bq)] + list(ready)
                    if p == 0 and c % TGRP == 0:
                        waits.append(cg[c // TGRP])
                    S_.op("tensor", lambda g, p=p, c=c, lh=lh, rh=rh, bq=bq: g.matmul(
                        pb[bq][0:R, 0:SIDE], lh[:, c * R:(c + 1) * R],
                        rh[:, c * SIDE:(c + 1) * SIDE],
                        start=(p == 0 and c == 0), stop=(p == 2 and c == KC - 1)),
                        waits=waits, sig=(p == 2 and c == KC - 1))
            te = S_.now("tensor")
            S_.op("scalar", lambda g, xvs=xvs, bq=bq: g.activation(
                xvs[0:R, :], pb[bq][0:R, 0:SIDE], ACT.Identity), waits=[te], sig=True)
            sc = S_.now("scalar")
            S_.op("vector", lambda g, xvs=xvs, bq=bq: g.tensor_tensor(
                xvlo[0:R, 0:SIDE], pb[bq][0:R, 0:SIDE], xvs[0:R, :], op=ALU.subtract),
                waits=[te, sc, lo_free], sig=True)
            ve = S_.now("vector")
            S_.bank_set(bq, ve)
            S_.op("sync", lambda g, xvs=xvs: g.dma_start(
                out=xvs[R + 1:2 * R + 1, :], in_=xvlo[0:R, 0:SIDE]),
                waits=[ve], sig=True, dma=True)
            return S_.now("sync")

        def dense_stage(S_, xFh, xFl, wh, wl, ready, evac, dnst, xvs=None,
                        cg=None, xv_sig=None, post_u=None):
            evs = []
            for u in range(NSUB):
                for hh in range(NH):
                    bk = dnst[1][dnst[0] % len(dnst[1])]
                    dnst[0] += 1
                    first = (u == 0 and hh == 0)
                    for p, (lh, rh) in enumerate(((xFh, wh), (xFh, wl), (xFl, wh))):
                        for c in range(KC):
                            waits = []
                            if p == 0 and c == 0:
                                waits.append(S_.bank_wait(bk))
                                if first:
                                    waits += list(ready)
                            if first and p == 0 and cg is not None and c % TGRP == 0:
                                waits.append(cg[c // TGRP])
                            S_.op("tensor", lambda g, u=u, hh=hh, bk=bk, c=c, lh=lh, rh=rh, p=p: g.matmul(
                                pb[bk][:, 0:HCOL],
                                lh[:, c * SIDE + u * 128: c * SIDE + u * 128 + 128],
                                rh[:, c * D + hh * HCOL: c * D + hh * HCOL + HCOL],
                                start=(p == 0 and c == 0),
                                stop=(xvs is None and p == 2 and c == KC - 1)),
                                waits=waits, sig=(xvs is None and p == 2 and c == KC - 1))
                    if xvs is not None:
                        for p, rh in enumerate((cts_h, cts_l)):
                            S_.op("tensor", lambda g, u=u, hh=hh, bk=bk, rh=rh, p=p, xvs=xvs: g.matmul(
                                pb[bk][:, 0:HCOL],
                                xvs[0:96, u * 128:(u + 1) * 128],
                                rh[0:96, hh * HCOL:(hh + 1) * HCOL],
                                start=False, stop=(p == 1)),
                                waits=[xv_sig] if (first and p == 0) else (),
                                sig=(p == 1))
                    te = S_.now("tensor")
                    S_.op("vector", lambda g, u=u, hh=hh, bk=bk: evac(
                        g, u, hh, pb[bk][:, 0:HCOL]), waits=[te], sig=True)
                    evs.append(S_.now("vector"))
                    S_.bank_set(bk, evs[-1])
                    if post_u is not None and hh == NH - 1:
                        post_u(u, evs[-1])
            return evs

        ctx0 = ExitStack()
        e0 = ctx0.enter_context
        stg = e0(nc.sbuf_tensor("stg", [128, 2048], F32))
        stg2 = e0(nc.sbuf_tensor("stg2", [128, 2048], F32))
        grow = e0(nc.sbuf_tensor("grow", [1, 3 * D], F32))
        lnwf = e0(nc.sbuf_tensor("lnwf", [128, KC], F32))
        lnbf = e0(nc.sbuf_tensor("lnbf", [128, KC], F32))
        onecol = e0(nc.sbuf_tensor("onecol", [1, 128], F32))
        kvb = e0(nc.sbuf_tensor("kvb", [128, 1], F32))
        blk = e0(nc.Block())
        if True:
            S0 = Sched()
            S0.op("vector", lambda g: g.memset(epsv[:, :], EPS), sig=True)

            def ld(dst, src, waits=(), slow=False):
                return S0.op("sync", lambda g, d=dst, s=src, slow=slow: g.dma_start(
                    out=d, in_=s, allow_slow_non_contiguous=slow),
                    waits=waits, sig=True, dma=True)

            ld(ident[:, :], id_in[:, :])
            w_td0 = ld(wdec[0:S, :], td_in[:, :])
            w_td1 = ld(wdec[S:2 * S, :], td_in[:, :])
            w_tf0 = ld(negef[0:S, :], tf_in[:, :])
            w_tf1 = ld(negef[S:2 * S, :], tf_in[:, :])
            w_sg = ld(grow[0:1, 0:D], sg_in[:, :])
            lnw_src = bass.AP(lnw_in.tensor, 0, [[1, 128], [128, KC]])
            lnb_src = bass.AP(lnb_in.tensor, 0, [[1, 128], [128, KC]])
            w_lnw = ld(lnwf[:, :], lnw_src, slow=True)
            w_lnb = ld(lnbf[:, :], lnb_src, slow=True)
            w_kvt = ld(stg2[:, 0:KC * 2 * S].rearrange("p (c n) -> p c n", c=KC),
                       bass.AP(wkvt_in.tensor, 0, [[2 * S, 128], [128 * 2 * S, KC], [1, 2 * S]]))
            w_vr = ld(stg[:, 0:KC * R].rearrange("p (c n) -> p c n", c=KC),
                      bass.AP(vr_in.tensor, 0, [[R, 128], [128 * R, KC], [1, R]]))
            w_ctb = ld(stg[0:R + 1, 1024:1024 + D], ctb_in[:, :])
            w_wot = ld(stg2[0:S, 1024:1024 + D], wot_in[:, :])
            w_wot2 = ld(stg2[S:2 * S, 1024:1024 + D], wot_in[:, :])

            S0.op("scalar", lambda g: g.activation(wdec[:, :], wdec[:, :], ACT.Exp),
                  waits=[w_td0, w_td1], sig=True)
            S0.op("scalar", lambda g: g.activation(negef[:, :], negef[:, :], ACT.Exp),
                  waits=[w_tf0, w_tf1], sig=True)
            sc_e = S0.now("scalar")
            ve_ef = S0.op("vector", lambda g: g.tensor_scalar_mul(negef[:, :], negef[:, :], -1.0),
                          waits=[sc_e], sig=True)

            S0.op("scalar", lambda g: g.activation(grow[0:1, D:2 * D], grow[0:1, 0:D], ACT.Exp),
                  waits=[w_sg], sig=True)
            S0.op("scalar", lambda g: g.activation(grow[0:1, 2 * D:3 * D], grow[0:1, 0:D],
                                                   ACT.Exp, scale=-1.0), sig=True)
            sc_g = S0.now("scalar")
            S0.op("vector", lambda g: g.tensor_scalar_add(grow[0:1, D:3 * D],
                                                          grow[0:1, D:3 * D], 1.0),
                  waits=[sc_g], sig=True)
            fence_g = S0.now("vector")
            ve_g = S0.op("vector", lambda g: g.reciprocal(grow[0:1, D:3 * D],
                                                         grow[0:1, D:3 * D]),
                         waits=[fence_g], sig=True)
            ve_one = S0.op("vector", lambda g: g.memset(onecol[:, :], 1.0), sig=True)

            for row, dst in ((1, oneg),):
                for hc in range(max(1, D // 512)):
                    w = min(512, D)
                    ssl = slice(row * D + hc * 512, row * D + hc * 512 + w)
                    bk = hc % 2 + (0 if row == 1 else 2)
                    S0.op("tensor", lambda g, ssl=ssl, bk=bk, w=w: g.matmul(
                        pb[bk][:, 0:w], onecol[:, :], grow[0:1, ssl],
                        start=True, stop=True), waits=[ve_g, ve_one], sig=True)
                    te = S0.now("tensor")
                    S0.op("vector", lambda g, hc=hc, w=w, bk=bk, dst=dst: g.tensor_copy(
                        dst[:, hc * 512:hc * 512 + w], pb[bk][:, 0:w]),
                        waits=[te], sig=True)
            ve_gbc = S0.now("vector")

            for c in range(KC):
                S0.op("tensor", lambda g, c=c: g.matmul(
                    pb[4][0:2 * S, 0:1], stg2[:, c * 2 * S:(c + 1) * 2 * S],
                    lnbf[:, c:c + 1], start=(c == 0), stop=(c == KC - 1)),
                    waits=[w_kvt, w_lnb] if c == 0 else (), sig=(c == KC - 1))
            te_kvb = S0.now("tensor")
            ve_kvb = S0.op("vector", lambda g: g.tensor_copy(kvb[0:2 * S, :], pb[4][0:2 * S, 0:1]),
                           waits=[te_kvb], sig=True)
            ve_nkb = S0.op("vector", lambda g: g.tensor_scalar_mul(negkb[0:S, :], kvb[0:S, :], -1.0),
                           waits=[ve_kvb], sig=True)
            g_b1 = S0.op("gpsimd", lambda g: g.dma_start(out=negkb[S:2 * S, :], in_=negkb[0:S, :]),
                         waits=[ve_nkb], sig=True, dma=True)
            g_b2 = S0.op("gpsimd", lambda g: g.dma_start(out=vbias[0:S, :], in_=kvb[S:2 * S, :]),
                         waits=[ve_kvb], sig=True, dma=True)
            g_b3 = S0.op("gpsimd", lambda g: g.dma_start(out=vbias[S:2 * S, :], in_=kvb[S:2 * S, :]),
                         waits=[ve_kvb], sig=True, dma=True)

            for c in range(KC):
                S0.op("vector", lambda g, c=c: g.tensor_scalar_mul(
                    stg2[:, c * 2 * S:(c + 1) * 2 * S], stg2[:, c * 2 * S:(c + 1) * 2 * S],
                    lnwf[:, c:c + 1]),
                    waits=[w_lnw, te_kvb] if c == 0 else (), sig=(c == KC - 1))
            f1_ = S0.now("vector")
            S0.op("vector", lambda g: g.tensor_copy(wkv_h[:, :], stg2[:, 0:KC * 2 * S]),
                  waits=[f1_], sig=True)
            f2_ = S0.now("vector")
            S0.op("vector", lambda g: g.tensor_tensor(wkv_l[:, :], stg2[:, 0:KC * 2 * S],
                                                      wkv_h[:, :], op=ALU.subtract),
                  waits=[f2_], sig=True)
            S0.op("vector", lambda g: g.tensor_copy(vr_h[:, :], stg[:, 0:KC * R]),
                  waits=[w_vr], sig=True)
            f3_ = S0.now("vector")
            S0.op("vector", lambda g: g.tensor_tensor(vr_l[:, :], stg[:, 0:KC * R],
                                                      vr_h[:, :], op=ALU.subtract),
                  waits=[f3_], sig=True)
            S0.op("vector", lambda g: g.memset(cts_h[:, :], 0.0), sig=True)
            S0.op("vector", lambda g: g.memset(cts_l[:, :], 0.0), sig=True)
            zc = S0.now("vector")
            S0.op("vector", lambda g: g.tensor_copy(cts_h[0:R + 1, :], stg[0:R + 1, 1024:1024 + D]),
                  waits=[w_ctb, zc], sig=True)
            f4_ = S0.now("vector")
            S0.op("vector", lambda g: g.tensor_tensor(cts_l[0:R + 1, :], stg[0:R + 1, 1024:1024 + D],
                                                      cts_h[0:R + 1, :], op=ALU.subtract),
                  waits=[f4_], sig=True)
            g_ct = S0.op("gpsimd", lambda g: g.dma_start(
                out=cts_h[R + 1:2 * R + 2, :], in_=cts_h[0:R + 1, :]),
                waits=[f4_], sig=True, dma=True)
            S0.op("vector", lambda g: g.memset(wo_l[:, :], 0.0), sig=True)
            zw = S0.now("vector")
            S0.op("vector", lambda g: g.tensor_copy(wo_h[:, :], stg2[:, 1024:1024 + D]),
                  waits=[w_wot, w_wot2], sig=True)
            f5_ = S0.now("vector")
            S0.op("vector", lambda g: g.tensor_tensor(wo_l[0:S, :], stg2[0:S, 1024:1024 + D],
                                                      wo_h[0:S, :], op=ALU.subtract),
                  waits=[f5_, zw], sig=True)
            ve_small = S0.now("vector")

            S0.op("gpsimd", lambda g: g.nop(), waits=[g_b1, g_b2, g_b3, g_ct])
            S0.op("sync", lambda g: g.nop(), waits=[ve_ef, S0.now("vector")])
            emit(blk, S0, semsets[0])
        ctx0.close()

        ctx1 = ExitStack()
        e1 = ctx1.enter_context
        PAr = [e1(nc.sbuf_tensor(f"pa{i}", [128, D * NSUB], F32)) for i in range(3)]
        PBr = [e1(nc.sbuf_tensor(f"pbx{i}", [128, D * NSUB], F32)) for i in range(3)]
        yF = [(e1(nc.sbuf_tensor(f"yfh{i}", [128, KC * SIDE], FP16)),
               e1(nc.sbuf_tensor(f"yfl{i}", [128, KC * SIDE], FP16))) for i in range(2)]
        F1 = [(e1(nc.sbuf_tensor(f"f1h{i}", [128, KC * SIDE], FP16)),
               e1(nc.sbuf_tensor(f"f1l{i}", [128, KC * SIDE], FP16))) for i in range(2)]
        Cn = e1(nc.sbuf_tensor("cn", [128, CNS * D], F32))
        xvs2 = [e1(nc.sbuf_tensor(f"xvs{i}", [128, SIDE], FP16)) for i in range(2)]
        blk = e1(nc.Block())
        if True:
            S1 = Sched()
            tpst = [0]
            dnst = [0, (0, 1)]
            cn_ring = [0]
            cn_sig = [None] * CNS
            lnpar = LnPar()

            for xs_ in xvs2:
                S1.op("vector", lambda g, xs_=xs_: g.memset(xs_[:, :], 0.0), sig=True)
                zv = S1.now("vector")
                S1.op("vector", lambda g, xs_=xs_: g.memset(xs_[R:R + 1, :], 1.0),
                      waits=[zv], sig=True)
            ve_ones = S1.now("vector")

            ring_trace = []

            def ring_slot(u):
                i = cn_ring[0] % CNS
                cn_ring[0] += 1
                ring_trace.append(i)
                return Cn[:, i * D:(i + 1) * D], cn_sig[i]

            def run_ln(src, waits, dst=None, waits_u=None):
                if dst is None:
                    i0 = cn_ring[0]
                    slots, sigs = ln_chain(S1, src, waits, ring_slot, lnpar,
                                           waits_u=waits_u)
                    return slots, sigs, i0
                slots, sigs = ln_chain(
                    S1, src, waits,
                    lambda u: (dst[:, u * D:(u + 1) * D], None), lnpar,
                    waits_u=waits_u)
                return slots, sigs, None

            def run_T(slots, sigs, i0, dsth, dstl, ring_idx=None):
                cg = transpose_stage(S1, slots, sigs, dsth, dstl, tpst)
                te_sig = S1.now("tensor")
                if i0 is not None:
                    for u in range(NSUB):
                        cn_sig[(i0 + u) % CNS] = te_sig
                if ring_idx is not None:
                    for i in ring_idx:
                        cn_sig[i] = te_sig
                return cg

            def load_x(dst, b, t0, waits):
                ap_s = bass.AP(x_in.tensor, b * T * D + t0 * D,
                               [[D, 128], [128 * D, NSUB], [1, D]])
                S1.op("sync", lambda g, dst=dst, ap_s=ap_s: g.dma_start(
                    out=dst[:, :].rearrange("p (u d) -> p u d", u=NSUB), in_=ap_s),
                    waits=waits, sig=True, dma=True)
                return S1.now("sync")

            def kv_stage(xFh, xFl, b, t0, cg):
                lo, hi = b * S, (b + 1) * S
                kvp = kvpar[0] % 2
                c0 = kvp * SIDE
                kvpar[0] += 1
                tp = (0, b * S) if b > 0 else None
                te_sigs = []
                for bank, col0 in ((0, 0), (1, S)):
                    for p, (lh, rh) in enumerate(((wkv_h, xFh), (wkv_h, xFl), (wkv_l, xFh))):
                        for c in range(KC):
                            waits = []
                            if p == 0 and c == 0:
                                waits.append(S1.bank_wait(bank))
                            if bank == 6 and p == 0 and c % TGRP == 0:
                                waits.append(cg[c // TGRP])
                            S1.op("tensor", lambda g, bank=bank, col0=col0, p=p, c=c, lh=lh, rh=rh, lo=lo, hi=hi, tp=tp: g.matmul(
                                pb[bank][lo:hi, 0:SIDE],
                                lh[:, c * 2 * S + col0: c * 2 * S + col0 + S],
                                rh[:, c * SIDE:(c + 1) * SIDE],
                                start=(p == 0 and c == 0), stop=(p == 2 and c == KC - 1),
                                tile_position=tp),
                                waits=waits,
                                sig=(p == 2 and c == KC - 1))
                    te_sigs.append(S1.now("tensor"))
                te_k, te_v = te_sigs
                S1.op("scalar", lambda g, lo=lo, hi=hi, c0=c0: g.activation(
                    eb[lo:hi, c0:c0 + SIDE], pb[0][lo:hi, 0:SIDE], ACT.Exp,
                    bias=negkb[lo:hi, :], scale=-1.0),
                    waits=[te_k, kv_free[kvp]], sig=True)
                sc1 = S1.now("scalar")
                S1.bank_set(0, sc1)
                S1.op("gpsimd", lambda g, lo=lo, hi=hi, c0=c0: g.tensor_scalar_add(
                    eb[lo:hi, c0:c0 + SIDE], eb[lo:hi, c0:c0 + SIDE], 1.0),
                    waits=[sc1], sig=True, sem="gpsimdc")
                fe = S1.now("gpsimdc")
                S1.op("vector", lambda g, lo=lo, hi=hi, c0=c0: g.reciprocal(
                    sb[lo:hi, c0:c0 + SIDE], eb[lo:hi, c0:c0 + SIDE]),
                    waits=[fe], sig=True)
                ve1 = S1.now("vector")
                S1.op("scalar", lambda g, lo=lo, hi=hi, c0=c0: g.activation(
                    eb[lo:hi, c0:c0 + SIDE], sb[lo:hi, c0:c0 + SIDE], ACT.Exp,
                    scale=negef[lo:hi, :]),
                    waits=[ve1], sig=True)
                sc2 = S1.now("scalar")
                S1.op("vector", lambda g, lo=lo, hi=hi, t0=t0, c0=c0: g.scalar_tensor_tensor(
                    wkvbuf[lo:hi, t0:t0 + SIDE], pb[1][lo:hi, 0:SIDE], vbias[lo:hi, :],
                    eb[lo:hi, c0:c0 + SIDE], op0=ALU.add, op1=ALU.mult),
                    waits=[te_v, sc2], sig=True)
                ve2 = S1.now("vector")
                kv_free[kvp] = ve2
                S1.bank_set(1, ve2)
                return ve2

            def wsplit(chunks, wsrc, dh, dl, scale_by_g):
                last = None
                for c in chunks:
                    del ring_trace[:]
                    slot, sfree = ring_slot(0)
                    i_ = ring_trace[0]
                    S1.op("sync", lambda g, slot=slot, wsrc=wsrc, c=c: g.dma_start(
                        out=slot, in_=bass.AP(wsrc.tensor, c * 128 * D,
                                              [[D, 128], [1, D]])),
                        waits=[sfree], sig=True, dma=True)
                    wld = S1.now("sync")
                    waits = [wld]
                    if scale_by_g:
                        del ring_trace[:]
                        slot2, sfree2 = ring_slot(0)
                        i2_ = ring_trace[0]
                        S1.op("vector", lambda g, slot=slot, slot2=slot2: g.tensor_tensor(
                            slot2, slot, oneg[:, 0:D], op=ALU.mult),
                            waits=[wld, sfree2], sig=True)
                        fw = S1.now("vector")
                        S1.op("vector", lambda g, slot=slot, slot2=slot2: g.tensor_tensor(
                            slot, slot, slot2, op=ALU.subtract),
                            waits=[fw], sig=True)
                        cn_sig[i2_] = S1.now("vector")
                        waits = [S1.now("vector")]
                    S1.op("vector", lambda g, slot=slot, dh=dh, c=c: g.tensor_copy(
                        dh[:, c * D:(c + 1) * D], slot), waits=waits, sig=True)
                    fw2 = S1.now("vector")
                    S1.op("vector", lambda g, slot=slot, dh=dh, dl=dl, c=c: g.tensor_tensor(
                        dl[:, c * D:(c + 1) * D], slot, dh[:, c * D:(c + 1) * D],
                        op=ALU.subtract), waits=[fw2], sig=True)
                    last = S1.now("vector")
                    cn_sig[i_] = last
                return last

            pa_free = [None, None, None]
            pb_free = [[], [], []]
            xv_lo_free = None
            carry = None
            wd0_ready = None
            wd1_ready = None
            kvpar = [0]
            kv_free = [None, None]
            carry_n = {}

            pairs = [(b, pr) for b in range(B) for pr in range(NPAIR)]
            nside = 0
            for pi, (b, pr) in enumerate(pairs):
                if True:
                    t0A = pr * SIDE
                    t0B = t0A + half
                    sA = nside
                    nside += 2
                    iA, iB = sA % 3, (sA + 1) % 3
                    jA, jB = 0, 1
                    if carry is None:
                        lxA = load_x(PAr[iA], b, t0A, [pa_free[iA]])
                        lxB = load_x(PAr[iB], b, t0B, [pa_free[iB]])
                        slA, sgA, i0A = run_ln(PAr[iA], [lxA])
                        slB, sgB, i0B = run_ln(PAr[iB], [lxB])
                        cgA = run_T(slA, sgA, i0A, F1[jA][0], F1[jA][1])
                        cgB = run_T(slB, sgB, i0B, F1[jB][0], F1[jB][1])
                    else:
                        cgA, cgB = carry
                    xvA = xvr_stage(S1, F1[jA][0], F1[jA][1], xvs2[jA],
                                    [ve_ones], xv_lo_free, cgA)
                    if pi == 0:
                        wd0_ready = wsplit(range(KC), wt_in, wd0_h, wd0_l, False)
                    xvB = xvr_stage(S1, F1[jB][0], F1[jB][1], xvs2[jB],
                                    [ve_ones], xvA, cgB)
                    xv_lo_free = xvB
                    ev1 = {}
                    ysl = {}
                    for si, ii, jj, xv, cg, t0 in (("A", iA, jA, xvA, cgA, t0A),
                                                   ("B", iB, jB, xvB, cgB, t0B)):
                        def evac_x1(g, u, hh, ps, ii=ii):
                            return g.scalar_tensor_tensor(
                                PBr[ii][:, u * D + hh * HCOL: u * D + hh * HCOL + HCOL],
                                ps, 0.5,
                                PAr[ii][:, u * D + hh * HCOL: u * D + hh * HCOL + HCOL],
                                op0=ALU.mult, op1=ALU.add)
                        yslots = []
                        ysigs = []
                        deferred = []

                        def post_u(u, ev, si=si, ii=ii, t0=t0, yslots=yslots,
                                   ysigs=ysigs, deferred=deferred):
                            ap_d = bass.AP(x1_hbm.tensor,
                                           b * T * D + (t0 + u * 128) * D,
                                           [[D, 128], [1, D]])
                            S1.op("scalar", lambda g, ii=ii, ap_d=ap_d, u=u: g.dma_start(
                                out=ap_d, in_=PBr[ii][:, u * D:(u + 1) * D]),
                                waits=[ev], sig=True, dma=True)
                            if si == "B" and u == NSUB - 1:
                                deferred.append((u, ev))
                                return
                            sl, sg = ln_sub(S1, PBr[ii], u, [ev],
                                            lambda uu: (PAr[ii][:, uu * D:(uu + 1) * D],
                                                        None), lnpar)
                            yslots.append(sl)
                            ysigs.append(sg)
                        evs = dense_stage(S1, F1[jj][0], F1[jj][1], wd0_h, wd0_l,
                                          pb_free[ii] + ([wd0_ready] if pi == 0 else []),
                                          evac_x1, dnst,
                                          xvs=xvs2[jj], cg=cg, xv_sig=xv,
                                          post_u=post_u)
                        ev1[si] = evs[-1]
                        ev1[si + "sp"] = S1.now("scalar")
                        ysl[si] = (yslots, ysigs, deferred, ii)
                        if pi == 0:
                            half_c = range(0, KC // 2) if si == "A" else range(KC // 2, KC)
                            wd1_ready = wsplit(half_c, wst_in, wd1_h, wd1_l, True)
                    ycgA = run_T(ysl["A"][0], ysl["A"][1], None, yF[0][0], yF[0][1])
                    teA2 = S1.now("tensor")
                    onegb = bass.AP(oneg[:, :].tensor, 0,
                                    [[D, 128], [0, NSUB], [1, D]])
                    S1.op("vector", lambda g, iA=iA, onegb=onegb: g.tensor_tensor(
                        PAr[iA][:, :].rearrange("p (u d) -> p u d", u=NSUB),
                        PAr[iA][:, :].rearrange("p (u d) -> p u d", u=NSUB),
                        onegb, op=ALU.mult),
                        waits=[teA2, ysl["A"][1][-1]], sig=True)
                    ugA = S1.now("vector")
                    for (du, dev) in ysl["B"][2]:
                        iiB = ysl["B"][3]
                        sl, sg = ln_sub(S1, PBr[iiB], du, [dev],
                                        lambda uu: (PAr[iiB][:, uu * D:(uu + 1) * D],
                                                    None), lnpar)
                        ysl["B"][0].append(sl)
                        ysl["B"][1].append(sg)
                    ycgB = run_T(ysl["B"][0], ysl["B"][1], None, yF[1][0], yF[1][1])
                    teB2 = S1.now("tensor")
                    S1.op("vector", lambda g, iB=iB, onegb=onegb: g.tensor_tensor(
                        PAr[iB][:, :].rearrange("p (u d) -> p u d", u=NSUB),
                        PAr[iB][:, :].rearrange("p (u d) -> p u d", u=NSUB),
                        onegb, op=ALU.mult),
                        waits=[teB2, ysl["B"][1][-1]], sig=True)
                    ugB = S1.now("vector")
                    usig = {"A": ugA, "B": ugB}
                    sl3 = {}
                    for si, ii, jj, oj, t0 in (("A", iA, jA, 1, t0A),
                                               ("B", iB, jB, 0, t0B)):
                        ycg = ycgB if si == "A" else ycgA
                        last_ev = None
                        uev = []
                        for u in range(NSUB):
                            for hh in range(NH):
                                bk = dnst[1][dnst[0] % len(dnst[1])]
                                dnst[0] += 1
                                first = (u == 0 and hh == 0)
                                for p, (lh, rh) in enumerate(((yF[oj][0], wd1_h),
                                                              (yF[oj][0], wd1_l),
                                                              (yF[oj][1], wd1_h))):
                                    for c in range(KC):
                                        waits = []
                                        if p == 0 and c == 0:
                                            waits.append(S1.bank_wait(bk))
                                            if first:
                                                waits.append(ev1[si + "sp"])
                                                if pi == 0:
                                                    waits.append(wd1_ready)
                                        if first and p == 0 and c % TGRP == 0:
                                            waits.append(ycg[c // TGRP])
                                        S1.op("tensor", lambda g, u=u, hh=hh, bk=bk, c=c, lh=lh, rh=rh, p=p: g.matmul(
                                            pb[bk][:, 0:HCOL],
                                            lh[:, c * SIDE + u * 128: c * SIDE + u * 128 + 128],
                                            rh[:, c * D + hh * HCOL: c * D + hh * HCOL + HCOL],
                                            start=(p == 0 and c == 0),
                                            stop=(p == 2 and c == KC - 1)),
                                            waits=waits,
                                            sig=(p == 2 and c == KC - 1))
                                te = S1.now("tensor")
                                S1.op("vector", lambda g, u=u, hh=hh, bk=bk, ii=ii: g.tensor_tensor(
                                    PBr[ii][:, u * D + hh * HCOL: u * D + hh * HCOL + HCOL],
                                    pb[bk][:, 0:HCOL],
                                    PAr[ii][:, u * D + hh * HCOL: u * D + hh * HCOL + HCOL],
                                    op=ALU.add),
                                    waits=[te, usig[si]] if (u == 0 and hh == 0) else [te],
                                    sig=True)
                                last_ev = S1.now("vector")
                                S1.bank_set(bk, last_ev)
                                if hh == NH - 1:
                                    if si == "B" and u == NSUB - 1:
                                        uev.append((u, last_ev, True))
                                    else:
                                        del ring_trace[:]
                                        sl_, sg_ = ln_sub(S1, PBr[ii], u, [last_ev],
                                                          ring_slot, lnpar)
                                        uev.append((sl_, sg_, False, ring_trace[0]))
                        ev1[si + "xs"] = last_ev
                        sl3[si] = (uev, ii)
                        if pi + 1 < len(pairs):
                            nb_, npr_ = pairs[pi + 1]
                            if si == "A":
                                ni = nside % 3
                                nt0 = npr_ * SIDE
                            else:
                                ni = (nside + 1) % 3
                                nt0 = npr_ * SIDE + half
                                pa_free[iA] = ev1["Axs"]
                            nlx = load_x(PAr[ni], nb_, nt0, [pa_free[ni]])
                            stats_n = []
                            for u in range(NSUB):
                                pp, ve = ln_stats(S1, PAr[ni], u,
                                                  [nlx] if u == 0 else [], lnpar)
                                stats_n.append((pp, ve, ln_rln(S1, u, pp, ve)))
                            carry_n[si] = (ni, stats_n)
                    slotsA = [x[0] for x in sl3["A"][0]]
                    sigsA = [x[1] for x in sl3["A"][0]]
                    idxA = [x[3] for x in sl3["A"][0]]
                    cg3A = run_T(slotsA, sigsA, None, F1[jA][0], F1[jA][1],
                                 ring_idx=idxA)
                    slotsB = []
                    sigsB = []
                    idxB = []
                    for x in sl3["B"][0]:
                        if x[2]:
                            del ring_trace[:]
                            sl_, sg_ = ln_sub(S1, PBr[sl3["B"][1]], x[0], [x[1]],
                                              ring_slot, lnpar)
                            slotsB.append(sl_)
                            sigsB.append(sg_)
                            idxB.append(ring_trace[0])
                        else:
                            slotsB.append(x[0])
                            sigsB.append(x[1])
                            idxB.append(x[3])
                    cg3B = run_T(slotsB, sigsB, None, F1[jB][0], F1[jB][1],
                                 ring_idx=idxB)
                    kv_stage(F1[jA][0], F1[jA][1], b, t0A, cg3A)
                    pb_free[iA] = [S1.now("vector"), ev1["Asp"]]
                    pa_free[iA] = ev1["Axs"]
                    pa_free[iB] = ev1["Bxs"]
                    carry = None
                    if pi + 1 < len(pairs):
                        cc = []
                        for si2 in ("A", "B"):
                            ni, stats_n = carry_n[si2]
                            slots = []
                            sigs = []
                            idxs = []
                            for u in range(NSUB):
                                pp, ve, rs = stats_n[u]
                                del ring_trace[:]
                                sl_, sg_ = ln_apply(S1, PAr[ni], u, pp, ve,
                                                    ring_slot, lnpar,
                                                    on_vector=False, rln_sig=rs)
                                slots.append(sl_)
                                sigs.append(sg_)
                                idxs.append(ring_trace[0])
                            cc.append((slots, sigs, idxs))
                        ncgA = run_T(cc[0][0], cc[0][1], None,
                                     F1[jA][0], F1[jA][1], ring_idx=cc[0][2])
                        carry = (cc, ncgA)
                    kv_stage(F1[jB][0], F1[jB][1], b, t0B, cg3B)
                    pb_free[iB] = [S1.now("vector"), ev1["Bsp"]]
                    if carry is not None:
                        cc, ncgA = carry
                        ncgB = run_T(cc[1][0], cc[1][1], None,
                                     F1[jB][0], F1[jB][1], ring_idx=cc[1][2])
                        carry = (ncgA, ncgB)

            S1.op("scalar", lambda g: g.nop(), waits=[S1.now("scalar")])
            emit(blk, S1, semsets[1])
        ctx1.close()

        ctx2 = ExitStack()
        e2 = ctx2.enter_context
        q2 = [e2(nc.sbuf_tensor(f"q{i}", [128, D * NSUB], F32)) for i in range(3)]
        x2b2 = [e2(nc.sbuf_tensor(f"x2b{i}", [128, D * NSUB], F32)) for i in range(2)]
        Cn2 = e2(nc.sbuf_tensor("cn2", [128, CNS * D], F32))
        G1 = [(e2(nc.sbuf_tensor(f"g1h{i}", [128, KC * SIDE], FP16)),
               e2(nc.sbuf_tensor(f"g1l{i}", [128, KC * SIDE], FP16))) for i in range(2)]
        xv22 = [e2(nc.sbuf_tensor(f"xv2{i}", [128, SIDE], FP16)) for i in range(2)]
        stHL = e2(nc.sbuf_tensor("sthl", [128, B * T], FP16))
        states = e2(nc.sbuf_tensor("states", [128, T], F32))
        blk = e2(nc.Block())
        if True:
            S3 = Sched()
            xv_mod[0] = 1
            tpst3 = [0]
            dnst3 = [0, (0, 1, 7)]
            cn_ring3 = [0]
            cn_sig3 = [None] * CNS
            lnpar3 = LnPar()

            for xs_ in xv22:
                S3.op("vector", lambda g, xs_=xs_: g.memset(xs_[:, :], 0.0), sig=True)
                zv = S3.now("vector")
                S3.op("vector", lambda g, xs_=xs_: g.memset(xs_[R:R + 1, :], 1.0),
                      waits=[zv], sig=True)
            ve_ones3 = S3.now("vector")

            def ring_slot3(u):
                i = cn_ring3[0] % CNS
                cn_ring3[0] += 1
                return Cn2[:, i * D:(i + 1) * D], cn_sig3[i]

            def run_ln3(src, waits, waits_u=None):
                i0 = cn_ring3[0]
                slots, sigs = ln_chain(S3, src, waits, ring_slot3, lnpar3,
                                       waits_u=waits_u)
                return slots, sigs, i0

            def run_T3(slots, sigs, i0, dsth, dstl):
                cg = transpose_stage(S3, slots, sigs, dsth, dstl, tpst3)
                te_sig = S3.now("tensor")
                for u in range(NSUB):
                    cn_sig3[(i0 + u) % CNS] = te_sig
                return cg

            st_ready = {}
            wbc = wdec[:, 0:1].broadcast_to((128, T))

            def scan_seq(bq):
                lo, hi = bq * S, (bq + 1) * S
                S3.op("vector", lambda g, lo=lo, hi=hi: g.tensor_tensor_scan(
                    states[lo:hi, :], wbc[lo:hi, :], wkvbuf[lo:hi, :], 0.0,
                    op0=ALU.mult, op1=ALU.add), sig=True)
                ve_scan = S3.now("vector")
                if bq == 0:
                    S3.op("scalar", lambda g: g.activation(
                        stHL[0:S, 0:T], states[0:S, :], ACT.Identity),
                        waits=[ve_scan], sig=True)
                    sc_hi = S3.now("scalar")
                    S3.op("vector", lambda g: g.tensor_tensor(
                        wkvbuf[0:S, 0:T], states[0:S, :], stHL[0:S, 0:T],
                        op=ALU.subtract), waits=[sc_hi], sig=True)
                    ve_lo = S3.now("vector")
                    S3.op("gpsimd", lambda g: g.dma_start(
                        out=stHL[S:2 * S, 0:T], in_=wkvbuf[0:S, 0:T]),
                        waits=[ve_lo], sig=True, dma=True)
                else:
                    S3.op("scalar", lambda g: g.activation(
                        stHL[S:2 * S, T:2 * T], states[S:2 * S, :], ACT.Identity),
                        waits=[ve_scan], sig=True)
                    sc_hi = S3.now("scalar")
                    S3.op("gpsimd", lambda g: g.dma_start(
                        out=stHL[0:S, T:2 * T], in_=stHL[S:2 * S, T:2 * T]),
                        waits=[sc_hi], sig=True, dma=True)
                    gp_hi = S3.now("gpsimd")
                    S3.op("vector", lambda g: g.tensor_tensor(
                        stHL[S:2 * S, T:2 * T], states[S:2 * S, :],
                        stHL[S:2 * S, T:2 * T],
                        op=ALU.subtract), waits=[gp_hi], sig=True)
                st_ready[bq] = [S3.now("gpsimd"), S3.now("vector"), S3.now("scalar")]

            scan_seq(0)

            def load_x1(dst, b, t0, waits):
                ap_sx = bass.AP(x1_hbm.tensor, b * T * D + t0 * D,
                                [[D, 128], [128 * D, NSUB], [1, D]])
                S3.op("sync", lambda g, dst=dst, ap_sx=ap_sx: g.dma_start(
                    out=dst[:, :].rearrange("p (u d) -> p u d", u=NSUB), in_=ap_sx),
                    waits=waits, sig=True, dma=True)
                return S3.now("sync")

            free_q = [None, None, None]

            def stage1_op(i, b, t0):
                par = i % 2
                qar = i % 3
                lx = load_x1(q2[qar], b, t0, [free_q[qar]])
                uev = []
                for u in range(NSUB):
                    for hh in range(NH):
                        bk = dnst3[1][dnst3[0] % len(dnst3[1])]
                        dnst3[0] += 1
                        waits = [S3.bank_wait(bk)]
                        if u == 0 and hh == 0:
                            waits += st_ready[b]
                        for p, rh in enumerate((wo_h, wo_l)):
                            S3.op("tensor", lambda g, u=u, hh=hh, bk=bk, rh=rh, p=p, b=b, t0=t0: g.matmul(
                                pb[bk][:, 0:HCOL],
                                stHL[:, b * T + t0 + u * 128: b * T + t0 + u * 128 + 128],
                                rh[:, hh * HCOL:(hh + 1) * HCOL],
                                start=(p == 0), stop=(p == 1)),
                                waits=waits if p == 0 else (), sig=(p == 1))
                        te = S3.now("tensor")
                        S3.op("vector", lambda g, u=u, hh=hh, bk=bk, par=par, qar=qar: g.tensor_tensor(
                            x2b2[par][:, u * D + hh * HCOL: u * D + hh * HCOL + HCOL],
                            pb[bk][:, 0:HCOL],
                            q2[qar][:, u * D + hh * HCOL: u * D + hh * HCOL + HCOL],
                            op=ALU.add),
                            waits=[te, lx] if (u == 0 and hh == 0) else [te], sig=True)
                        last = S3.now("vector")
                        S3.bank_set(bk, last)
                        if hh == NH - 1:
                            uev.append(last)
                return par, qar, uev

            def stage1_ln(par, uev):
                i0 = cn_ring3[0]
                uln = [ln_sub(S3, x2b2[par], u, [uev[u]], ring_slot3, lnpar3)
                       for u in range(NSUB)]
                return [x[0] for x in uln], [x[1] for x in uln], i0

            xv_lo_free3 = [None]

            def stage2_T(i, par, slots, sigs, i0):
                return run_T3(slots, sigs, i0, G1[par][0], G1[par][1])

            def stage2(i, b, t0, par, qar, cg):
                xv = xvr_stage(S3, G1[par][0], G1[par][1], xv22[par],
                               [ve_ones3] if i < 2 else [], xv_lo_free3[0], cg)
                xv_lo_free3[0] = xv

                def evac_x3(g, u, hh, ps, par=par, qar=qar):
                    return g.scalar_tensor_tensor(
                        q2[qar][:, u * D + hh * HCOL: u * D + hh * HCOL + HCOL],
                        ps, 0.5,
                        x2b2[par][:, u * D + hh * HCOL: u * D + hh * HCOL + HCOL],
                        op0=ALU.mult, op1=ALU.add)
                evs = dense_stage(S3, G1[par][0], G1[par][1], wd0_h, wd0_l,
                                  [], evac_x3, dnst3, xvs=xv22[par],
                                  cg=cg, xv_sig=xv)
                last = None
                for u in range(NSUB):
                    ap_d = bass.AP(out_hbm.tensor, b * T * D + (t0 + u * 128) * D,
                                   [[D, 128], [1, D]])
                    S3.op("scalar", lambda g, qar=qar, ap_d=ap_d, u=u: g.dma_start(
                        out=ap_d, in_=q2[qar][:, u * D:(u + 1) * D]),
                        waits=[evs[u * NH + NH - 1]], sig=True, dma=True)
                    last = S3.now("scalar")
                free_q[qar] = last
                return last

            sides = [(b, t0) for b in range(B) for t0 in range(0, T, SIDE)]
            pend = None
            out_last = None
            for i, (b, t0) in enumerate(sides):
                if i == 1 and B > 1:
                    scan_seq(1)
                par, qar, uev = stage1_op(i, b, t0)
                if pend is not None:
                    (pi, pbb, pt0, ppar, pqar, pslots, psigs, pi0) = pend
                    pcg = stage2_T(pi, ppar, pslots, psigs, pi0)
                slots, sigs, i0 = stage1_ln(par, uev)
                if pend is not None:
                    out_last = stage2(pi, pbb, pt0, ppar, pqar, pcg)
                pend = (i, b, t0, par, qar, slots, sigs, i0)
            (pi, pbb, pt0, ppar, pqar, pslots, psigs, pi0) = pend
            pcg = stage2_T(pi, ppar, pslots, psigs, pi0)
            out_last = stage2(pi, pbb, pt0, ppar, pqar, pcg)

            S3.op("scalar", lambda g: g.nop(), waits=[out_last])
            emit(blk, S3, semsets[2])
        ctx2.close()

    return nc


def kernel(L0_attn_x, L0_attn_time_decay, L0_attn_time_first, L0_attn_W_key,
           L0_attn_W_value, L0_attn_W_output, L0_attn_W_shift, L0_attn_shift_gate,
           L0_attn_ln_w, L0_attn_ln_b, L0_ffn_W, L0_ffn_b, L0_ffn_V_r, L0_ffn_C):
    N = 8
    x = np.ascontiguousarray(np.asarray(L0_attn_x, dtype=np.float32))
    wt = np.ascontiguousarray(np.asarray(L0_ffn_W, dtype=np.float32).T)
    wst = np.ascontiguousarray(np.asarray(L0_attn_W_shift, dtype=np.float32).T)
    wkvt = np.ascontiguousarray(np.concatenate(
        [np.asarray(L0_attn_W_key, np.float32).T,
         np.asarray(L0_attn_W_value, np.float32).T], axis=1))
    vr = np.ascontiguousarray(np.asarray(L0_ffn_V_r, np.float32))
    ctb = np.ascontiguousarray(np.concatenate(
        [np.asarray(L0_ffn_C, np.float32).T,
         np.asarray(L0_ffn_b, np.float32)[None, :]], axis=0))
    wot = np.ascontiguousarray(np.asarray(L0_attn_W_output, np.float32).T)
    td = np.ascontiguousarray(np.asarray(L0_attn_time_decay, np.float32)[:, None])
    tf = np.ascontiguousarray(np.asarray(L0_attn_time_first, np.float32)[:, None])
    sg = np.ascontiguousarray(np.asarray(L0_attn_shift_gate, np.float32)[None, :])
    lnw = np.ascontiguousarray(np.asarray(L0_attn_ln_w, np.float32))
    lnb = np.ascontiguousarray(np.asarray(L0_attn_ln_b, np.float32))
    ident = np.eye(128, dtype=np.float32)

    nc = build_program()
    in_maps = []
    for c in range(N):
        in_maps.append(dict(x=np.ascontiguousarray(x[2 * c:2 * c + 2]),
                            wt=wt, wst=wst, wkvt=wkvt, vr=vr, ctb=ctb, wot=wot,
                            td=td, tf=tf, sg=sg, lnw=lnw, lnb=lnb, ident=ident))
    res = run_bass_kernel_spmd(nc, in_maps, core_ids=list(range(N)))
    return np.concatenate([np.asarray(r["out"], dtype=np.float32)
                           for r in res.results], axis=0)


# version 29
# speedup vs baseline: 1.0287x; 1.0062x over previous
import numpy as np
import concourse.bass as bass
import concourse.mybir as mybir
from concourse.bass_utils import run_bass_kernel_spmd

F32 = mybir.dt.float32
FP16 = mybir.dt.float16
ALU = mybir.AluOpType
ACT = mybir.ActivationFunctionType

ENGINES = ("sync", "tensor", "vector", "scalar", "gpsimd")
SEMS = ENGINES + ("gpsimdc",)
EPS = 1e-5
R = 32


class Sched:

    def __init__(self):
        self.ops = {e: [] for e in ENGINES}
        self.cnt = {e: 0 for e in SEMS}
        self.bank_hist = {}

    def op(self, eng, fn, waits=(), sig=False, dma=False, sem=None):
        ws = [w for w in waits if w is not None]
        for (e, v) in ws:
            assert v <= self.cnt[e], f"wait on future {e}>={v} (now {self.cnt[e]})"
        amt = 16 if dma else 1
        sname = sem or eng
        if sig:
            self.cnt[sname] += amt
        self.ops[eng].append((fn, tuple(ws), sig and sname, amt))
        return (sname, self.cnt[sname]) if sig else None

    def now(self, eng):
        return (eng, self.cnt[eng])

    def bank_wait(self, bk):
        return self.bank_hist.get(bk)

    def bank_set(self, bk, sig):
        self.bank_hist[bk] = sig


class LnPar(dict):

    def __init__(self):
        super().__init__()
        self[0] = 0
        for i in range(1, 7):
            self[i] = None
        self["pend"] = [False] * 6


def emit(blk, sched, sems):
    for ename in ENGINES:
        ops = sched.ops[ename]
        if not ops:
            continue

        def body(eng, ops=ops):
            for fn, waits, sig, amt in ops:
                for (we, wv) in waits:
                    if wv > 0:
                        eng.wait_ge(sems[we], wv)
                instr = fn(eng)
                if sig:
                    if instr is None:
                        instr = eng.nop()
                    instr.then_inc(sems[sig], amt)
        getattr(blk, ename)(body)


def build_program(B=2, T=4096, D=1024, S=64, SIDE=256):
    assert T % (2 * SIDE) == 0 and D % 128 == 0 and SIDE % 128 == 0
    KC = D // 128
    NSUB = SIDE // 128
    NH = max(1, D // 512)
    HCOL = D // NH
    NST = max(1, D // 512)
    STW = min(D, 512)
    NPAIR = T // (2 * SIDE)
    half = T // 2
    BS = B * S
    TGRP = 4
    NTG = (KC + TGRP - 1) // TGRP
    CNS = 2 * NSUB

    nc = bass.Bass("TRN2", target_bir_lowering=False,
                   detect_race_conditions=False, dynamic_dma_scratch_size=4096)

    x_in = nc.dram_tensor("x", [B, T, D], F32, kind="ExternalInput").ap()
    wt_in = nc.dram_tensor("wt", [D, D], F32, kind="ExternalInput").ap()
    wst_in = nc.dram_tensor("wst", [D, D], F32, kind="ExternalInput").ap()
    wkvt_in = nc.dram_tensor("wkvt", [D, 2 * S], F32, kind="ExternalInput").ap()
    vr_in = nc.dram_tensor("vr", [D, R], F32, kind="ExternalInput").ap()
    ctb_in = nc.dram_tensor("ctb", [R + 1, D], F32, kind="ExternalInput").ap()
    wot_in = nc.dram_tensor("wot", [S, D], F32, kind="ExternalInput").ap()
    td_in = nc.dram_tensor("td", [S, 1], F32, kind="ExternalInput").ap()
    tf_in = nc.dram_tensor("tf", [S, 1], F32, kind="ExternalInput").ap()
    sg_in = nc.dram_tensor("sg", [1, D], F32, kind="ExternalInput").ap()
    lnw_in = nc.dram_tensor("lnw", [D], F32, kind="ExternalInput").ap()
    lnb_in = nc.dram_tensor("lnb", [D], F32, kind="ExternalInput").ap()
    id_in = nc.dram_tensor("ident", [128, 128], F32, kind="ExternalInput").ap()
    out_hbm = nc.dram_tensor("out", [B, T, D], F32, kind="ExternalOutput").ap()
    x1_hbm = nc.dram_tensor("x1s", [B, T, D], F32).ap()

    from contextlib import ExitStack
    with ExitStack() as ctx:
        e = ctx.enter_context
        wd0_h = e(nc.sbuf_tensor("wd0_h", [128, KC * D], FP16))
        wd0_l = e(nc.sbuf_tensor("wd0_l", [128, KC * D], FP16))
        wd1_h = e(nc.sbuf_tensor("wd1_h", [128, KC * D], FP16))
        wd1_l = e(nc.sbuf_tensor("wd1_l", [128, KC * D], FP16))
        wkv_h = e(nc.sbuf_tensor("wkv_h", [128, KC * 2 * S], FP16))
        wkv_l = e(nc.sbuf_tensor("wkv_l", [128, KC * 2 * S], FP16))
        vr_h = e(nc.sbuf_tensor("vr_h", [128, KC * R], FP16))
        vr_l = e(nc.sbuf_tensor("vr_l", [128, KC * R], FP16))
        cts_h = e(nc.sbuf_tensor("cts_h", [96, D], FP16))
        cts_l = e(nc.sbuf_tensor("cts_l", [96, D], FP16))
        wo_h = e(nc.sbuf_tensor("wo_h", [128, D], FP16))
        wo_l = e(nc.sbuf_tensor("wo_l", [128, D], FP16))
        oneg = e(nc.sbuf_tensor("oneg", [128, D], F32))
        ident = e(nc.sbuf_tensor("ident_s", [128, 128], F32))
        wdec = e(nc.sbuf_tensor("wdec", [128, 1], F32))
        negef = e(nc.sbuf_tensor("negef", [128, 1], F32))
        negkb = e(nc.sbuf_tensor("negkb", [128, 1], F32))
        vbias = e(nc.sbuf_tensor("vbias", [128, 1], F32))
        epsv = e(nc.sbuf_tensor("epsv", [128, 1], F32))
        wkvbuf = e(nc.sbuf_tensor("wkvbuf", [128, T], F32))
        st6 = e(nc.sbuf_tensor("st6", [128, 2 * 6 * NST], F32))
        st2 = e(nc.sbuf_tensor("st2", [128, 6 * 2 * NSUB], F32))
        rln = e(nc.sbuf_tensor("rln", [128, 6 * NSUB], F32))
        negmr = e(nc.sbuf_tensor("negmr", [128, 6 * NSUB], F32))
        sc512 = e(nc.sbuf_tensor("sc512", [128, 2 * HCOL], F32))
        eb = e(nc.sbuf_tensor("eb", [128, 2 * SIDE], F32))
        sb = e(nc.sbuf_tensor("sb", [128, 2 * SIDE], F32))
        xvlo = e(nc.sbuf_tensor("xvlo", [R, SIDE], FP16))
        pb = [e(nc.psum_tensor(f"pb{i}", [128, 512], F32)) for i in range(8)]

        semsets = [{n: e(nc.semaphore(f"s{bi}_{n}")) for n in SEMS} for bi in range(3)]

        ctxc = ExitStack()
        blk_clr = ctxc.enter_context(nc.Block())

        def _clear(eng):
            for ss_ in semsets:
                for hh_ in ss_.values():
                    eng.sem_clear(hh_)
        blk_clr.sync(_clear)
        ctxc.close()

        LNRING = 6

        def ln_stats(S_, src, u, waits, lnpar):
            pp = lnpar[0] % LNRING
            assert not lnpar["pend"][pp], "LN stats ring slot still pending apply"
            lnpar["pend"][pp] = True
            st2c = 2 * NSUB * pp + 2 * u
            lnpar[0] += 1
            w_u = list(waits) + [lnpar[1 + pp]]
            for hh in range(NST):
                S_.op("vector", lambda g, u=u, hh=hh, pp=pp: g.bn_stats(
                    st6[:, 6 * NST * (pp % 2) + 6 * hh: 6 * NST * (pp % 2) + 6 * hh + 6],
                    src[:, u * D + hh * STW: u * D + hh * STW + STW]),
                    waits=w_u if hh == 0 else (),
                    sig=(hh == NST - 1))
            fence = S_.now("vector")
            S_.op("vector", lambda g, st2c=st2c: g.bn_aggr(
                st2[:, st2c:st2c + 2],
                st6[:, 6 * NST * (pp % 2): 6 * NST * (pp % 2) + 6 * NST]),
                waits=[fence], sig=True)
            return pp, S_.now("vector")

        def ln_rln(S_, u, pp, ve):
            rlc = NSUB * pp + u
            st2c = 2 * NSUB * pp + 2 * u
            var_ap = st2[:, st2c + 1:st2c + 2]
            S_.op("scalar", lambda g, rlc=rlc, var_ap=var_ap: g.activation(
                rln[:, rlc:rlc + 1], var_ap, ACT.Ln, bias=epsv[:, :]),
                waits=[ve], sig=True)
            S_.op("scalar", lambda g, rlc=rlc: g.activation(
                rln[:, rlc:rlc + 1], rln[:, rlc:rlc + 1], ACT.Exp, scale=-0.5),
                sig=True)
            return S_.now("scalar")

        def ln_apply(S_, src, u, pp, ve, slot_of, lnpar, on_vector=True,
                     rln_sig=None):
            st2c = 2 * NSUB * pp + 2 * u
            rlc = NSUB * pp + u
            m_ap = st2[:, st2c:st2c + 1]
            if rln_sig is None:
                rln_sig = ln_rln(S_, u, pp, ve)
            slot, slot_free = slot_of(u)
            if on_vector:
                S_.op("vector", lambda g, u=u, slot=slot, rlc=rlc, m_ap=m_ap: g.tensor_scalar(
                    slot, src[:, u * D:(u + 1) * D], m_ap, rln[:, rlc:rlc + 1],
                    op0=ALU.subtract, op1=ALU.mult),
                    waits=[rln_sig, slot_free], sig=True)
                sig = S_.now("vector")
            else:
                S_.op("scalar", lambda g, rlc=rlc, m_ap=m_ap: g.activation(
                    negmr[:, rlc:rlc + 1], m_ap, ACT.Identity, scale=-1.0), sig=True)
                S_.op("scalar", lambda g, rlc=rlc: g.activation(
                    negmr[:, rlc:rlc + 1], negmr[:, rlc:rlc + 1], ACT.Identity,
                    scale=rln[:, rlc:rlc + 1]), sig=True)
                S_.op("scalar", lambda g, u=u, slot=slot, rlc=rlc: g.activation(
                    slot, src[:, u * D:(u + 1) * D], ACT.Identity,
                    scale=rln[:, rlc:rlc + 1], bias=negmr[:, rlc:rlc + 1]),
                    waits=[slot_free], sig=True)
                sig = S_.now("scalar")
            lnpar[1 + pp] = sig
            lnpar["pend"][pp] = False
            return slot, sig

        def ln_sub(S_, src, u, waits, slot_of, lnpar):
            pp, ve = ln_stats(S_, src, u, waits, lnpar)
            return ln_apply(S_, src, u, pp, ve, slot_of, lnpar)

        def ln_chain(S_, src, waits, slot_of, lnpar, waits_u=None):
            sigs = []
            slots = []
            for u in range(NSUB):
                w_u = list(waits) if u == 0 else []
                if waits_u is not None:
                    w_u = w_u + [waits_u[u]]
                slot, sig = ln_sub(S_, src, u, w_u, slot_of, lnpar)
                slots.append(slot)
                sigs.append(sig)
            return slots, sigs

        def transpose_stage(S_, slots, apply_sigs, dsth, dstl, tpst):
            cg = [None] * NTG
            for u in range(NSUB):
                for h4 in range(NTG):
                    nck = min(TGRP, KC - h4 * TGRP)
                    bk = 2 + (tpst[0] % 4)
                    tpst[0] += 1
                    waits = [S_.bank_wait(bk), apply_sigs[u]]
                    for cc in range(nck):
                        c = h4 * TGRP + cc
                        S_.op("tensor", lambda g, u=u, c=c, cc=cc, bk=bk: g.matmul(
                            pb[bk][:, cc * 128:(cc + 1) * 128],
                            slots[u][:, c * 128:(c + 1) * 128],
                            ident[:, :], is_transpose=True,
                            start=(cc == 0), stop=(cc == nck - 1)),
                            waits=waits if cc == 0 else (), sig=(cc == nck - 1))
                    te = S_.now("tensor")
                    dh = bass.AP(dsth[:, :].tensor, h4 * TGRP * SIDE + u * 128,
                                 [[KC * SIDE, 128], [SIDE, nck], [1, 128]])
                    dl = bass.AP(dstl[:, :].tensor, h4 * TGRP * SIDE + u * 128,
                                 [[KC * SIDE, 128], [SIDE, nck], [1, 128]])
                    S_.op("scalar", lambda g, bk=bk, dh=dh, nck=nck: g.activation(
                        dh, pb[bk][:, 0:nck * 128], ACT.Identity),
                        waits=[te], sig=True)
                    sc = S_.now("scalar")
                    S_.op("vector", lambda g, bk=bk, dh=dh, dl=dl, nck=nck: g.tensor_tensor(
                        dl, pb[bk][:, 0:nck * 128], dh, op=ALU.subtract),
                        waits=[te, sc], sig=True)
                    S_.bank_set(bk, S_.now("vector"))
                    cg[h4] = S_.now("vector")
            return cg

        xv_bank = [0]
        xv_mod = [2]

        def xvr_stage(S_, xFh, xFl, xvs, ready, lo_free, cg):
            bq = 6 + (xv_bank[0] % xv_mod[0])
            xv_bank[0] += 1
            for p, (lh, rh) in enumerate(((vr_h, xFh), (vr_h, xFl), (vr_l, xFh))):
                for c in range(KC):
                    waits = []
                    if p == 0 and c == 0:
                        waits += [S_.bank_wait(bq)] + list(ready)
                    if p == 0 and c % TGRP == 0:
                        waits.append(cg[c // TGRP])
                    S_.op("tensor", lambda g, p=p, c=c, lh=lh, rh=rh, bq=bq: g.matmul(
                        pb[bq][0:R, 0:SIDE], lh[:, c * R:(c + 1) * R],
                        rh[:, c * SIDE:(c + 1) * SIDE],
                        start=(p == 0 and c == 0), stop=(p == 2 and c == KC - 1)),
                        waits=waits, sig=(p == 2 and c == KC - 1))
            te = S_.now("tensor")
            S_.op("scalar", lambda g, xvs=xvs, bq=bq: g.activation(
                xvs[0:R, :], pb[bq][0:R, 0:SIDE], ACT.Identity), waits=[te], sig=True)
            sc = S_.now("scalar")
            S_.op("vector", lambda g, xvs=xvs, bq=bq: g.tensor_tensor(
                xvlo[0:R, 0:SIDE], pb[bq][0:R, 0:SIDE], xvs[0:R, :], op=ALU.subtract),
                waits=[te, sc, lo_free], sig=True)
            ve = S_.now("vector")
            S_.bank_set(bq, ve)
            S_.op("sync", lambda g, xvs=xvs: g.dma_start(
                out=xvs[R + 1:2 * R + 1, :], in_=xvlo[0:R, 0:SIDE]),
                waits=[ve], sig=True, dma=True)
            return S_.now("sync")

        def dense_stage(S_, xFh, xFl, wh, wl, ready, evac, dnst, xvs=None,
                        cg=None, xv_sig=None, post_u=None):
            evs = []
            for u in range(NSUB):
                for hh in range(NH):
                    bk = dnst[1][dnst[0] % len(dnst[1])]
                    dnst[0] += 1
                    first = (u == 0 and hh == 0)
                    for p, (lh, rh) in enumerate(((xFh, wh), (xFh, wl), (xFl, wh))):
                        for c in range(KC):
                            waits = []
                            if p == 0 and c == 0:
                                waits.append(S_.bank_wait(bk))
                                if first:
                                    waits += list(ready)
                            if first and p == 0 and cg is not None and c % TGRP == 0:
                                waits.append(cg[c // TGRP])
                            S_.op("tensor", lambda g, u=u, hh=hh, bk=bk, c=c, lh=lh, rh=rh, p=p: g.matmul(
                                pb[bk][:, 0:HCOL],
                                lh[:, c * SIDE + u * 128: c * SIDE + u * 128 + 128],
                                rh[:, c * D + hh * HCOL: c * D + hh * HCOL + HCOL],
                                start=(p == 0 and c == 0),
                                stop=(xvs is None and p == 2 and c == KC - 1)),
                                waits=waits, sig=(xvs is None and p == 2 and c == KC - 1))
                    if xvs is not None:
                        for p, rh in enumerate((cts_h, cts_l)):
                            S_.op("tensor", lambda g, u=u, hh=hh, bk=bk, rh=rh, p=p, xvs=xvs: g.matmul(
                                pb[bk][:, 0:HCOL],
                                xvs[0:96, u * 128:(u + 1) * 128],
                                rh[0:96, hh * HCOL:(hh + 1) * HCOL],
                                start=False, stop=(p == 1)),
                                waits=[xv_sig] if (first and p == 0) else (),
                                sig=(p == 1))
                    te = S_.now("tensor")
                    S_.op("vector", lambda g, u=u, hh=hh, bk=bk: evac(
                        g, u, hh, pb[bk][:, 0:HCOL]), waits=[te], sig=True)
                    evs.append(S_.now("vector"))
                    S_.bank_set(bk, evs[-1])
                    if post_u is not None and hh == NH - 1:
                        post_u(u, evs[-1])
            return evs

        ctx0 = ExitStack()
        e0 = ctx0.enter_context
        stg = e0(nc.sbuf_tensor("stg", [128, 2048], F32))
        stg2 = e0(nc.sbuf_tensor("stg2", [128, 2048], F32))
        grow = e0(nc.sbuf_tensor("grow", [1, 3 * D], F32))
        lnwf = e0(nc.sbuf_tensor("lnwf", [128, KC], F32))
        lnbf = e0(nc.sbuf_tensor("lnbf", [128, KC], F32))
        onecol = e0(nc.sbuf_tensor("onecol", [1, 128], F32))
        kvb = e0(nc.sbuf_tensor("kvb", [128, 1], F32))
        blk = e0(nc.Block())
        if True:
            S0 = Sched()
            S0.op("vector", lambda g: g.memset(epsv[:, :], EPS), sig=True)

            def ld(dst, src, waits=(), slow=False):
                return S0.op("sync", lambda g, d=dst, s=src, slow=slow: g.dma_start(
                    out=d, in_=s, allow_slow_non_contiguous=slow),
                    waits=waits, sig=True, dma=True)

            ld(ident[:, :], id_in[:, :])
            w_td0 = ld(wdec[0:S, :], td_in[:, :])
            w_td1 = ld(wdec[S:2 * S, :], td_in[:, :])
            w_tf0 = ld(negef[0:S, :], tf_in[:, :])
            w_tf1 = ld(negef[S:2 * S, :], tf_in[:, :])
            w_sg = ld(grow[0:1, 0:D], sg_in[:, :])
            lnw_src = bass.AP(lnw_in.tensor, 0, [[1, 128], [128, KC]])
            lnb_src = bass.AP(lnb_in.tensor, 0, [[1, 128], [128, KC]])
            w_lnw = ld(lnwf[:, :], lnw_src, slow=True)
            w_lnb = ld(lnbf[:, :], lnb_src, slow=True)
            w_kvt = ld(stg2[:, 0:KC * 2 * S].rearrange("p (c n) -> p c n", c=KC),
                       bass.AP(wkvt_in.tensor, 0, [[2 * S, 128], [128 * 2 * S, KC], [1, 2 * S]]))
            w_vr = ld(stg[:, 0:KC * R].rearrange("p (c n) -> p c n", c=KC),
                      bass.AP(vr_in.tensor, 0, [[R, 128], [128 * R, KC], [1, R]]))
            w_ctb = ld(stg[0:R + 1, 1024:1024 + D], ctb_in[:, :])
            w_wot = ld(stg2[0:S, 1024:1024 + D], wot_in[:, :])
            w_wot2 = ld(stg2[S:2 * S, 1024:1024 + D], wot_in[:, :])

            S0.op("scalar", lambda g: g.activation(wdec[:, :], wdec[:, :], ACT.Exp),
                  waits=[w_td0, w_td1], sig=True)
            S0.op("scalar", lambda g: g.activation(negef[:, :], negef[:, :], ACT.Exp),
                  waits=[w_tf0, w_tf1], sig=True)
            sc_e = S0.now("scalar")
            ve_ef = S0.op("vector", lambda g: g.tensor_scalar_mul(negef[:, :], negef[:, :], -1.0),
                          waits=[sc_e], sig=True)

            S0.op("scalar", lambda g: g.activation(grow[0:1, D:2 * D], grow[0:1, 0:D], ACT.Exp),
                  waits=[w_sg], sig=True)
            S0.op("scalar", lambda g: g.activation(grow[0:1, 2 * D:3 * D], grow[0:1, 0:D],
                                                   ACT.Exp, scale=-1.0), sig=True)
            sc_g = S0.now("scalar")
            S0.op("vector", lambda g: g.tensor_scalar_add(grow[0:1, D:3 * D],
                                                          grow[0:1, D:3 * D], 1.0),
                  waits=[sc_g], sig=True)
            fence_g = S0.now("vector")
            ve_g = S0.op("vector", lambda g: g.reciprocal(grow[0:1, D:3 * D],
                                                         grow[0:1, D:3 * D]),
                         waits=[fence_g], sig=True)
            ve_one = S0.op("vector", lambda g: g.memset(onecol[:, :], 1.0), sig=True)

            for row, dst in ((1, oneg),):
                for hc in range(max(1, D // 512)):
                    w = min(512, D)
                    ssl = slice(row * D + hc * 512, row * D + hc * 512 + w)
                    bk = hc % 2 + (0 if row == 1 else 2)
                    S0.op("tensor", lambda g, ssl=ssl, bk=bk, w=w: g.matmul(
                        pb[bk][:, 0:w], onecol[:, :], grow[0:1, ssl],
                        start=True, stop=True), waits=[ve_g, ve_one], sig=True)
                    te = S0.now("tensor")
                    S0.op("vector", lambda g, hc=hc, w=w, bk=bk, dst=dst: g.tensor_copy(
                        dst[:, hc * 512:hc * 512 + w], pb[bk][:, 0:w]),
                        waits=[te], sig=True)
            ve_gbc = S0.now("vector")

            for c in range(KC):
                S0.op("tensor", lambda g, c=c: g.matmul(
                    pb[4][0:2 * S, 0:1], stg2[:, c * 2 * S:(c + 1) * 2 * S],
                    lnbf[:, c:c + 1], start=(c == 0), stop=(c == KC - 1)),
                    waits=[w_kvt, w_lnb] if c == 0 else (), sig=(c == KC - 1))
            te_kvb = S0.now("tensor")
            ve_kvb = S0.op("vector", lambda g: g.tensor_copy(kvb[0:2 * S, :], pb[4][0:2 * S, 0:1]),
                           waits=[te_kvb], sig=True)
            ve_nkb = S0.op("vector", lambda g: g.tensor_scalar_mul(negkb[0:S, :], kvb[0:S, :], -1.0),
                           waits=[ve_kvb], sig=True)
            g_b1 = S0.op("gpsimd", lambda g: g.dma_start(out=negkb[S:2 * S, :], in_=negkb[0:S, :]),
                         waits=[ve_nkb], sig=True, dma=True)
            g_b2 = S0.op("gpsimd", lambda g: g.dma_start(out=vbias[0:S, :], in_=kvb[S:2 * S, :]),
                         waits=[ve_kvb], sig=True, dma=True)
            g_b3 = S0.op("gpsimd", lambda g: g.dma_start(out=vbias[S:2 * S, :], in_=kvb[S:2 * S, :]),
                         waits=[ve_kvb], sig=True, dma=True)

            for c in range(KC):
                S0.op("vector", lambda g, c=c: g.tensor_scalar_mul(
                    stg2[:, c * 2 * S:(c + 1) * 2 * S], stg2[:, c * 2 * S:(c + 1) * 2 * S],
                    lnwf[:, c:c + 1]),
                    waits=[w_lnw, te_kvb] if c == 0 else (), sig=(c == KC - 1))
            f1_ = S0.now("vector")
            S0.op("vector", lambda g: g.tensor_copy(wkv_h[:, :], stg2[:, 0:KC * 2 * S]),
                  waits=[f1_], sig=True)
            f2_ = S0.now("vector")
            S0.op("vector", lambda g: g.tensor_tensor(wkv_l[:, :], stg2[:, 0:KC * 2 * S],
                                                      wkv_h[:, :], op=ALU.subtract),
                  waits=[f2_], sig=True)
            S0.op("vector", lambda g: g.tensor_copy(vr_h[:, :], stg[:, 0:KC * R]),
                  waits=[w_vr], sig=True)
            f3_ = S0.now("vector")
            S0.op("vector", lambda g: g.tensor_tensor(vr_l[:, :], stg[:, 0:KC * R],
                                                      vr_h[:, :], op=ALU.subtract),
                  waits=[f3_], sig=True)
            S0.op("vector", lambda g: g.memset(cts_h[:, :], 0.0), sig=True)
            S0.op("vector", lambda g: g.memset(cts_l[:, :], 0.0), sig=True)
            zc = S0.now("vector")
            S0.op("vector", lambda g: g.tensor_copy(cts_h[0:R + 1, :], stg[0:R + 1, 1024:1024 + D]),
                  waits=[w_ctb, zc], sig=True)
            f4_ = S0.now("vector")
            S0.op("vector", lambda g: g.tensor_tensor(cts_l[0:R + 1, :], stg[0:R + 1, 1024:1024 + D],
                                                      cts_h[0:R + 1, :], op=ALU.subtract),
                  waits=[f4_], sig=True)
            g_ct = S0.op("gpsimd", lambda g: g.dma_start(
                out=cts_h[R + 1:2 * R + 2, :], in_=cts_h[0:R + 1, :]),
                waits=[f4_], sig=True, dma=True)
            S0.op("vector", lambda g: g.memset(wo_l[:, :], 0.0), sig=True)
            zw = S0.now("vector")
            S0.op("vector", lambda g: g.tensor_copy(wo_h[:, :], stg2[:, 1024:1024 + D]),
                  waits=[w_wot, w_wot2], sig=True)
            f5_ = S0.now("vector")
            S0.op("vector", lambda g: g.tensor_tensor(wo_l[0:S, :], stg2[0:S, 1024:1024 + D],
                                                      wo_h[0:S, :], op=ALU.subtract),
                  waits=[f5_, zw], sig=True)
            ve_small = S0.now("vector")

            S0.op("gpsimd", lambda g: g.nop(), waits=[g_b1, g_b2, g_b3, g_ct])
            S0.op("sync", lambda g: g.nop(), waits=[ve_ef, S0.now("vector")])
            emit(blk, S0, semsets[0])
        ctx0.close()

        ctx1 = ExitStack()
        e1 = ctx1.enter_context
        PAr = [e1(nc.sbuf_tensor(f"pa{i}", [128, D * NSUB], F32)) for i in range(3)]
        PBr = [e1(nc.sbuf_tensor(f"pbx{i}", [128, D * NSUB], F32)) for i in range(3)]
        yF = [(e1(nc.sbuf_tensor(f"yfh{i}", [128, KC * SIDE], FP16)),
               e1(nc.sbuf_tensor(f"yfl{i}", [128, KC * SIDE], FP16))) for i in range(2)]
        F1 = [(e1(nc.sbuf_tensor(f"f1h{i}", [128, KC * SIDE], FP16)),
               e1(nc.sbuf_tensor(f"f1l{i}", [128, KC * SIDE], FP16))) for i in range(2)]
        Cn = e1(nc.sbuf_tensor("cn", [128, CNS * D], F32))
        xvs2 = [e1(nc.sbuf_tensor(f"xvs{i}", [128, SIDE], FP16)) for i in range(2)]
        blk = e1(nc.Block())
        if True:
            S1 = Sched()
            tpst = [0]
            dnst = [0, (0, 1)]
            cn_ring = [0]
            cn_sig = [None] * CNS
            lnpar = LnPar()

            for xs_ in xvs2:
                S1.op("vector", lambda g, xs_=xs_: g.memset(xs_[:, :], 0.0), sig=True)
                zv = S1.now("vector")
                S1.op("vector", lambda g, xs_=xs_: g.memset(xs_[R:R + 1, :], 1.0),
                      waits=[zv], sig=True)
            ve_ones = S1.now("vector")

            ring_trace = []

            def ring_slot(u):
                i = cn_ring[0] % CNS
                cn_ring[0] += 1
                ring_trace.append(i)
                return Cn[:, i * D:(i + 1) * D], cn_sig[i]

            def run_ln(src, waits, dst=None, waits_u=None):
                if dst is None:
                    i0 = cn_ring[0]
                    slots, sigs = ln_chain(S1, src, waits, ring_slot, lnpar,
                                           waits_u=waits_u)
                    return slots, sigs, i0
                slots, sigs = ln_chain(
                    S1, src, waits,
                    lambda u: (dst[:, u * D:(u + 1) * D], None), lnpar,
                    waits_u=waits_u)
                return slots, sigs, None

            def run_T(slots, sigs, i0, dsth, dstl, ring_idx=None):
                cg = transpose_stage(S1, slots, sigs, dsth, dstl, tpst)
                te_sig = S1.now("tensor")
                if i0 is not None:
                    for u in range(NSUB):
                        cn_sig[(i0 + u) % CNS] = te_sig
                if ring_idx is not None:
                    for i in ring_idx:
                        cn_sig[i] = te_sig
                return cg

            def load_x(dst, b, t0, waits):
                ap_s = bass.AP(x_in.tensor, b * T * D + t0 * D,
                               [[D, 128], [128 * D, NSUB], [1, D]])
                S1.op("sync", lambda g, dst=dst, ap_s=ap_s: g.dma_start(
                    out=dst[:, :].rearrange("p (u d) -> p u d", u=NSUB), in_=ap_s),
                    waits=waits, sig=True, dma=True)
                return S1.now("sync")

            def kv_stage(xFh, xFl, b, t0, cg):
                lo, hi = b * S, (b + 1) * S
                kvp = kvpar[0] % 2
                c0 = kvp * SIDE
                kvpar[0] += 1
                tp = (0, b * S) if b > 0 else None
                te_sigs = []
                for bank, col0 in ((0, 0), (1, S)):
                    for p, (lh, rh) in enumerate(((wkv_h, xFh), (wkv_h, xFl), (wkv_l, xFh))):
                        for c in range(KC):
                            waits = []
                            if p == 0 and c == 0:
                                waits.append(S1.bank_wait(bank))
                            if bank == 0 and p == 0 and c % TGRP == 0:
                                waits.append(cg[c // TGRP])
                            S1.op("tensor", lambda g, bank=bank, col0=col0, p=p, c=c, lh=lh, rh=rh, lo=lo, hi=hi, tp=tp: g.matmul(
                                pb[bank][lo:hi, 0:SIDE],
                                lh[:, c * 2 * S + col0: c * 2 * S + col0 + S],
                                rh[:, c * SIDE:(c + 1) * SIDE],
                                start=(p == 0 and c == 0), stop=(p == 2 and c == KC - 1),
                                tile_position=tp),
                                waits=waits,
                                sig=(p == 2 and c == KC - 1))
                    te_sigs.append(S1.now("tensor"))
                te_k, te_v = te_sigs
                S1.op("scalar", lambda g, lo=lo, hi=hi, c0=c0: g.activation(
                    eb[lo:hi, c0:c0 + SIDE], pb[0][lo:hi, 0:SIDE], ACT.Exp,
                    bias=negkb[lo:hi, :], scale=-1.0),
                    waits=[te_k, kv_free[kvp]], sig=True)
                sc1 = S1.now("scalar")
                S1.bank_set(0, sc1)
                S1.op("gpsimd", lambda g, lo=lo, hi=hi, c0=c0: g.tensor_scalar_add(
                    eb[lo:hi, c0:c0 + SIDE], eb[lo:hi, c0:c0 + SIDE], 1.0),
                    waits=[sc1], sig=True, sem="gpsimdc")
                fe = S1.now("gpsimdc")
                S1.op("vector", lambda g, lo=lo, hi=hi, c0=c0: g.reciprocal(
                    sb[lo:hi, c0:c0 + SIDE], eb[lo:hi, c0:c0 + SIDE]),
                    waits=[fe], sig=True)
                ve1 = S1.now("vector")
                S1.op("scalar", lambda g, lo=lo, hi=hi, c0=c0: g.activation(
                    eb[lo:hi, c0:c0 + SIDE], sb[lo:hi, c0:c0 + SIDE], ACT.Exp,
                    scale=negef[lo:hi, :]),
                    waits=[ve1], sig=True)
                sc2 = S1.now("scalar")
                S1.op("vector", lambda g, lo=lo, hi=hi, t0=t0, c0=c0: g.scalar_tensor_tensor(
                    wkvbuf[lo:hi, t0:t0 + SIDE], pb[1][lo:hi, 0:SIDE], vbias[lo:hi, :],
                    eb[lo:hi, c0:c0 + SIDE], op0=ALU.add, op1=ALU.mult),
                    waits=[te_v, sc2], sig=True)
                ve2 = S1.now("vector")
                kv_free[kvp] = ve2
                S1.bank_set(1, ve2)
                return ve2

            def wsplit(chunks, wsrc, dh, dl, scale_by_g):
                last = None
                for c in chunks:
                    del ring_trace[:]
                    slot, sfree = ring_slot(0)
                    i_ = ring_trace[0]
                    S1.op("sync", lambda g, slot=slot, wsrc=wsrc, c=c: g.dma_start(
                        out=slot, in_=bass.AP(wsrc.tensor, c * 128 * D,
                                              [[D, 128], [1, D]])),
                        waits=[sfree], sig=True, dma=True)
                    wld = S1.now("sync")
                    waits = [wld]
                    if scale_by_g:
                        del ring_trace[:]
                        slot2, sfree2 = ring_slot(0)
                        i2_ = ring_trace[0]
                        S1.op("vector", lambda g, slot=slot, slot2=slot2: g.tensor_tensor(
                            slot2, slot, oneg[:, 0:D], op=ALU.mult),
                            waits=[wld, sfree2], sig=True)
                        fw = S1.now("vector")
                        S1.op("vector", lambda g, slot=slot, slot2=slot2: g.tensor_tensor(
                            slot, slot, slot2, op=ALU.subtract),
                            waits=[fw], sig=True)
                        cn_sig[i2_] = S1.now("vector")
                        waits = [S1.now("vector")]
                    S1.op("vector", lambda g, slot=slot, dh=dh, c=c: g.tensor_copy(
                        dh[:, c * D:(c + 1) * D], slot), waits=waits, sig=True)
                    fw2 = S1.now("vector")
                    S1.op("vector", lambda g, slot=slot, dh=dh, dl=dl, c=c: g.tensor_tensor(
                        dl[:, c * D:(c + 1) * D], slot, dh[:, c * D:(c + 1) * D],
                        op=ALU.subtract), waits=[fw2], sig=True)
                    last = S1.now("vector")
                    cn_sig[i_] = last
                return last

            pa_free = [None, None, None]
            pb_free = [[], [], []]
            xv_lo_free = None
            carry = None
            wd0_ready = None
            wd1_ready = None
            kvpar = [0]
            kv_free = [None, None]
            carry_n = {}

            pairs = [(b, pr) for b in range(B) for pr in range(NPAIR)]
            nside = 0
            for pi, (b, pr) in enumerate(pairs):
                if True:
                    t0A = pr * SIDE
                    t0B = t0A + half
                    sA = nside
                    nside += 2
                    iA, iB = sA % 3, (sA + 1) % 3
                    jA, jB = 0, 1
                    if carry is None:
                        lxA = load_x(PAr[iA], b, t0A, [pa_free[iA]])
                        lxB = load_x(PAr[iB], b, t0B, [pa_free[iB]])
                        slA, sgA, i0A = run_ln(PAr[iA], [lxA])
                        slB, sgB, i0B = run_ln(PAr[iB], [lxB])
                        cgA = run_T(slA, sgA, i0A, F1[jA][0], F1[jA][1])
                        cgB = run_T(slB, sgB, i0B, F1[jB][0], F1[jB][1])
                    else:
                        cgA, cgB = carry
                    xvA = xvr_stage(S1, F1[jA][0], F1[jA][1], xvs2[jA],
                                    [ve_ones], xv_lo_free, cgA)
                    if pi == 0:
                        wd0_ready = wsplit(range(KC), wt_in, wd0_h, wd0_l, False)
                    xvB = xvr_stage(S1, F1[jB][0], F1[jB][1], xvs2[jB],
                                    [ve_ones], xvA, cgB)
                    xv_lo_free = xvB
                    ev1 = {}
                    ysl = {}
                    for si, ii, jj, xv, cg, t0 in (("A", iA, jA, xvA, cgA, t0A),
                                                   ("B", iB, jB, xvB, cgB, t0B)):
                        def evac_x1(g, u, hh, ps, ii=ii):
                            return g.scalar_tensor_tensor(
                                PBr[ii][:, u * D + hh * HCOL: u * D + hh * HCOL + HCOL],
                                ps, 0.5,
                                PAr[ii][:, u * D + hh * HCOL: u * D + hh * HCOL + HCOL],
                                op0=ALU.mult, op1=ALU.add)
                        yslots = []
                        ysigs = []
                        deferred = []

                        def post_u(u, ev, si=si, ii=ii, t0=t0, yslots=yslots,
                                   ysigs=ysigs, deferred=deferred):
                            ap_d = bass.AP(x1_hbm.tensor,
                                           b * T * D + (t0 + u * 128) * D,
                                           [[D, 128], [1, D]])
                            S1.op("scalar", lambda g, ii=ii, ap_d=ap_d, u=u: g.dma_start(
                                out=ap_d, in_=PBr[ii][:, u * D:(u + 1) * D]),
                                waits=[ev], sig=True, dma=True)
                            if si == "B" and u == NSUB - 1:
                                deferred.append((u, ev))
                                return
                            sl, sg = ln_sub(S1, PBr[ii], u, [ev],
                                            lambda uu: (PAr[ii][:, uu * D:(uu + 1) * D],
                                                        None), lnpar)
                            yslots.append(sl)
                            ysigs.append(sg)
                        evs = dense_stage(S1, F1[jj][0], F1[jj][1], wd0_h, wd0_l,
                                          pb_free[ii] + ([wd0_ready] if pi == 0 else []),
                                          evac_x1, dnst,
                                          xvs=xvs2[jj], cg=cg, xv_sig=xv,
                                          post_u=post_u)
                        ev1[si] = evs[-1]
                        ev1[si + "sp"] = S1.now("scalar")
                        ysl[si] = (yslots, ysigs, deferred, ii)
                        if pi == 0:
                            half_c = range(0, KC // 2) if si == "A" else range(KC // 2, KC)
                            wd1_ready = wsplit(half_c, wst_in, wd1_h, wd1_l, True)
                    ycgA = run_T(ysl["A"][0], ysl["A"][1], None, yF[0][0], yF[0][1])
                    teA2 = S1.now("tensor")
                    onegb = bass.AP(oneg[:, :].tensor, 0,
                                    [[D, 128], [0, NSUB], [1, D]])
                    S1.op("vector", lambda g, iA=iA, onegb=onegb: g.tensor_tensor(
                        PAr[iA][:, :].rearrange("p (u d) -> p u d", u=NSUB),
                        PAr[iA][:, :].rearrange("p (u d) -> p u d", u=NSUB),
                        onegb, op=ALU.mult),
                        waits=[teA2, ysl["A"][1][-1]], sig=True)
                    ugA = S1.now("vector")
                    for (du, dev) in ysl["B"][2]:
                        iiB = ysl["B"][3]
                        sl, sg = ln_sub(S1, PBr[iiB], du, [dev],
                                        lambda uu: (PAr[iiB][:, uu * D:(uu + 1) * D],
                                                    None), lnpar)
                        ysl["B"][0].append(sl)
                        ysl["B"][1].append(sg)
                    ycgB = run_T(ysl["B"][0], ysl["B"][1], None, yF[1][0], yF[1][1])
                    teB2 = S1.now("tensor")
                    S1.op("vector", lambda g, iB=iB, onegb=onegb: g.tensor_tensor(
                        PAr[iB][:, :].rearrange("p (u d) -> p u d", u=NSUB),
                        PAr[iB][:, :].rearrange("p (u d) -> p u d", u=NSUB),
                        onegb, op=ALU.mult),
                        waits=[teB2, ysl["B"][1][-1]], sig=True)
                    ugB = S1.now("vector")
                    usig = {"A": ugA, "B": ugB}
                    sl3 = {}
                    for si, ii, jj, oj, t0 in (("A", iA, jA, 1, t0A),
                                               ("B", iB, jB, 0, t0B)):
                        ycg = ycgB if si == "A" else ycgA
                        last_ev = None
                        uev = []
                        for u in range(NSUB):
                            for hh in range(NH):
                                bk = dnst[1][dnst[0] % len(dnst[1])]
                                dnst[0] += 1
                                first = (u == 0 and hh == 0)
                                for p, (lh, rh) in enumerate(((yF[oj][0], wd1_h),
                                                              (yF[oj][0], wd1_l),
                                                              (yF[oj][1], wd1_h))):
                                    for c in range(KC):
                                        waits = []
                                        if p == 0 and c == 0:
                                            waits.append(S1.bank_wait(bk))
                                            if first:
                                                waits.append(ev1[si + "sp"])
                                                if pi == 0:
                                                    waits.append(wd1_ready)
                                        if first and p == 0 and c % TGRP == 0:
                                            waits.append(ycg[c // TGRP])
                                        S1.op("tensor", lambda g, u=u, hh=hh, bk=bk, c=c, lh=lh, rh=rh, p=p: g.matmul(
                                            pb[bk][:, 0:HCOL],
                                            lh[:, c * SIDE + u * 128: c * SIDE + u * 128 + 128],
                                            rh[:, c * D + hh * HCOL: c * D + hh * HCOL + HCOL],
                                            start=(p == 0 and c == 0),
                                            stop=(p == 2 and c == KC - 1)),
                                            waits=waits,
                                            sig=(p == 2 and c == KC - 1))
                                te = S1.now("tensor")
                                S1.op("vector", lambda g, u=u, hh=hh, bk=bk, ii=ii: g.tensor_tensor(
                                    PBr[ii][:, u * D + hh * HCOL: u * D + hh * HCOL + HCOL],
                                    pb[bk][:, 0:HCOL],
                                    PAr[ii][:, u * D + hh * HCOL: u * D + hh * HCOL + HCOL],
                                    op=ALU.add),
                                    waits=[te, usig[si]] if (u == 0 and hh == 0) else [te],
                                    sig=True)
                                last_ev = S1.now("vector")
                                S1.bank_set(bk, last_ev)
                                if hh == NH - 1:
                                    if si == "B" and u == NSUB - 1:
                                        uev.append((u, last_ev, True))
                                    else:
                                        del ring_trace[:]
                                        sl_, sg_ = ln_sub(S1, PBr[ii], u, [last_ev],
                                                          ring_slot, lnpar)
                                        uev.append((sl_, sg_, False, ring_trace[0]))
                        ev1[si + "xs"] = last_ev
                        sl3[si] = (uev, ii)
                        if pi + 1 < len(pairs):
                            nb_, npr_ = pairs[pi + 1]
                            if si == "A":
                                ni = nside % 3
                                nt0 = npr_ * SIDE
                            else:
                                ni = (nside + 1) % 3
                                nt0 = npr_ * SIDE + half
                                pa_free[iA] = ev1["Axs"]
                            nlx = load_x(PAr[ni], nb_, nt0, [pa_free[ni]])
                            stats_n = []
                            for u in range(NSUB):
                                pp, ve = ln_stats(S1, PAr[ni], u,
                                                  [nlx] if u == 0 else [], lnpar)
                                stats_n.append((pp, ve, ln_rln(S1, u, pp, ve)))
                            carry_n[si] = (ni, stats_n)
                    slotsA = [x[0] for x in sl3["A"][0]]
                    sigsA = [x[1] for x in sl3["A"][0]]
                    idxA = [x[3] for x in sl3["A"][0]]
                    cg3A = run_T(slotsA, sigsA, None, F1[jA][0], F1[jA][1],
                                 ring_idx=idxA)
                    slotsB = []
                    sigsB = []
                    idxB = []
                    for x in sl3["B"][0]:
                        if x[2]:
                            del ring_trace[:]
                            sl_, sg_ = ln_sub(S1, PBr[sl3["B"][1]], x[0], [x[1]],
                                              ring_slot, lnpar)
                            slotsB.append(sl_)
                            sigsB.append(sg_)
                            idxB.append(ring_trace[0])
                        else:
                            slotsB.append(x[0])
                            sigsB.append(x[1])
                            idxB.append(x[3])
                    cg3B = run_T(slotsB, sigsB, None, F1[jB][0], F1[jB][1],
                                 ring_idx=idxB)
                    kv_stage(F1[jA][0], F1[jA][1], b, t0A, cg3A)
                    pb_free[iA] = [S1.now("vector"), ev1["Asp"]]
                    pa_free[iA] = ev1["Axs"]
                    pa_free[iB] = ev1["Bxs"]
                    carry = None
                    if pi + 1 < len(pairs):
                        cc = []
                        for si2 in ("A", "B"):
                            ni, stats_n = carry_n[si2]
                            slots = []
                            sigs = []
                            idxs = []
                            for u in range(NSUB):
                                pp, ve, rs = stats_n[u]
                                del ring_trace[:]
                                sl_, sg_ = ln_apply(S1, PAr[ni], u, pp, ve,
                                                    ring_slot, lnpar,
                                                    on_vector=True, rln_sig=rs)
                                slots.append(sl_)
                                sigs.append(sg_)
                                idxs.append(ring_trace[0])
                            cc.append((slots, sigs, idxs))
                        ncgA = run_T(cc[0][0], cc[0][1], None,
                                     F1[jA][0], F1[jA][1], ring_idx=cc[0][2])
                        carry = (cc, ncgA)
                    kv_stage(F1[jB][0], F1[jB][1], b, t0B, cg3B)
                    pb_free[iB] = [S1.now("vector"), ev1["Bsp"]]
                    if carry is not None:
                        cc, ncgA = carry
                        ncgB = run_T(cc[1][0], cc[1][1], None,
                                     F1[jB][0], F1[jB][1], ring_idx=cc[1][2])
                        carry = (ncgA, ncgB)

            S1.op("scalar", lambda g: g.nop(), waits=[S1.now("scalar")])
            emit(blk, S1, semsets[1])
        ctx1.close()

        ctx2 = ExitStack()
        e2 = ctx2.enter_context
        q2 = [e2(nc.sbuf_tensor(f"q{i}", [128, D * NSUB], F32)) for i in range(3)]
        x2b2 = [e2(nc.sbuf_tensor(f"x2b{i}", [128, D * NSUB], F32)) for i in range(2)]
        Cn2 = e2(nc.sbuf_tensor("cn2", [128, CNS * D], F32))
        G1 = [(e2(nc.sbuf_tensor(f"g1h{i}", [128, KC * SIDE], FP16)),
               e2(nc.sbuf_tensor(f"g1l{i}", [128, KC * SIDE], FP16))) for i in range(2)]
        xv22 = [e2(nc.sbuf_tensor(f"xv2{i}", [128, SIDE], FP16)) for i in range(2)]
        stHL = e2(nc.sbuf_tensor("sthl", [128, B * T], FP16))
        states = e2(nc.sbuf_tensor("states", [128, T], F32))
        blk = e2(nc.Block())
        if True:
            S3 = Sched()
            xv_mod[0] = 1
            tpst3 = [0]
            dnst3 = [0, (0, 1, 7)]
            cn_ring3 = [0]
            cn_sig3 = [None] * CNS
            lnpar3 = LnPar()

            for xs_ in xv22:
                S3.op("vector", lambda g, xs_=xs_: g.memset(xs_[:, :], 0.0), sig=True)
                zv = S3.now("vector")
                S3.op("vector", lambda g, xs_=xs_: g.memset(xs_[R:R + 1, :], 1.0),
                      waits=[zv], sig=True)
            ve_ones3 = S3.now("vector")

            def ring_slot3(u):
                i = cn_ring3[0] % CNS
                cn_ring3[0] += 1
                return Cn2[:, i * D:(i + 1) * D], cn_sig3[i]

            def run_ln3(src, waits, waits_u=None):
                i0 = cn_ring3[0]
                slots, sigs = ln_chain(S3, src, waits, ring_slot3, lnpar3,
                                       waits_u=waits_u)
                return slots, sigs, i0

            def run_T3(slots, sigs, i0, dsth, dstl):
                cg = transpose_stage(S3, slots, sigs, dsth, dstl, tpst3)
                te_sig = S3.now("tensor")
                for u in range(NSUB):
                    cn_sig3[(i0 + u) % CNS] = te_sig
                return cg

            st_ready = {}
            wbc = wdec[:, 0:1].broadcast_to((128, T))

            def scan_seq(bq):
                lo, hi = bq * S, (bq + 1) * S
                S3.op("vector", lambda g, lo=lo, hi=hi: g.tensor_tensor_scan(
                    states[lo:hi, :], wbc[lo:hi, :], wkvbuf[lo:hi, :], 0.0,
                    op0=ALU.mult, op1=ALU.add), sig=True)
                ve_scan = S3.now("vector")
                if bq == 0:
                    S3.op("scalar", lambda g: g.activation(
                        stHL[0:S, 0:T], states[0:S, :], ACT.Identity),
                        waits=[ve_scan], sig=True)
                    sc_hi = S3.now("scalar")
                    S3.op("vector", lambda g: g.tensor_tensor(
                        wkvbuf[0:S, 0:T], states[0:S, :], stHL[0:S, 0:T],
                        op=ALU.subtract), waits=[sc_hi], sig=True)
                    ve_lo = S3.now("vector")
                    S3.op("gpsimd", lambda g: g.dma_start(
                        out=stHL[S:2 * S, 0:T], in_=wkvbuf[0:S, 0:T]),
                        waits=[ve_lo], sig=True, dma=True)
                else:
                    S3.op("scalar", lambda g: g.activation(
                        stHL[S:2 * S, T:2 * T], states[S:2 * S, :], ACT.Identity),
                        waits=[ve_scan], sig=True)
                    sc_hi = S3.now("scalar")
                    S3.op("gpsimd", lambda g: g.dma_start(
                        out=stHL[0:S, T:2 * T], in_=stHL[S:2 * S, T:2 * T]),
                        waits=[sc_hi], sig=True, dma=True)
                    gp_hi = S3.now("gpsimd")
                    S3.op("vector", lambda g: g.tensor_tensor(
                        stHL[S:2 * S, T:2 * T], states[S:2 * S, :],
                        stHL[S:2 * S, T:2 * T],
                        op=ALU.subtract), waits=[gp_hi], sig=True)
                st_ready[bq] = [S3.now("gpsimd"), S3.now("vector"), S3.now("scalar")]

            scan_seq(0)

            def load_x1(dst, b, t0, waits):
                ap_sx = bass.AP(x1_hbm.tensor, b * T * D + t0 * D,
                                [[D, 128], [128 * D, NSUB], [1, D]])
                S3.op("sync", lambda g, dst=dst, ap_sx=ap_sx: g.dma_start(
                    out=dst[:, :].rearrange("p (u d) -> p u d", u=NSUB), in_=ap_sx),
                    waits=waits, sig=True, dma=True)
                return S3.now("sync")

            free_q = [None, None, None]

            def stage1_op(i, b, t0):
                par = i % 2
                qar = i % 3
                lx = load_x1(q2[qar], b, t0, [free_q[qar]])
                uev = []
                for u in range(NSUB):
                    for hh in range(NH):
                        bk = dnst3[1][dnst3[0] % len(dnst3[1])]
                        dnst3[0] += 1
                        waits = [S3.bank_wait(bk)]
                        if u == 0 and hh == 0:
                            waits += st_ready[b]
                        for p, rh in enumerate((wo_h, wo_l)):
                            S3.op("tensor", lambda g, u=u, hh=hh, bk=bk, rh=rh, p=p, b=b, t0=t0: g.matmul(
                                pb[bk][:, 0:HCOL],
                                stHL[:, b * T + t0 + u * 128: b * T + t0 + u * 128 + 128],
                                rh[:, hh * HCOL:(hh + 1) * HCOL],
                                start=(p == 0), stop=(p == 1)),
                                waits=waits if p == 0 else (), sig=(p == 1))
                        te = S3.now("tensor")
                        S3.op("vector", lambda g, u=u, hh=hh, bk=bk, par=par, qar=qar: g.tensor_tensor(
                            x2b2[par][:, u * D + hh * HCOL: u * D + hh * HCOL + HCOL],
                            pb[bk][:, 0:HCOL],
                            q2[qar][:, u * D + hh * HCOL: u * D + hh * HCOL + HCOL],
                            op=ALU.add),
                            waits=[te, lx] if (u == 0 and hh == 0) else [te], sig=True)
                        last = S3.now("vector")
                        S3.bank_set(bk, last)
                        if hh == NH - 1:
                            uev.append(last)
                return par, qar, uev

            def stage1_ln(par, uev):
                i0 = cn_ring3[0]
                uln = [ln_sub(S3, x2b2[par], u, [uev[u]], ring_slot3, lnpar3)
                       for u in range(NSUB)]
                return [x[0] for x in uln], [x[1] for x in uln], i0

            xv_lo_free3 = [None]

            def stage2_T(i, par, slots, sigs, i0):
                return run_T3(slots, sigs, i0, G1[par][0], G1[par][1])

            def stage2(i, b, t0, par, qar, cg):
                xv = xvr_stage(S3, G1[par][0], G1[par][1], xv22[par],
                               [ve_ones3] if i < 2 else [], xv_lo_free3[0], cg)
                xv_lo_free3[0] = xv

                def evac_x3(g, u, hh, ps, par=par, qar=qar):
                    return g.scalar_tensor_tensor(
                        q2[qar][:, u * D + hh * HCOL: u * D + hh * HCOL + HCOL],
                        ps, 0.5,
                        x2b2[par][:, u * D + hh * HCOL: u * D + hh * HCOL + HCOL],
                        op0=ALU.mult, op1=ALU.add)
                evs = dense_stage(S3, G1[par][0], G1[par][1], wd0_h, wd0_l,
                                  [], evac_x3, dnst3, xvs=xv22[par],
                                  cg=cg, xv_sig=xv)
                last = None
                for u in range(NSUB):
                    ap_d = bass.AP(out_hbm.tensor, b * T * D + (t0 + u * 128) * D,
                                   [[D, 128], [1, D]])
                    S3.op("scalar", lambda g, qar=qar, ap_d=ap_d, u=u: g.dma_start(
                        out=ap_d, in_=q2[qar][:, u * D:(u + 1) * D]),
                        waits=[evs[u * NH + NH - 1]], sig=True, dma=True)
                    last = S3.now("scalar")
                free_q[qar] = last
                return last

            sides = [(b, t0) for b in range(B) for t0 in range(0, T, SIDE)]
            pend = None
            out_last = None
            for i, (b, t0) in enumerate(sides):
                if i == 1 and B > 1:
                    scan_seq(1)
                par, qar, uev = stage1_op(i, b, t0)
                if pend is not None:
                    (pi, pbb, pt0, ppar, pqar, pslots, psigs, pi0) = pend
                    pcg = stage2_T(pi, ppar, pslots, psigs, pi0)
                slots, sigs, i0 = stage1_ln(par, uev)
                if pend is not None:
                    out_last = stage2(pi, pbb, pt0, ppar, pqar, pcg)
                pend = (i, b, t0, par, qar, slots, sigs, i0)
            (pi, pbb, pt0, ppar, pqar, pslots, psigs, pi0) = pend
            pcg = stage2_T(pi, ppar, pslots, psigs, pi0)
            out_last = stage2(pi, pbb, pt0, ppar, pqar, pcg)

            S3.op("scalar", lambda g: g.nop(), waits=[out_last])
            emit(blk, S3, semsets[2])
        ctx2.close()

    return nc


def kernel(L0_attn_x, L0_attn_time_decay, L0_attn_time_first, L0_attn_W_key,
           L0_attn_W_value, L0_attn_W_output, L0_attn_W_shift, L0_attn_shift_gate,
           L0_attn_ln_w, L0_attn_ln_b, L0_ffn_W, L0_ffn_b, L0_ffn_V_r, L0_ffn_C):
    N = 8
    x = np.ascontiguousarray(np.asarray(L0_attn_x, dtype=np.float32))
    wt = np.ascontiguousarray(np.asarray(L0_ffn_W, dtype=np.float32).T)
    wst = np.ascontiguousarray(np.asarray(L0_attn_W_shift, dtype=np.float32).T)
    wkvt = np.ascontiguousarray(np.concatenate(
        [np.asarray(L0_attn_W_key, np.float32).T,
         np.asarray(L0_attn_W_value, np.float32).T], axis=1))
    vr = np.ascontiguousarray(np.asarray(L0_ffn_V_r, np.float32))
    ctb = np.ascontiguousarray(np.concatenate(
        [np.asarray(L0_ffn_C, np.float32).T,
         np.asarray(L0_ffn_b, np.float32)[None, :]], axis=0))
    wot = np.ascontiguousarray(np.asarray(L0_attn_W_output, np.float32).T)
    td = np.ascontiguousarray(np.asarray(L0_attn_time_decay, np.float32)[:, None])
    tf = np.ascontiguousarray(np.asarray(L0_attn_time_first, np.float32)[:, None])
    sg = np.ascontiguousarray(np.asarray(L0_attn_shift_gate, np.float32)[None, :])
    lnw = np.ascontiguousarray(np.asarray(L0_attn_ln_w, np.float32))
    lnb = np.ascontiguousarray(np.asarray(L0_attn_ln_b, np.float32))
    ident = np.eye(128, dtype=np.float32)

    nc = build_program()
    in_maps = []
    for c in range(N):
        in_maps.append(dict(x=np.ascontiguousarray(x[2 * c:2 * c + 2]),
                            wt=wt, wst=wst, wkvt=wkvt, vr=vr, ctb=ctb, wot=wot,
                            td=td, tf=tf, sg=sg, lnw=lnw, lnb=lnb, ident=ident))
    res = run_bass_kernel_spmd(nc, in_maps, core_ids=list(range(N)))
    return np.concatenate([np.asarray(r["out"], dtype=np.float32)
                           for r in res.results], axis=0)


# version 35
# speedup vs baseline: 1.0416x; 1.0125x over previous
import numpy as np
import concourse.bass as bass
import concourse.mybir as mybir
from concourse.bass_utils import run_bass_kernel_spmd

F32 = mybir.dt.float32
FP16 = mybir.dt.float16
ALU = mybir.AluOpType
ACT = mybir.ActivationFunctionType

ENGINES = ("sync", "tensor", "vector", "scalar", "gpsimd")
SEMS = ENGINES + ("gpsimdc",)
EPS = 1e-5
R = 32


class Sched:

    def __init__(self):
        self.ops = {e: [] for e in ENGINES}
        self.cnt = {e: 0 for e in SEMS}
        self.bank_hist = {}

    def op(self, eng, fn, waits=(), sig=False, dma=False, sem=None):
        ws = [w for w in waits if w is not None]
        for (e, v) in ws:
            assert v <= self.cnt[e], f"wait on future {e}>={v} (now {self.cnt[e]})"
        amt = 16 if dma else 1
        sname = sem or eng
        if sig:
            self.cnt[sname] += amt
        self.ops[eng].append((fn, tuple(ws), sig and sname, amt))
        return (sname, self.cnt[sname]) if sig else None

    def now(self, eng):
        return (eng, self.cnt[eng])

    def bank_wait(self, bk):
        return self.bank_hist.get(bk)

    def bank_set(self, bk, sig):
        self.bank_hist[bk] = sig


class LnPar(dict):

    def __init__(self):
        super().__init__()
        self[0] = 0
        for i in range(1, 7):
            self[i] = None
        self["pend"] = [False] * 6


def emit(blk, sched, sems):
    for ename in ENGINES:
        ops = sched.ops[ename]
        if not ops:
            continue

        def body(eng, ops=ops):
            for fn, waits, sig, amt in ops:
                for (we, wv) in waits:
                    if wv > 0:
                        eng.wait_ge(sems[we], wv)
                instr = fn(eng)
                if sig:
                    if instr is None:
                        instr = eng.nop()
                    instr.then_inc(sems[sig], amt)
        getattr(blk, ename)(body)


def build_program(B=2, T=4096, D=1024, S=64, SIDE=256):
    assert T % (2 * SIDE) == 0 and D % 128 == 0 and SIDE % 128 == 0
    KC = D // 128
    NSUB = SIDE // 128
    NH = max(1, D // 512)
    HCOL = D // NH
    NST = max(1, D // 512)
    STW = min(D, 512)
    NPAIR = T // (2 * SIDE)
    half = T // 2
    BS = B * S
    TGRP = 4
    NTG = (KC + TGRP - 1) // TGRP
    CNS = 2 * NSUB

    nc = bass.Bass("TRN2", target_bir_lowering=False,
                   detect_race_conditions=False, dynamic_dma_scratch_size=4096)

    x_in = nc.dram_tensor("x", [B, T, D], F32, kind="ExternalInput").ap()
    wt_in = nc.dram_tensor("wt", [D, D], F32, kind="ExternalInput").ap()
    wst_in = nc.dram_tensor("wst", [D, D], F32, kind="ExternalInput").ap()
    wkvt_in = nc.dram_tensor("wkvt", [D, 2 * S], F32, kind="ExternalInput").ap()
    vr_in = nc.dram_tensor("vr", [D, R], F32, kind="ExternalInput").ap()
    ctb_in = nc.dram_tensor("ctb", [R + 1, D], F32, kind="ExternalInput").ap()
    wot_in = nc.dram_tensor("wot", [S, D], F32, kind="ExternalInput").ap()
    td_in = nc.dram_tensor("td", [S, 1], F32, kind="ExternalInput").ap()
    tf_in = nc.dram_tensor("tf", [S, 1], F32, kind="ExternalInput").ap()
    sg_in = nc.dram_tensor("sg", [1, D], F32, kind="ExternalInput").ap()
    lnw_in = nc.dram_tensor("lnw", [D], F32, kind="ExternalInput").ap()
    lnb_in = nc.dram_tensor("lnb", [D], F32, kind="ExternalInput").ap()
    id_in = nc.dram_tensor("ident", [128, 128], F32, kind="ExternalInput").ap()
    out_hbm = nc.dram_tensor("out", [B, T, D], F32, kind="ExternalOutput").ap()
    x1_hbm = nc.dram_tensor("x1s", [B, T, D], F32).ap()

    from contextlib import ExitStack
    with ExitStack() as ctx:
        e = ctx.enter_context
        wd0_h = e(nc.sbuf_tensor("wd0_h", [128, KC * D], FP16))
        wd0_l = e(nc.sbuf_tensor("wd0_l", [128, KC * D], FP16))
        wd1_h = e(nc.sbuf_tensor("wd1_h", [128, KC * D], FP16))
        wd1_l = e(nc.sbuf_tensor("wd1_l", [128, KC * D], FP16))
        wkv_h = e(nc.sbuf_tensor("wkv_h", [128, KC * 2 * S], FP16))
        wkv_l = e(nc.sbuf_tensor("wkv_l", [128, KC * 2 * S], FP16))
        vr_h = e(nc.sbuf_tensor("vr_h", [128, KC * R], FP16))
        vr_l = e(nc.sbuf_tensor("vr_l", [128, KC * R], FP16))
        cts_h = e(nc.sbuf_tensor("cts_h", [96, D], FP16))
        cts_l = e(nc.sbuf_tensor("cts_l", [96, D], FP16))
        wo_h = e(nc.sbuf_tensor("wo_h", [128, D], FP16))
        wo_l = e(nc.sbuf_tensor("wo_l", [128, D], FP16))
        oneg = e(nc.sbuf_tensor("oneg", [128, D], F32))
        ident = e(nc.sbuf_tensor("ident_s", [128, 128], F32))
        wdec = e(nc.sbuf_tensor("wdec", [128, 1], F32))
        negef = e(nc.sbuf_tensor("negef", [128, 1], F32))
        negkb = e(nc.sbuf_tensor("negkb", [128, 1], F32))
        vbias = e(nc.sbuf_tensor("vbias", [128, 1], F32))
        epsv = e(nc.sbuf_tensor("epsv", [128, 1], F32))
        wkvbuf = e(nc.sbuf_tensor("wkvbuf", [128, T], F32))
        st6 = e(nc.sbuf_tensor("st6", [128, 2 * 6 * NST], F32))
        st2 = e(nc.sbuf_tensor("st2", [128, 6 * 2 * NSUB], F32))
        rln = e(nc.sbuf_tensor("rln", [128, 6 * NSUB], F32))
        negmr = e(nc.sbuf_tensor("negmr", [128, 6 * NSUB], F32))
        sc512 = e(nc.sbuf_tensor("sc512", [128, 2 * HCOL], F32))
        eb = e(nc.sbuf_tensor("eb", [128, 2 * SIDE], F32))
        sb = e(nc.sbuf_tensor("sb", [128, 2 * SIDE], F32))
        xvlo = e(nc.sbuf_tensor("xvlo", [R, SIDE], FP16))
        pb = [e(nc.psum_tensor(f"pb{i}", [128, 512], F32)) for i in range(8)]

        semsets = [{n: e(nc.semaphore(f"s{bi}_{n}")) for n in SEMS} for bi in range(3)]

        ctxc = ExitStack()
        blk_clr = ctxc.enter_context(nc.Block())

        def _clear(eng):
            for ss_ in semsets:
                for hh_ in ss_.values():
                    eng.sem_clear(hh_)
        blk_clr.sync(_clear)
        ctxc.close()

        LNRING = 6

        def ln_stats(S_, src, u, waits, lnpar):
            pp = lnpar[0] % LNRING
            assert not lnpar["pend"][pp], "LN stats ring slot still pending apply"
            lnpar["pend"][pp] = True
            st2c = 2 * NSUB * pp + 2 * u
            lnpar[0] += 1
            w_u = list(waits) + [lnpar[1 + pp]]
            for hh in range(NST):
                S_.op("vector", lambda g, u=u, hh=hh, pp=pp: g.bn_stats(
                    st6[:, 6 * NST * (pp % 2) + 6 * hh: 6 * NST * (pp % 2) + 6 * hh + 6],
                    src[:, u * D + hh * STW: u * D + hh * STW + STW]),
                    waits=w_u if hh == 0 else (),
                    sig=(hh == NST - 1))
            fence = S_.now("vector")
            S_.op("vector", lambda g, st2c=st2c: g.bn_aggr(
                st2[:, st2c:st2c + 2],
                st6[:, 6 * NST * (pp % 2): 6 * NST * (pp % 2) + 6 * NST]),
                waits=[fence], sig=True)
            return pp, S_.now("vector")

        def ln_rln(S_, u, pp, ve):
            rlc = NSUB * pp + u
            st2c = 2 * NSUB * pp + 2 * u
            var_ap = st2[:, st2c + 1:st2c + 2]
            S_.op("scalar", lambda g, rlc=rlc, var_ap=var_ap: g.activation(
                rln[:, rlc:rlc + 1], var_ap, ACT.Ln, bias=epsv[:, :]),
                waits=[ve], sig=True)
            S_.op("scalar", lambda g, rlc=rlc: g.activation(
                rln[:, rlc:rlc + 1], rln[:, rlc:rlc + 1], ACT.Exp, scale=-0.5),
                sig=True)
            return S_.now("scalar")

        def ln_apply(S_, src, u, pp, ve, slot_of, lnpar, on_vector=True,
                     rln_sig=None, eng="vector"):
            st2c = 2 * NSUB * pp + 2 * u
            rlc = NSUB * pp + u
            m_ap = st2[:, st2c:st2c + 1]
            if rln_sig is None:
                rln_sig = ln_rln(S_, u, pp, ve)
            slot, slot_free = slot_of(u)
            sem = "gpsimdc" if eng == "gpsimd" else eng
            S_.op(eng, lambda g, u=u, slot=slot, rlc=rlc, m_ap=m_ap: g.tensor_scalar(
                slot, src[:, u * D:(u + 1) * D], m_ap, rln[:, rlc:rlc + 1],
                op0=ALU.subtract, op1=ALU.mult),
                waits=[rln_sig, slot_free], sig=True, sem=sem)
            sig = S_.now(sem)
            lnpar[1 + pp] = sig
            lnpar["pend"][pp] = False
            return slot, sig

        def ln_sub(S_, src, u, waits, slot_of, lnpar):
            pp, ve = ln_stats(S_, src, u, waits, lnpar)
            return ln_apply(S_, src, u, pp, ve, slot_of, lnpar)

        def ln_chain(S_, src, waits, slot_of, lnpar, waits_u=None):
            sigs = []
            slots = []
            for u in range(NSUB):
                w_u = list(waits) if u == 0 else []
                if waits_u is not None:
                    w_u = w_u + [waits_u[u]]
                slot, sig = ln_sub(S_, src, u, w_u, slot_of, lnpar)
                slots.append(slot)
                sigs.append(sig)
            return slots, sigs

        def transpose_stage(S_, slots, apply_sigs, dsth, dstl, tpst):
            cg = [None] * NTG
            for u in range(NSUB):
                for h4 in range(NTG):
                    nck = min(TGRP, KC - h4 * TGRP)
                    bk = 2 + (tpst[0] % 4)
                    tpst[0] += 1
                    waits = [S_.bank_wait(bk), apply_sigs[u]]
                    for cc in range(nck):
                        c = h4 * TGRP + cc
                        S_.op("tensor", lambda g, u=u, c=c, cc=cc, bk=bk: g.matmul(
                            pb[bk][:, cc * 128:(cc + 1) * 128],
                            slots[u][:, c * 128:(c + 1) * 128],
                            ident[:, :], is_transpose=True,
                            start=(cc == 0), stop=(cc == nck - 1)),
                            waits=waits if cc == 0 else (), sig=(cc == nck - 1))
                    te = S_.now("tensor")
                    dh = bass.AP(dsth[:, :].tensor, h4 * TGRP * SIDE + u * 128,
                                 [[KC * SIDE, 128], [SIDE, nck], [1, 128]])
                    dl = bass.AP(dstl[:, :].tensor, h4 * TGRP * SIDE + u * 128,
                                 [[KC * SIDE, 128], [SIDE, nck], [1, 128]])
                    S_.op("scalar", lambda g, bk=bk, dh=dh, nck=nck: g.activation(
                        dh, pb[bk][:, 0:nck * 128], ACT.Identity),
                        waits=[te], sig=True)
                    sc = S_.now("scalar")
                    S_.op("vector", lambda g, bk=bk, dh=dh, dl=dl, nck=nck: g.tensor_tensor(
                        dl, pb[bk][:, 0:nck * 128], dh, op=ALU.subtract),
                        waits=[te, sc], sig=True)
                    S_.bank_set(bk, S_.now("vector"))
                    cg[h4] = S_.now("vector")
            return cg

        xv_bank = [0]
        xv_mod = [2]

        def xvr_stage(S_, xFh, xFl, xvs, ready, lo_free, cg):
            bq = 6 + (xv_bank[0] % xv_mod[0])
            xv_bank[0] += 1
            for p, (lh, rh) in enumerate(((vr_h, xFh), (vr_h, xFl), (vr_l, xFh))):
                for c in range(KC):
                    waits = []
                    if p == 0 and c == 0:
                        waits += [S_.bank_wait(bq)] + list(ready)
                    if p == 0 and c % TGRP == 0:
                        waits.append(cg[c // TGRP])
                    S_.op("tensor", lambda g, p=p, c=c, lh=lh, rh=rh, bq=bq: g.matmul(
                        pb[bq][0:R, 0:SIDE], lh[:, c * R:(c + 1) * R],
                        rh[:, c * SIDE:(c + 1) * SIDE],
                        start=(p == 0 and c == 0), stop=(p == 2 and c == KC - 1)),
                        waits=waits, sig=(p == 2 and c == KC - 1))
            te = S_.now("tensor")
            S_.op("scalar", lambda g, xvs=xvs, bq=bq: g.activation(
                xvs[0:R, :], pb[bq][0:R, 0:SIDE], ACT.Identity), waits=[te], sig=True)
            sc = S_.now("scalar")
            S_.op("vector", lambda g, xvs=xvs, bq=bq: g.tensor_tensor(
                xvlo[0:R, 0:SIDE], pb[bq][0:R, 0:SIDE], xvs[0:R, :], op=ALU.subtract),
                waits=[te, sc, lo_free], sig=True)
            ve = S_.now("vector")
            S_.bank_set(bq, ve)
            S_.op("sync", lambda g, xvs=xvs: g.dma_start(
                out=xvs[R + 1:2 * R + 1, :], in_=xvlo[0:R, 0:SIDE]),
                waits=[ve], sig=True, dma=True)
            return S_.now("sync")

        def dense_stage(S_, xFh, xFl, wh, wl, ready, evac, dnst, xvs=None,
                        cg=None, xv_sig=None, post_u=None):
            evs = []
            for u in range(NSUB):
                for hh in range(NH):
                    bk = dnst[1][dnst[0] % len(dnst[1])]
                    dnst[0] += 1
                    first = (u == 0 and hh == 0)
                    for p, (lh, rh) in enumerate(((xFh, wh), (xFh, wl), (xFl, wh))):
                        for c in range(KC):
                            waits = []
                            if p == 0 and c == 0:
                                waits.append(S_.bank_wait(bk))
                                if first:
                                    waits += list(ready)
                            if first and p == 0 and cg is not None and c % TGRP == 0:
                                waits.append(cg[c // TGRP])
                            S_.op("tensor", lambda g, u=u, hh=hh, bk=bk, c=c, lh=lh, rh=rh, p=p: g.matmul(
                                pb[bk][:, 0:HCOL],
                                lh[:, c * SIDE + u * 128: c * SIDE + u * 128 + 128],
                                rh[:, c * D + hh * HCOL: c * D + hh * HCOL + HCOL],
                                start=(p == 0 and c == 0),
                                stop=(xvs is None and p == 2 and c == KC - 1)),
                                waits=waits, sig=(xvs is None and p == 2 and c == KC - 1))
                    if xvs is not None:
                        for p, rh in enumerate((cts_h, cts_l)):
                            S_.op("tensor", lambda g, u=u, hh=hh, bk=bk, rh=rh, p=p, xvs=xvs: g.matmul(
                                pb[bk][:, 0:HCOL],
                                xvs[0:96, u * 128:(u + 1) * 128],
                                rh[0:96, hh * HCOL:(hh + 1) * HCOL],
                                start=False, stop=(p == 1)),
                                waits=[xv_sig] if (first and p == 0) else (),
                                sig=(p == 1))
                    te = S_.now("tensor")
                    S_.op("vector", lambda g, u=u, hh=hh, bk=bk: evac(
                        g, u, hh, pb[bk][:, 0:HCOL]), waits=[te], sig=True)
                    evs.append(S_.now("vector"))
                    S_.bank_set(bk, evs[-1])
                    if post_u is not None and hh == NH - 1:
                        post_u(u, evs[-1])
            return evs

        ctx0 = ExitStack()
        e0 = ctx0.enter_context
        stg = e0(nc.sbuf_tensor("stg", [128, 2048], F32))
        stg2 = e0(nc.sbuf_tensor("stg2", [128, 2048], F32))
        grow = e0(nc.sbuf_tensor("grow", [1, 3 * D], F32))
        lnwf = e0(nc.sbuf_tensor("lnwf", [128, KC], F32))
        lnbf = e0(nc.sbuf_tensor("lnbf", [128, KC], F32))
        onecol = e0(nc.sbuf_tensor("onecol", [1, 128], F32))
        kvb = e0(nc.sbuf_tensor("kvb", [128, 1], F32))
        blk = e0(nc.Block())
        if True:
            S0 = Sched()
            S0.op("vector", lambda g: g.memset(epsv[:, :], EPS), sig=True)

            def ld(dst, src, waits=(), slow=False):
                return S0.op("sync", lambda g, d=dst, s=src, slow=slow: g.dma_start(
                    out=d, in_=s, allow_slow_non_contiguous=slow),
                    waits=waits, sig=True, dma=True)

            ld(ident[:, :], id_in[:, :])
            w_td0 = ld(wdec[0:S, :], td_in[:, :])
            w_td1 = ld(wdec[S:2 * S, :], td_in[:, :])
            w_tf0 = ld(negef[0:S, :], tf_in[:, :])
            w_tf1 = ld(negef[S:2 * S, :], tf_in[:, :])
            w_sg = ld(grow[0:1, 0:D], sg_in[:, :])
            lnw_src = bass.AP(lnw_in.tensor, 0, [[1, 128], [128, KC]])
            lnb_src = bass.AP(lnb_in.tensor, 0, [[1, 128], [128, KC]])
            w_lnw = ld(lnwf[:, :], lnw_src, slow=True)
            w_lnb = ld(lnbf[:, :], lnb_src, slow=True)
            w_kvt = ld(stg2[:, 0:KC * 2 * S].rearrange("p (c n) -> p c n", c=KC),
                       bass.AP(wkvt_in.tensor, 0, [[2 * S, 128], [128 * 2 * S, KC], [1, 2 * S]]))
            w_vr = ld(stg[:, 0:KC * R].rearrange("p (c n) -> p c n", c=KC),
                      bass.AP(vr_in.tensor, 0, [[R, 128], [128 * R, KC], [1, R]]))
            w_ctb = ld(stg[0:R + 1, 1024:1024 + D], ctb_in[:, :])
            w_wot = ld(stg2[0:S, 1024:1024 + D], wot_in[:, :])
            w_wot2 = ld(stg2[S:2 * S, 1024:1024 + D], wot_in[:, :])

            S0.op("scalar", lambda g: g.activation(wdec[:, :], wdec[:, :], ACT.Exp),
                  waits=[w_td0, w_td1], sig=True)
            S0.op("scalar", lambda g: g.activation(negef[:, :], negef[:, :], ACT.Exp),
                  waits=[w_tf0, w_tf1], sig=True)
            sc_e = S0.now("scalar")
            ve_ef = S0.op("vector", lambda g: g.tensor_scalar_mul(negef[:, :], negef[:, :], -1.0),
                          waits=[sc_e], sig=True)

            S0.op("scalar", lambda g: g.activation(grow[0:1, D:2 * D], grow[0:1, 0:D], ACT.Exp),
                  waits=[w_sg], sig=True)
            S0.op("scalar", lambda g: g.activation(grow[0:1, 2 * D:3 * D], grow[0:1, 0:D],
                                                   ACT.Exp, scale=-1.0), sig=True)
            sc_g = S0.now("scalar")
            S0.op("vector", lambda g: g.tensor_scalar_add(grow[0:1, D:3 * D],
                                                          grow[0:1, D:3 * D], 1.0),
                  waits=[sc_g], sig=True)
            fence_g = S0.now("vector")
            ve_g = S0.op("vector", lambda g: g.reciprocal(grow[0:1, D:3 * D],
                                                         grow[0:1, D:3 * D]),
                         waits=[fence_g], sig=True)
            ve_one = S0.op("vector", lambda g: g.memset(onecol[:, :], 1.0), sig=True)

            for row, dst in ((1, oneg),):
                for hc in range(max(1, D // 512)):
                    w = min(512, D)
                    ssl = slice(row * D + hc * 512, row * D + hc * 512 + w)
                    bk = hc % 2 + (0 if row == 1 else 2)
                    S0.op("tensor", lambda g, ssl=ssl, bk=bk, w=w: g.matmul(
                        pb[bk][:, 0:w], onecol[:, :], grow[0:1, ssl],
                        start=True, stop=True), waits=[ve_g, ve_one], sig=True)
                    te = S0.now("tensor")
                    S0.op("vector", lambda g, hc=hc, w=w, bk=bk, dst=dst: g.tensor_copy(
                        dst[:, hc * 512:hc * 512 + w], pb[bk][:, 0:w]),
                        waits=[te], sig=True)
            ve_gbc = S0.now("vector")

            for c in range(KC):
                S0.op("tensor", lambda g, c=c: g.matmul(
                    pb[4][0:2 * S, 0:1], stg2[:, c * 2 * S:(c + 1) * 2 * S],
                    lnbf[:, c:c + 1], start=(c == 0), stop=(c == KC - 1)),
                    waits=[w_kvt, w_lnb] if c == 0 else (), sig=(c == KC - 1))
            te_kvb = S0.now("tensor")
            ve_kvb = S0.op("vector", lambda g: g.tensor_copy(kvb[0:2 * S, :], pb[4][0:2 * S, 0:1]),
                           waits=[te_kvb], sig=True)
            ve_nkb = S0.op("vector", lambda g: g.tensor_scalar_mul(negkb[0:S, :], kvb[0:S, :], -1.0),
                           waits=[ve_kvb], sig=True)
            g_b1 = S0.op("gpsimd", lambda g: g.dma_start(out=negkb[S:2 * S, :], in_=negkb[0:S, :]),
                         waits=[ve_nkb], sig=True, dma=True)
            g_b2 = S0.op("gpsimd", lambda g: g.dma_start(out=vbias[0:S, :], in_=kvb[S:2 * S, :]),
                         waits=[ve_kvb], sig=True, dma=True)
            g_b3 = S0.op("gpsimd", lambda g: g.dma_start(out=vbias[S:2 * S, :], in_=kvb[S:2 * S, :]),
                         waits=[ve_kvb], sig=True, dma=True)

            for c in range(KC):
                S0.op("vector", lambda g, c=c: g.tensor_scalar_mul(
                    stg2[:, c * 2 * S:(c + 1) * 2 * S], stg2[:, c * 2 * S:(c + 1) * 2 * S],
                    lnwf[:, c:c + 1]),
                    waits=[w_lnw, te_kvb] if c == 0 else (), sig=(c == KC - 1))
            f1_ = S0.now("vector")
            S0.op("vector", lambda g: g.tensor_copy(wkv_h[:, :], stg2[:, 0:KC * 2 * S]),
                  waits=[f1_], sig=True)
            f2_ = S0.now("vector")
            S0.op("vector", lambda g: g.tensor_tensor(wkv_l[:, :], stg2[:, 0:KC * 2 * S],
                                                      wkv_h[:, :], op=ALU.subtract),
                  waits=[f2_], sig=True)
            S0.op("vector", lambda g: g.tensor_copy(vr_h[:, :], stg[:, 0:KC * R]),
                  waits=[w_vr], sig=True)
            f3_ = S0.now("vector")
            S0.op("vector", lambda g: g.tensor_tensor(vr_l[:, :], stg[:, 0:KC * R],
                                                      vr_h[:, :], op=ALU.subtract),
                  waits=[f3_], sig=True)
            S0.op("vector", lambda g: g.memset(cts_h[:, :], 0.0), sig=True)
            S0.op("vector", lambda g: g.memset(cts_l[:, :], 0.0), sig=True)
            zc = S0.now("vector")
            S0.op("vector", lambda g: g.tensor_copy(cts_h[0:R + 1, :], stg[0:R + 1, 1024:1024 + D]),
                  waits=[w_ctb, zc], sig=True)
            f4_ = S0.now("vector")
            S0.op("vector", lambda g: g.tensor_tensor(cts_l[0:R + 1, :], stg[0:R + 1, 1024:1024 + D],
                                                      cts_h[0:R + 1, :], op=ALU.subtract),
                  waits=[f4_], sig=True)
            g_ct = S0.op("gpsimd", lambda g: g.dma_start(
                out=cts_h[R + 1:2 * R + 2, :], in_=cts_h[0:R + 1, :]),
                waits=[f4_], sig=True, dma=True)
            S0.op("vector", lambda g: g.memset(wo_l[:, :], 0.0), sig=True)
            zw = S0.now("vector")
            S0.op("vector", lambda g: g.tensor_copy(wo_h[:, :], stg2[:, 1024:1024 + D]),
                  waits=[w_wot, w_wot2], sig=True)
            f5_ = S0.now("vector")
            S0.op("vector", lambda g: g.tensor_tensor(wo_l[0:S, :], stg2[0:S, 1024:1024 + D],
                                                      wo_h[0:S, :], op=ALU.subtract),
                  waits=[f5_, zw], sig=True)
            ve_small = S0.now("vector")

            S0.op("gpsimd", lambda g: g.nop(), waits=[g_b1, g_b2, g_b3, g_ct])
            S0.op("sync", lambda g: g.nop(), waits=[ve_ef, S0.now("vector")])
            emit(blk, S0, semsets[0])
        ctx0.close()

        ctx1 = ExitStack()
        e1 = ctx1.enter_context
        PAr = [e1(nc.sbuf_tensor(f"pa{i}", [128, D * NSUB], F32)) for i in range(3)]
        PBr = [e1(nc.sbuf_tensor(f"pbx{i}", [128, D * NSUB], F32)) for i in range(3)]
        yF = [(e1(nc.sbuf_tensor(f"yfh{i}", [128, KC * SIDE], FP16)),
               e1(nc.sbuf_tensor(f"yfl{i}", [128, KC * SIDE], FP16))) for i in range(2)]
        F1 = [(e1(nc.sbuf_tensor(f"f1h{i}", [128, KC * SIDE], FP16)),
               e1(nc.sbuf_tensor(f"f1l{i}", [128, KC * SIDE], FP16))) for i in range(2)]
        Cn = e1(nc.sbuf_tensor("cn", [128, CNS * D], F32))
        xvs2 = [e1(nc.sbuf_tensor(f"xvs{i}", [128, SIDE], FP16)) for i in range(2)]
        blk = e1(nc.Block())
        if True:
            S1 = Sched()
            tpst = [0]
            dnst = [0, (0, 1)]
            cn_ring = [0]
            cn_sig = [None] * CNS
            lnpar = LnPar()

            for xs_ in xvs2:
                S1.op("vector", lambda g, xs_=xs_: g.memset(xs_[:, :], 0.0), sig=True)
                zv = S1.now("vector")
                S1.op("vector", lambda g, xs_=xs_: g.memset(xs_[R:R + 1, :], 1.0),
                      waits=[zv], sig=True)
            ve_ones = S1.now("vector")

            ring_trace = []

            def ring_slot(u):
                i = cn_ring[0] % CNS
                cn_ring[0] += 1
                ring_trace.append(i)
                return Cn[:, i * D:(i + 1) * D], cn_sig[i]

            def run_ln(src, waits, dst=None, waits_u=None):
                if dst is None:
                    i0 = cn_ring[0]
                    slots, sigs = ln_chain(S1, src, waits, ring_slot, lnpar,
                                           waits_u=waits_u)
                    return slots, sigs, i0
                slots, sigs = ln_chain(
                    S1, src, waits,
                    lambda u: (dst[:, u * D:(u + 1) * D], None), lnpar,
                    waits_u=waits_u)
                return slots, sigs, None

            def run_T(slots, sigs, i0, dsth, dstl, ring_idx=None):
                cg = transpose_stage(S1, slots, sigs, dsth, dstl, tpst)
                te_sig = S1.now("tensor")
                if i0 is not None:
                    for u in range(NSUB):
                        cn_sig[(i0 + u) % CNS] = te_sig
                if ring_idx is not None:
                    for i in ring_idx:
                        cn_sig[i] = te_sig
                return cg

            def load_x(dst, b, t0, waits):
                ap_s = bass.AP(x_in.tensor, b * T * D + t0 * D,
                               [[D, 128], [128 * D, NSUB], [1, D]])
                S1.op("sync", lambda g, dst=dst, ap_s=ap_s: g.dma_start(
                    out=dst[:, :].rearrange("p (u d) -> p u d", u=NSUB), in_=ap_s),
                    waits=waits, sig=True, dma=True)
                return S1.now("sync")

            def kv_stage(xFh, xFl, b, t0, cg):
                lo, hi = b * S, (b + 1) * S
                kvp = kvpar[0] % 2
                c0 = kvp * SIDE
                kvpar[0] += 1
                tp = (0, b * S) if b > 0 else None
                te_sigs = []
                for bank, col0 in ((0, 0), (1, S)):
                    for p, (lh, rh) in enumerate(((wkv_h, xFh), (wkv_h, xFl), (wkv_l, xFh))):
                        for c in range(KC):
                            waits = []
                            if p == 0 and c == 0:
                                waits.append(S1.bank_wait(bank))
                            if bank == 0 and p == 0 and c % TGRP == 0:
                                waits.append(cg[c // TGRP])
                            S1.op("tensor", lambda g, bank=bank, col0=col0, p=p, c=c, lh=lh, rh=rh, lo=lo, hi=hi, tp=tp: g.matmul(
                                pb[bank][lo:hi, 0:SIDE],
                                lh[:, c * 2 * S + col0: c * 2 * S + col0 + S],
                                rh[:, c * SIDE:(c + 1) * SIDE],
                                start=(p == 0 and c == 0), stop=(p == 2 and c == KC - 1),
                                tile_position=tp),
                                waits=waits,
                                sig=(p == 2 and c == KC - 1))
                    te_sigs.append(S1.now("tensor"))
                te_k, te_v = te_sigs
                S1.op("scalar", lambda g, lo=lo, hi=hi, c0=c0: g.activation(
                    eb[lo:hi, c0:c0 + SIDE], pb[0][lo:hi, 0:SIDE], ACT.Exp,
                    bias=negkb[lo:hi, :], scale=-1.0),
                    waits=[te_k, kv_free[kvp]], sig=True)
                sc1 = S1.now("scalar")
                S1.bank_set(0, sc1)
                S1.op("gpsimd", lambda g, lo=lo, hi=hi, c0=c0: g.tensor_scalar_add(
                    eb[lo:hi, c0:c0 + SIDE], eb[lo:hi, c0:c0 + SIDE], 1.0),
                    waits=[sc1], sig=True, sem="gpsimdc")
                fe = S1.now("gpsimdc")
                S1.op("vector", lambda g, lo=lo, hi=hi, c0=c0: g.reciprocal(
                    sb[lo:hi, c0:c0 + SIDE], eb[lo:hi, c0:c0 + SIDE]),
                    waits=[fe], sig=True)
                ve1 = S1.now("vector")
                S1.op("scalar", lambda g, lo=lo, hi=hi, c0=c0: g.activation(
                    eb[lo:hi, c0:c0 + SIDE], sb[lo:hi, c0:c0 + SIDE], ACT.Exp,
                    scale=negef[lo:hi, :]),
                    waits=[ve1], sig=True)
                sc2 = S1.now("scalar")
                S1.op("vector", lambda g, lo=lo, hi=hi, t0=t0, c0=c0: g.scalar_tensor_tensor(
                    wkvbuf[lo:hi, t0:t0 + SIDE], pb[1][lo:hi, 0:SIDE], vbias[lo:hi, :],
                    eb[lo:hi, c0:c0 + SIDE], op0=ALU.add, op1=ALU.mult),
                    waits=[te_v, sc2], sig=True)
                ve2 = S1.now("vector")
                kv_free[kvp] = ve2
                S1.bank_set(1, ve2)
                return ve2

            def wsplit(chunks, wsrc, dh, dl, scale_by_g):
                last = None
                for c in chunks:
                    del ring_trace[:]
                    slot, sfree = ring_slot(0)
                    i_ = ring_trace[0]
                    S1.op("sync", lambda g, slot=slot, wsrc=wsrc, c=c: g.dma_start(
                        out=slot, in_=bass.AP(wsrc.tensor, c * 128 * D,
                                              [[D, 128], [1, D]])),
                        waits=[sfree], sig=True, dma=True)
                    wld = S1.now("sync")
                    waits = [wld]
                    if scale_by_g:
                        del ring_trace[:]
                        slot2, sfree2 = ring_slot(0)
                        i2_ = ring_trace[0]
                        S1.op("vector", lambda g, slot=slot, slot2=slot2: g.tensor_tensor(
                            slot2, slot, oneg[:, 0:D], op=ALU.mult),
                            waits=[wld, sfree2], sig=True)
                        fw = S1.now("vector")
                        S1.op("vector", lambda g, slot=slot, slot2=slot2: g.tensor_tensor(
                            slot, slot, slot2, op=ALU.subtract),
                            waits=[fw], sig=True)
                        cn_sig[i2_] = S1.now("vector")
                        waits = [S1.now("vector")]
                    S1.op("vector", lambda g, slot=slot, dh=dh, c=c: g.tensor_copy(
                        dh[:, c * D:(c + 1) * D], slot), waits=waits, sig=True)
                    fw2 = S1.now("vector")
                    S1.op("vector", lambda g, slot=slot, dh=dh, dl=dl, c=c: g.tensor_tensor(
                        dl[:, c * D:(c + 1) * D], slot, dh[:, c * D:(c + 1) * D],
                        op=ALU.subtract), waits=[fw2], sig=True)
                    last = S1.now("vector")
                    cn_sig[i_] = last
                return last

            pa_free = [None, None, None]
            pb_free = [[], [], []]
            xv_lo_free = None
            carry = None
            wd0_ready = None
            wd1_ready = None
            kvpar = [0]
            kv_free = [None, None]
            carry_n = {}

            pairs = [(b, pr) for b in range(B) for pr in range(NPAIR)]
            nside = 0
            for pi, (b, pr) in enumerate(pairs):
                if True:
                    t0A = pr * SIDE
                    t0B = t0A + half
                    sA = nside
                    nside += 2
                    iA, iB = sA % 3, (sA + 1) % 3
                    jA, jB = 0, 1
                    if carry is None:
                        lxA = load_x(PAr[iA], b, t0A, [pa_free[iA]])
                        lxB = load_x(PAr[iB], b, t0B, [pa_free[iB]])
                        slA, sgA, i0A = run_ln(PAr[iA], [lxA])
                        slB, sgB, i0B = run_ln(PAr[iB], [lxB])
                        cgA = run_T(slA, sgA, i0A, F1[jA][0], F1[jA][1])
                        cgB = run_T(slB, sgB, i0B, F1[jB][0], F1[jB][1])
                    else:
                        cgA, cgB = carry
                    xvA = xvr_stage(S1, F1[jA][0], F1[jA][1], xvs2[jA],
                                    [ve_ones], xv_lo_free, cgA)
                    if pi == 0:
                        wd0_ready = wsplit(range(KC), wt_in, wd0_h, wd0_l, False)
                    xvB = xvr_stage(S1, F1[jB][0], F1[jB][1], xvs2[jB],
                                    [ve_ones], xvA, cgB)
                    xv_lo_free = xvB
                    ev1 = {}
                    ysl = {}
                    for si, ii, jj, xv, cg, t0 in (("A", iA, jA, xvA, cgA, t0A),
                                                   ("B", iB, jB, xvB, cgB, t0B)):
                        def evac_x1(g, u, hh, ps, ii=ii):
                            return g.scalar_tensor_tensor(
                                PBr[ii][:, u * D + hh * HCOL: u * D + hh * HCOL + HCOL],
                                ps, 0.5,
                                PAr[ii][:, u * D + hh * HCOL: u * D + hh * HCOL + HCOL],
                                op0=ALU.mult, op1=ALU.add)
                        yslots = []
                        ysigs = []
                        deferred = []

                        def post_u(u, ev, si=si, ii=ii, t0=t0, yslots=yslots,
                                   ysigs=ysigs, deferred=deferred):
                            ap_d = bass.AP(x1_hbm.tensor,
                                           b * T * D + (t0 + u * 128) * D,
                                           [[D, 128], [1, D]])
                            S1.op("scalar", lambda g, ii=ii, ap_d=ap_d, u=u: g.dma_start(
                                out=ap_d, in_=PBr[ii][:, u * D:(u + 1) * D]),
                                waits=[ev], sig=True, dma=True)
                            if si == "B" and u == NSUB - 1:
                                deferred.append((u, ev))
                                return
                            sl, sg = ln_sub(S1, PBr[ii], u, [ev],
                                            lambda uu: (PAr[ii][:, uu * D:(uu + 1) * D],
                                                        None), lnpar)
                            yslots.append(sl)
                            ysigs.append(sg)
                        evs = dense_stage(S1, F1[jj][0], F1[jj][1], wd0_h, wd0_l,
                                          pb_free[ii] + ([wd0_ready] if pi == 0 else []),
                                          evac_x1, dnst,
                                          xvs=xvs2[jj], cg=cg, xv_sig=xv,
                                          post_u=post_u)
                        ev1[si] = evs[-1]
                        ev1[si + "sp"] = S1.now("scalar")
                        ysl[si] = (yslots, ysigs, deferred, ii)
                        if pi == 0:
                            half_c = range(0, KC // 2) if si == "A" else range(KC // 2, KC)
                            wd1_ready = wsplit(half_c, wst_in, wd1_h, wd1_l, True)
                    ycgA = run_T(ysl["A"][0], ysl["A"][1], None, yF[0][0], yF[0][1])
                    teA2 = S1.now("tensor")
                    onegb = bass.AP(oneg[:, :].tensor, 0,
                                    [[D, 128], [0, NSUB], [1, D]])
                    S1.op("vector", lambda g, iA=iA, onegb=onegb: g.tensor_tensor(
                        PAr[iA][:, :].rearrange("p (u d) -> p u d", u=NSUB),
                        PAr[iA][:, :].rearrange("p (u d) -> p u d", u=NSUB),
                        onegb, op=ALU.mult),
                        waits=[teA2, ysl["A"][1][-1]], sig=True)
                    ugA = S1.now("vector")
                    for (du, dev) in ysl["B"][2]:
                        iiB = ysl["B"][3]
                        sl, sg = ln_sub(S1, PBr[iiB], du, [dev],
                                        lambda uu: (PAr[iiB][:, uu * D:(uu + 1) * D],
                                                    None), lnpar)
                        ysl["B"][0].append(sl)
                        ysl["B"][1].append(sg)
                    ycgB = run_T(ysl["B"][0], ysl["B"][1], None, yF[1][0], yF[1][1])
                    teB2 = S1.now("tensor")
                    S1.op("vector", lambda g, iB=iB, onegb=onegb: g.tensor_tensor(
                        PAr[iB][:, :].rearrange("p (u d) -> p u d", u=NSUB),
                        PAr[iB][:, :].rearrange("p (u d) -> p u d", u=NSUB),
                        onegb, op=ALU.mult),
                        waits=[teB2, ysl["B"][1][-1]], sig=True)
                    ugB = S1.now("vector")
                    usig = {"A": ugA, "B": ugB}
                    sl3 = {}
                    for si, ii, jj, oj, t0 in (("A", iA, jA, 1, t0A),
                                               ("B", iB, jB, 0, t0B)):
                        ycg = ycgB if si == "A" else ycgA
                        last_ev = None
                        uev = []
                        for u in range(NSUB):
                            for hh in range(NH):
                                bk = dnst[1][dnst[0] % len(dnst[1])]
                                dnst[0] += 1
                                first = (u == 0 and hh == 0)
                                for p, (lh, rh) in enumerate(((yF[oj][0], wd1_h),
                                                              (yF[oj][0], wd1_l),
                                                              (yF[oj][1], wd1_h))):
                                    for c in range(KC):
                                        waits = []
                                        if p == 0 and c == 0:
                                            waits.append(S1.bank_wait(bk))
                                            if first:
                                                waits.append(ev1[si + "sp"])
                                                if pi == 0:
                                                    waits.append(wd1_ready)
                                        if first and p == 0 and c % TGRP == 0:
                                            waits.append(ycg[c // TGRP])
                                        S1.op("tensor", lambda g, u=u, hh=hh, bk=bk, c=c, lh=lh, rh=rh, p=p: g.matmul(
                                            pb[bk][:, 0:HCOL],
                                            lh[:, c * SIDE + u * 128: c * SIDE + u * 128 + 128],
                                            rh[:, c * D + hh * HCOL: c * D + hh * HCOL + HCOL],
                                            start=(p == 0 and c == 0),
                                            stop=(p == 2 and c == KC - 1)),
                                            waits=waits,
                                            sig=(p == 2 and c == KC - 1))
                                te = S1.now("tensor")
                                S1.op("vector", lambda g, u=u, hh=hh, bk=bk, ii=ii: g.tensor_tensor(
                                    PBr[ii][:, u * D + hh * HCOL: u * D + hh * HCOL + HCOL],
                                    pb[bk][:, 0:HCOL],
                                    PAr[ii][:, u * D + hh * HCOL: u * D + hh * HCOL + HCOL],
                                    op=ALU.add),
                                    waits=[te, usig[si]] if (u == 0 and hh == 0) else [te],
                                    sig=True)
                                last_ev = S1.now("vector")
                                S1.bank_set(bk, last_ev)
                                if hh == NH - 1:
                                    del ring_trace[:]
                                    sl_, sg_ = ln_sub(S1, PBr[ii], u, [last_ev],
                                                      ring_slot, lnpar)
                                    uev.append((sl_, sg_, False, ring_trace[0]))
                        ev1[si + "xs"] = last_ev
                        sl3[si] = (uev, ii)
                        if pi + 1 < len(pairs):
                            nb_, npr_ = pairs[pi + 1]
                            if si == "A":
                                ni = nside % 3
                                nt0 = npr_ * SIDE
                            else:
                                ni = (nside + 1) % 3
                                nt0 = npr_ * SIDE + half
                                pa_free[iA] = ev1["Axs"]
                            nlx = load_x(PAr[ni], nb_, nt0, [pa_free[ni]])
                            stats_n = []
                            for u in range(NSUB):
                                pp, ve = ln_stats(S1, PAr[ni], u,
                                                  [nlx] if u == 0 else [], lnpar)
                                stats_n.append((pp, ve, ln_rln(S1, u, pp, ve)))
                            carry_n[si] = (ni, stats_n)
                    slotsA = [x[0] for x in sl3["A"][0]]
                    sigsA = [x[1] for x in sl3["A"][0]]
                    idxA = [x[3] for x in sl3["A"][0]]
                    cg3A = run_T(slotsA, sigsA, None, F1[jA][0], F1[jA][1],
                                 ring_idx=idxA)
                    slotsB = []
                    sigsB = []
                    idxB = []
                    for x in sl3["B"][0]:
                        if x[2]:
                            del ring_trace[:]
                            sl_, sg_ = ln_sub(S1, PBr[sl3["B"][1]], x[0], [x[1]],
                                              ring_slot, lnpar)
                            slotsB.append(sl_)
                            sigsB.append(sg_)
                            idxB.append(ring_trace[0])
                        else:
                            slotsB.append(x[0])
                            sigsB.append(x[1])
                            idxB.append(x[3])
                    cg3B = run_T(slotsB, sigsB, None, F1[jB][0], F1[jB][1],
                                 ring_idx=idxB)
                    kv_stage(F1[jA][0], F1[jA][1], b, t0A, cg3A)
                    pb_free[iA] = [S1.now("vector"), ev1["Asp"]]
                    pa_free[iA] = ev1["Axs"]
                    pa_free[iB] = ev1["Bxs"]
                    carry = None
                    if pi + 1 < len(pairs):
                        cc = []
                        for si2 in ("A", "B"):
                            ni, stats_n = carry_n[si2]
                            slots = []
                            sigs = []
                            idxs = []
                            for u in range(NSUB):
                                pp, ve, rs = stats_n[u]
                                del ring_trace[:]
                                sl_, sg_ = ln_apply(S1, PAr[ni], u, pp, ve,
                                                    ring_slot, lnpar,
                                                    on_vector=True, rln_sig=rs)
                                slots.append(sl_)
                                sigs.append(sg_)
                                idxs.append(ring_trace[0])
                            cc.append((slots, sigs, idxs))
                        carry = cc
                    kv_stage(F1[jB][0], F1[jB][1], b, t0B, cg3B)
                    pb_free[iB] = [S1.now("vector"), ev1["Bsp"]]
                    if carry is not None:
                        cc = carry
                        ncgA = run_T(cc[0][0], cc[0][1], None,
                                     F1[jA][0], F1[jA][1], ring_idx=cc[0][2])
                        ncgB = run_T(cc[1][0], cc[1][1], None,
                                     F1[jB][0], F1[jB][1], ring_idx=cc[1][2])
                        carry = (ncgA, ncgB)

            S1.op("scalar", lambda g: g.nop(), waits=[S1.now("scalar")])
            emit(blk, S1, semsets[1])
        ctx1.close()

        ctx2 = ExitStack()
        e2 = ctx2.enter_context
        q2 = [e2(nc.sbuf_tensor(f"q{i}", [128, D * NSUB], F32)) for i in range(3)]
        x2b2 = [e2(nc.sbuf_tensor(f"x2b{i}", [128, D * NSUB], F32)) for i in range(2)]
        Cn2 = e2(nc.sbuf_tensor("cn2", [128, CNS * D], F32))
        G1 = [(e2(nc.sbuf_tensor(f"g1h{i}", [128, KC * SIDE], FP16)),
               e2(nc.sbuf_tensor(f"g1l{i}", [128, KC * SIDE], FP16))) for i in range(2)]
        xv22 = [e2(nc.sbuf_tensor(f"xv2{i}", [128, SIDE], FP16)) for i in range(2)]
        stHL = e2(nc.sbuf_tensor("sthl", [128, B * T], FP16))
        states = e2(nc.sbuf_tensor("states", [128, T], F32))
        blk = e2(nc.Block())
        if True:
            S3 = Sched()
            xv_mod[0] = 1
            tpst3 = [0]
            dnst3 = [0, (0, 1, 7)]
            cn_ring3 = [0]
            cn_sig3 = [None] * CNS
            lnpar3 = LnPar()

            for xs_ in xv22:
                S3.op("vector", lambda g, xs_=xs_: g.memset(xs_[:, :], 0.0), sig=True)
                zv = S3.now("vector")
                S3.op("vector", lambda g, xs_=xs_: g.memset(xs_[R:R + 1, :], 1.0),
                      waits=[zv], sig=True)
            ve_ones3 = S3.now("vector")

            def ring_slot3(u):
                i = cn_ring3[0] % CNS
                cn_ring3[0] += 1
                return Cn2[:, i * D:(i + 1) * D], cn_sig3[i]

            def run_ln3(src, waits, waits_u=None):
                i0 = cn_ring3[0]
                slots, sigs = ln_chain(S3, src, waits, ring_slot3, lnpar3,
                                       waits_u=waits_u)
                return slots, sigs, i0

            def run_T3(slots, sigs, i0, dsth, dstl):
                cg = transpose_stage(S3, slots, sigs, dsth, dstl, tpst3)
                te_sig = S3.now("tensor")
                for u in range(NSUB):
                    cn_sig3[(i0 + u) % CNS] = te_sig
                return cg

            st_ready = {}
            wbc = wdec[:, 0:1].broadcast_to((128, T))

            def scan_seq(bq):
                lo, hi = bq * S, (bq + 1) * S
                S3.op("vector", lambda g, lo=lo, hi=hi: g.tensor_tensor_scan(
                    states[lo:hi, :], wbc[lo:hi, :], wkvbuf[lo:hi, :], 0.0,
                    op0=ALU.mult, op1=ALU.add), sig=True)
                ve_scan = S3.now("vector")
                if bq == 0:
                    S3.op("scalar", lambda g: g.activation(
                        stHL[0:S, 0:T], states[0:S, :], ACT.Identity),
                        waits=[ve_scan], sig=True)
                    sc_hi = S3.now("scalar")
                    S3.op("vector", lambda g: g.tensor_tensor(
                        wkvbuf[0:S, 0:T], states[0:S, :], stHL[0:S, 0:T],
                        op=ALU.subtract), waits=[sc_hi], sig=True)
                    ve_lo = S3.now("vector")
                    S3.op("gpsimd", lambda g: g.dma_start(
                        out=stHL[S:2 * S, 0:T], in_=wkvbuf[0:S, 0:T]),
                        waits=[ve_lo], sig=True, dma=True)
                else:
                    S3.op("scalar", lambda g: g.activation(
                        stHL[S:2 * S, T:2 * T], states[S:2 * S, :], ACT.Identity),
                        waits=[ve_scan], sig=True)
                    sc_hi = S3.now("scalar")
                    S3.op("gpsimd", lambda g: g.dma_start(
                        out=stHL[0:S, T:2 * T], in_=stHL[S:2 * S, T:2 * T]),
                        waits=[sc_hi], sig=True, dma=True)
                    gp_hi = S3.now("gpsimd")
                    S3.op("vector", lambda g: g.tensor_tensor(
                        stHL[S:2 * S, T:2 * T], states[S:2 * S, :],
                        stHL[S:2 * S, T:2 * T],
                        op=ALU.subtract), waits=[gp_hi], sig=True)
                st_ready[bq] = [S3.now("gpsimd"), S3.now("vector"), S3.now("scalar")]

            scan_seq(0)

            def load_x1(dst, b, t0, waits):
                ap_sx = bass.AP(x1_hbm.tensor, b * T * D + t0 * D,
                                [[D, 128], [128 * D, NSUB], [1, D]])
                S3.op("sync", lambda g, dst=dst, ap_sx=ap_sx: g.dma_start(
                    out=dst[:, :].rearrange("p (u d) -> p u d", u=NSUB), in_=ap_sx),
                    waits=waits, sig=True, dma=True)
                return S3.now("sync")

            free_q = [None, None, None]

            def stage1_op(i, b, t0):
                par = i % 2
                qar = i % 3
                lx = load_x1(q2[qar], b, t0, [free_q[qar]])
                uev = []
                for u in range(NSUB):
                    for hh in range(NH):
                        bk = dnst3[1][dnst3[0] % len(dnst3[1])]
                        dnst3[0] += 1
                        waits = [S3.bank_wait(bk)]
                        if u == 0 and hh == 0:
                            waits += st_ready[b]
                        for p, rh in enumerate((wo_h, wo_l)):
                            S3.op("tensor", lambda g, u=u, hh=hh, bk=bk, rh=rh, p=p, b=b, t0=t0: g.matmul(
                                pb[bk][:, 0:HCOL],
                                stHL[:, b * T + t0 + u * 128: b * T + t0 + u * 128 + 128],
                                rh[:, hh * HCOL:(hh + 1) * HCOL],
                                start=(p == 0), stop=(p == 1)),
                                waits=waits if p == 0 else (), sig=(p == 1))
                        te = S3.now("tensor")
                        S3.op("vector", lambda g, u=u, hh=hh, bk=bk, par=par, qar=qar: g.tensor_tensor(
                            x2b2[par][:, u * D + hh * HCOL: u * D + hh * HCOL + HCOL],
                            pb[bk][:, 0:HCOL],
                            q2[qar][:, u * D + hh * HCOL: u * D + hh * HCOL + HCOL],
                            op=ALU.add),
                            waits=[te, lx] if (u == 0 and hh == 0) else [te], sig=True)
                        last = S3.now("vector")
                        S3.bank_set(bk, last)
                        if hh == NH - 1:
                            uev.append(last)
                return par, qar, uev

            def stage1_ln(par, uev):
                i0 = cn_ring3[0]
                uln = [ln_sub(S3, x2b2[par], u, [uev[u]], ring_slot3, lnpar3)
                       for u in range(NSUB)]
                return [x[0] for x in uln], [x[1] for x in uln], i0

            xv_lo_free3 = [None]

            def stage2_T(i, par, slots, sigs, i0):
                return run_T3(slots, sigs, i0, G1[par][0], G1[par][1])

            def stage2(i, b, t0, par, qar, cg):
                xv = xvr_stage(S3, G1[par][0], G1[par][1], xv22[par],
                               [ve_ones3] if i < 2 else [], xv_lo_free3[0], cg)
                xv_lo_free3[0] = xv

                def evac_x3(g, u, hh, ps, par=par, qar=qar):
                    return g.scalar_tensor_tensor(
                        q2[qar][:, u * D + hh * HCOL: u * D + hh * HCOL + HCOL],
                        ps, 0.5,
                        x2b2[par][:, u * D + hh * HCOL: u * D + hh * HCOL + HCOL],
                        op0=ALU.mult, op1=ALU.add)
                evs = dense_stage(S3, G1[par][0], G1[par][1], wd0_h, wd0_l,
                                  [], evac_x3, dnst3, xvs=xv22[par],
                                  cg=cg, xv_sig=xv)
                last = None
                for u in range(NSUB):
                    ap_d = bass.AP(out_hbm.tensor, b * T * D + (t0 + u * 128) * D,
                                   [[D, 128], [1, D]])
                    S3.op("scalar", lambda g, qar=qar, ap_d=ap_d, u=u: g.dma_start(
                        out=ap_d, in_=q2[qar][:, u * D:(u + 1) * D]),
                        waits=[evs[u * NH + NH - 1]], sig=True, dma=True)
                    last = S3.now("scalar")
                free_q[qar] = last
                return last

            sides = [(b, t0) for b in range(B) for t0 in range(0, T, SIDE)]
            pend = None
            out_last = None
            for i, (b, t0) in enumerate(sides):
                if i == 1 and B > 1:
                    scan_seq(1)
                par, qar, uev = stage1_op(i, b, t0)
                if pend is not None:
                    (pi, pbb, pt0, ppar, pqar, pslots, psigs, pi0) = pend
                    pcg = stage2_T(pi, ppar, pslots, psigs, pi0)
                slots, sigs, i0 = stage1_ln(par, uev)
                if pend is not None:
                    out_last = stage2(pi, pbb, pt0, ppar, pqar, pcg)
                pend = (i, b, t0, par, qar, slots, sigs, i0)
            (pi, pbb, pt0, ppar, pqar, pslots, psigs, pi0) = pend
            pcg = stage2_T(pi, ppar, pslots, psigs, pi0)
            out_last = stage2(pi, pbb, pt0, ppar, pqar, pcg)

            S3.op("scalar", lambda g: g.nop(), waits=[out_last])
            emit(blk, S3, semsets[2])
        ctx2.close()

    return nc


def kernel(L0_attn_x, L0_attn_time_decay, L0_attn_time_first, L0_attn_W_key,
           L0_attn_W_value, L0_attn_W_output, L0_attn_W_shift, L0_attn_shift_gate,
           L0_attn_ln_w, L0_attn_ln_b, L0_ffn_W, L0_ffn_b, L0_ffn_V_r, L0_ffn_C):
    N = 8
    x = np.ascontiguousarray(np.asarray(L0_attn_x, dtype=np.float32))
    wt = np.ascontiguousarray(np.asarray(L0_ffn_W, dtype=np.float32).T)
    wst = np.ascontiguousarray(np.asarray(L0_attn_W_shift, dtype=np.float32).T)
    wkvt = np.ascontiguousarray(np.concatenate(
        [np.asarray(L0_attn_W_key, np.float32).T,
         np.asarray(L0_attn_W_value, np.float32).T], axis=1))
    vr = np.ascontiguousarray(np.asarray(L0_ffn_V_r, np.float32))
    ctb = np.ascontiguousarray(np.concatenate(
        [np.asarray(L0_ffn_C, np.float32).T,
         np.asarray(L0_ffn_b, np.float32)[None, :]], axis=0))
    wot = np.ascontiguousarray(np.asarray(L0_attn_W_output, np.float32).T)
    td = np.ascontiguousarray(np.asarray(L0_attn_time_decay, np.float32)[:, None])
    tf = np.ascontiguousarray(np.asarray(L0_attn_time_first, np.float32)[:, None])
    sg = np.ascontiguousarray(np.asarray(L0_attn_shift_gate, np.float32)[None, :])
    lnw = np.ascontiguousarray(np.asarray(L0_attn_ln_w, np.float32))
    lnb = np.ascontiguousarray(np.asarray(L0_attn_ln_b, np.float32))
    ident = np.eye(128, dtype=np.float32)

    nc = build_program()
    in_maps = []
    for c in range(N):
        in_maps.append(dict(x=np.ascontiguousarray(x[2 * c:2 * c + 2]),
                            wt=wt, wst=wst, wkvt=wkvt, vr=vr, ctb=ctb, wot=wot,
                            td=td, tf=tf, sg=sg, lnw=lnw, lnb=lnb, ident=ident))
    res = run_bass_kernel_spmd(nc, in_maps, core_ids=list(range(N)))
    return np.concatenate([np.asarray(r["out"], dtype=np.float32)
                           for r in res.results], axis=0)
